# version 01
# baseline (speedup 1.0000x reference)
import numpy as np
import concourse.bass as bass
import concourse.mybir as mybir
from concourse.bass_utils import run_bass_kernel_spmd
from contextlib import ExitStack

F32 = mybir.dt.float32
BF16 = mybir.dt.bfloat16
U8 = mybir.dt.uint8
AF = mybir.ActivationFunctionType
ALU = mybir.AluOpType
AX = mybir.AxisListType

D = 1024
S = 2048
NC = 8
NH = 16
HD = 64
NP = 8
DEPTH = 6
DI = 2730
NJ = 22
JW = [128] * 21 + [42]
NEG = -1e30


def _prep_shared(inputs):
    g = lambda k: np.ascontiguousarray(np.asarray(inputs[k], dtype=np.float32))
    d = {}
    wqkvT = g("L0_attn_w_qkv").T
    bqkv = g("L0_attn_b_qkv")
    idx = []
    for p in range(NP):
        idx += list(range(2 * p * HD, 2 * p * HD + 2 * HD))
        idx += list(range(D + 2 * p * HD, D + 2 * p * HD + 2 * HD))
    idx += list(range(2 * D, 3 * D))
    d["owqkv"] = np.ascontiguousarray(wqkvT[:, idx])
    d["obqkv"] = np.ascontiguousarray(bqkv[idx])
    d["owout"] = np.ascontiguousarray(g("L0_attn_w_out").T)
    d["obout"] = g("L0_attn_b_out")
    d["owent"] = np.ascontiguousarray(g("L0_attn_w_ent").T)
    d["obent"] = g("L0_attn_b_ent").reshape(1, 1)
    d["olng"] = g("L0_attn_ln_g")
    d["olnb"] = g("L0_attn_ln_b")
    Wq = g("L0_ffn_Wqkv")
    idxi = []
    for p in range(NP):
        idxi += list(range(p * 128, (p + 1) * 128))
        idxi += list(range(D + p * 128, D + (p + 1) * 128))
        idxi += list(range(2 * D + p * 128, 2 * D + (p + 1) * 128))
    d["iwqkv"] = np.ascontiguousarray(Wq[:, :, idxi])
    d["iwo"] = g("L0_ffn_Wo")
    W1 = g("L0_ffn_W1")
    b1 = g("L0_ffn_b1")
    idx1 = []
    for j in range(NJ):
        w = JW[j]
        idx1 += list(range(j * 128, j * 128 + w))
        idx1 += list(range(DI + j * 128, DI + j * 128 + w))
    d["iw1"] = np.ascontiguousarray(W1[:, :, idx1])
    d["ib1"] = np.ascontiguousarray(b1[:, idx1])
    d["iw2"] = g("L0_ffn_W2")
    d["ib2"] = g("L0_ffn_b2")
    d["ilng"] = g("L0_ffn_ln_g")
    d["ilnb"] = g("L0_ffn_ln_b")
    d["iqs"] = g("L0_ffn_q_scale")
    d["iks"] = g("L0_ffn_k_scale")
    d["eye"] = np.eye(128, dtype=np.float32)
    e4 = np.zeros((4, 4, 128), dtype=np.float32)
    for j in range(4):
        e4[j, j, :] = 1.0
    d["e4"] = e4
    u = np.arange(128)[:, None]
    v = np.arange(128)[None, :]
    d["tri_c"] = np.where(v <= u, 0.0, NEG).astype(np.float32)
    d["tri_g"] = np.where(v >= u, 0.0, NEG).astype(np.float32)
    half = np.arange(0, HD, 2, dtype=np.float64)
    inv_freq = 1.0 / 10000.0 ** (half / HD)
    t = np.arange(S, dtype=np.float64)
    fr = t[:, None] * inv_freq[None, :]
    fr = np.concatenate([fr, fr], -1)
    bs = (half + 0.4 * HD) / (1.4 * HD)
    pw = (t - S // 2) / (256 // 2)
    sc = bs[None, :] ** pw[:, None]
    sc = np.concatenate([sc, sc], -1)
    tm = lambda a: np.ascontiguousarray(
        a.reshape(16, 128, HD).transpose(1, 0, 2).astype(np.float32))
    d["cos8"] = tm(np.cos(fr) * sc * 8.0)
    d["sin8"] = tm(np.sin(fr) * sc * 8.0)
    d["cosi"] = tm(np.cos(fr) / sc)
    d["sini"] = tm(np.sin(fr) / sc)
    return d


class Ctr:
    __slots__ = ("sem", "n", "owner")

    def __init__(self, sem, owner=None):
        self.sem = sem
        self.n = 0
        self.owner = owner


class MK:
    ENGS = ("g", "y", "t", "v", "s")

    def __init__(self, nc):
        self.nc = nc
        self.q = {e: [] for e in self.ENGS}
        self.waited = {e: {} for e in self.ENGS}

    def op(self, eng, fn, wait=(), inc=None, inck=1):
        todo = []
        for (c, val) in wait:
            if val <= 0 or c.owner == eng:
                continue
            assert val <= c.n, f"wait on future {val} > {c.n} of {c.sem.name}"
            if self.waited[eng].get(c, -1) < val:
                self.waited[eng][c] = val
                todo.append((c, val))
        if inc is not None:
            inc.n += inck
        sem = inc.sem if inc is not None else None

        def run(e, todo=todo, fn=fn, sem=sem, inck=inck):
            for (c, val) in todo:
                e.wait_ge(c.sem, val)
            r = fn(e)
            if sem is not None:
                if r is None:
                    r = e.drain()
                r.then_inc(sem, inck)
        self.q[eng].append(run)


def view(region, off, dt, shape):
    esz = {F32: 4, BF16: 2, U8: 1}[dt]
    n = 1
    for s in shape[1:]:
        n *= s
    assert off % esz == 0
    a = region.ap().bitcast(dt)[:, off // esz: off // esz + n]
    if len(shape) == 3:
        a = a.rearrange("p (a b) -> p a b", a=shape[1])
    elif len(shape) == 4:
        a = a.rearrange("p (a b c) -> p a b c", a=shape[1], b=shape[2])
    if shape[0] != 128:
        a = a[0: shape[0]]
    return a


def build_program(n_layers=DEPTH, stage="final"):
    nc = bass.Bass("TRN2", target_bir_lowering=False, detect_race_conditions=False)
    dr = {}

    def din(name, shape):
        dr[name] = nc.dram_tensor(name, shape, F32, kind="ExternalInput").ap()

    din("x_fm", [D, S])
    din("owqkv", [D, 3 * D]); din("obqkv", [3 * D]); din("owout", [D, D])
    din("obout", [D]); din("owent", [D, 1]); din("obent", [1, 1])
    din("olng", [D]); din("olnb", [D])
    din("iwqkv", [DEPTH, D, 3 * D]); din("iwo", [DEPTH, D, D])
    din("iw1", [DEPTH, D, 2 * DI]); din("ib1", [DEPTH, 2 * DI])
    din("iw2", [DEPTH, DI, D]); din("ib2", [DEPTH, D])
    din("ilng", [DEPTH, D]); din("ilnb", [DEPTH, D])
    din("iqs", [DEPTH, HD]); din("iks", [DEPTH, HD])
    din("eye", [128, 128]); din("tri_c", [128, 128]); din("tri_g", [128, 128])
    din("e4", [4, 4, 128])
    din("cos8", [128, 16, HD]); din("sin8", [128, 16, HD])
    din("cosi", [128, 16, HD]); din("sini", [128, 16, HD])
    o_fm = nc.dram_tensor("o_fm", [D, S], F32, kind="ExternalOutput").ap()
    spX = nc.dram_tensor("spX", [D, S], F32, kind="Internal").ap()
    spX1 = nc.dram_tensor("spX1", [D, S], F32, kind="Internal").ap()
    spGate = nc.dram_tensor("spGate", [S], F32, kind="Internal").ap()
    d3 = lambda name: dr[name].rearrange("(c p) t -> p c t", p=128)
    dw3 = lambda ap2d: ap2d.rearrange("(c p) n -> p c n", p=128)
    dcol = lambda ap1d: ap1d.rearrange("(c p) -> p c", p=128)
    drow = lambda ap1d: ap1d.rearrange("(o n) -> o n", o=1)

    with ExitStack() as ctx:
        sbuf = lambda n, sh, dt: ctx.enter_context(nc.sbuf_tensor(n, sh, dt))
        psum = lambda n, sh: ctx.enter_context(nc.psum_tensor(n, sh, F32))

        def mksem(n, owner=None):
            return Ctr(ctx.enter_context(nc.semaphore(n)), owner)

        RY = sbuf("RY", [128, 65536], U8)
        RXA = sbuf("RXA", [128, 32768], U8)
        ROF = sbuf("ROF", [128, 32768], U8)
        RSC = sbuf("RSC", [128, 49152], U8)
        RCN = sbuf("RCN", [128, 32032], U8)

        psA = psum("psA", [128, 512])
        psB = psum("psB", [128, 512])
        psC = psum("psC", [128, 512])
        psS = psum("psS", [128, 2048])
        psT = psum("psT", [128, 512])

        T = mksem("T", "t"); V = mksem("V", "v"); SG = mksem("SG", "s")
        cDg = mksem("cDg"); cDs = mksem("cDs"); xD = mksem("xD")
        wS0 = mksem("wS0"); wS1 = mksem("wS1")
        w1S0 = mksem("w1S0"); w1S1 = mksem("w1S1")
        w2S0 = mksem("w2S0"); w2S1 = mksem("w2S1")
        woS = mksem("woS"); wvS = mksem("wvS")
        spXS = mksem("spXS"); spX1S = mksem("spX1S")
        rlS0 = mksem("rlS0"); rlS1 = mksem("rlS1")
        gcS = mksem("gcS"); ldS = mksem("ldS")
        stO = mksem("stO")

        mk = MK(nc)
        E = mk.op

        def BART(extra=()):
            return [(V, V.n), (SG, SG.n)] + list(extra)

        Y = view(RY, 0, F32, [128, NC, S])
        XA = view(RXA, 0, BF16, [128, NC, S])
        OF = view(ROF, 0, BF16, [128, NC, S])
        cEYE = view(RCN, 0, BF16, [128, 128])
        cTRIC = view(RCN, 256, BF16, [128, 128])
        cTRIG = view(RCN, 512, BF16, [128, 128])
        cC8 = view(RCN, 768, BF16, [128, 16, HD])
        cS8 = view(RCN, 2816, BF16, [128, 16, HD])
        cCI = view(RCN, 4864, BF16, [128, 16, HD])
        cSI = view(RCN, 6912, BF16, [128, 16, HD])
        cE8 = view(RCN, 8960, BF16, [128, 8, 4])
        cE4 = view(RCN, 9088, F32, [128, 4, 128])
        cONE1 = view(RCN, 11136, F32, [1, 128])
        cPOOL = view(RCN, 11648, F32, [128, 8])
        cBQK = view(RCN, 11680, F32, [128, 16])
        cGOUT = view(RCN, 11744, F32, [128, 8])
        cBOUT = view(RCN, 11776, F32, [128, 8])
        cBO01 = view(RCN, 11808, F32, [128, 8])
        cWENT = view(RCN, 11840, BF16, [128, 8])
        cBENT = view(RCN, 11856, F32, [1, 1])
        cGCOL = view(RCN, 11872, F32, [128, 16])
        cSTAT = view(RCN, 11936, F32, [4, 512])
        cST2 = view(RCN, 29936, F32, [4, 512])
        cNEG1 = view(RCN, 31984, F32, [128, 1])
        cMU = view(RCN, 13984, F32, [4, 512])
        cCS = view(RCN, 16032, F32, [4, 512])
        cROW = view(RCN, 18080, F32, [4, 512])
        cQSR = view(RCN, 20128, F32, [1, 256])
        cLGB = view(RCN, 21152, F32, [128, 16])
        cB1 = view(RCN, 21216, F32, [128, 44])
        cB2 = view(RCN, 21392, F32, [128, 8])
        cELL = view(RCN, 21424, F32, [128, 1])
        cELL2 = view(RCN, 21472, F32, [128, 1])
        cNM = view(RCN, 21440, F32, [128, 1])
        cRC = view(RCN, 21456, F32, [128, 1])
        cRN = view(RCN, 21488, F32, [128, 64])
        AQ = view(RCN, 21744, BF16, [128, 16, HD])
        BQ = view(RCN, 23792, BF16, [128, 16, HD])
        AK = view(RCN, 25840, BF16, [128, 16, HD])
        BK = view(RCN, 27888, BF16, [128, 16, HD])
        gRow = view(RCN, 11936, F32, [1, 2048])
        qsRep = view(RCN, 11936, BF16, [128, 4, 64])

        xsq = view(RSC, 25600, BF16, [128, S])
        tscr = view(RSC, 29696, F32, [128, 512])
        bvR = view(RSC, 0, F32, [128, D])
        tvs = view(RSC, 4096, F32, [128, 512])
        bvRow = view(RSC, 31744, F32, [1, D])
        vTM = view(RY, 0, BF16, [128, 16, D])
        wvSB = view(RY, 32768, BF16, [128, NC, D])
        qpSB = view(RSC, 0, BF16, [128, S])
        kpSB = view(RSC, 4096, BF16, [128, S])
        pSB = view(RSC, 8192, BF16, [128, S])
        ptSB = view(RSC, 12288, BF16, [128, S])
        wqhSB = [view(RSC, 16384, BF16, [128, NC, 256]),
                 view(RSC, 20480, BF16, [128, NC, 256])]
        diagO = view(RSC, 24576, BF16, [128, 128])
        woSBo = view(RSC, 16384, BF16, [128, NC, D])
        xrl = view(RSC, 8192, F32, [128, 2, 512])
        twx = view(RSC, 4096, F32, [128, 512])
        x1v = Y
        wqp = view(RSC, 0, BF16, [128, NC, 384])
        qkvTM = view(RSC, 6144, BF16, [128, 16, 384])
        qv6 = view(RSC, 6144, BF16, [128, 16, 6, HD])
        q2k2 = view(RSC, 18432, BF16, [128, 16, 256])
        q2k4 = view(RSC, 18432, BF16, [128, 16, 4, HD])
        qkFM = view(RSC, 26624, BF16, [128, 2, S])
        sqB = view(RSC, 34816, BF16, [128, 16, 256])
        sq64 = view(RSC, 34816, BF16, [128, 64, 64])
        qrB = view(RSC, 34816, BF16, [128, 16, 2, HD])
        m1B = view(RSC, 38912, BF16, [128, 16, 128])
        m14 = view(RSC, 38912, BF16, [128, 16, 2, HD])
        pIn = view(RSC, 43008, BF16, [128, 2, 384])
        ptIn = view(RSC, 44544, BF16, [128, 2, 384])
        diagI = view(RSC, 46080, BF16, [128, 128])
        wobuf = view(RXA, 0, BF16, [128, NC, D])
        w1b = [view(RSC, 12288, BF16, [128, NC, 256]),
               view(RSC, 16384, BF16, [128, NC, 256])]
        w2b = [view(RSC, 20480, BF16, [128, D]),
               view(RSC, 22528, BF16, [128, D])]
        gelB = view(RSC, 24576, BF16, [128, 512])
        Hlo = view(ROF, 0, BF16, [128, 16, 1024])
        Hhi = view(RSC, 0, BF16, [128, 6, 1024])
        x1rl = view(RSC, 0, F32, [128, 2, S])

        def H_ap(j, lo, w):
            return (Hlo[:, j, lo:lo + w] if j < 16
                    else Hhi[:, j - 16, lo:lo + w])

        def ps_oc(oc):
            if oc == 0:
                return psA[:, :]
            if oc == 1:
                return psB[:, :]
            if oc == 2:
                return psC[:, :]
            if oc < 7:
                return psS[:, (oc - 3) * 512:(oc - 2) * 512]
            return psT[:, :]

        def g_dma(dst, src, ctr, wait=(), slow=False):
            def fn(g, d2=dst, s2=src, sl=slow):
                if sl:
                    return g.dma_start(d2, s2, allow_slow_non_contiguous=True)
                return g.dma_start(d2, s2)
            E("g", fn, wait=wait, inc=ctr, inck=16)

        def y_dma(dst, src, ctr, wait=(), slow=False):
            def fn(y, d2=dst, s2=src, sl=slow):
                if sl:
                    return y.dma_start(d2, s2, allow_slow_non_contiguous=True)
                return y.dma_start(d2, s2)
            E("y", fn, wait=wait, inc=ctr, inck=16)

        y_dma(Y[:, :, :], d3("x_fm"), xD)
        g_dma(cEYE, dr["eye"], cDg)
        g_dma(cTRIC, dr["tri_c"], cDg)
        g_dma(cTRIG, dr["tri_g"], cDg)
        g_dma(cWENT, dr["owent"].rearrange("(c p) o -> p (c o)", p=128), cDg,
              slow=True)
        g_dma(cC8, dr["cos8"], cDg)
        g_dma(cS8, dr["sin8"], cDg)
        g_dma(cCI, dr["cosi"], cDg)
        g_dma(cSI, dr["sini"], cDg)
        NCDG = cDg.n
        y_dma(cBQK, dcol(dr["obqkv"][0:2048]), cDs, slow=True)
        y_dma(cGOUT, dcol(dr["olng"]), cDs, slow=True)
        y_dma(cBOUT, dcol(dr["olnb"]), cDs, slow=True)
        y_dma(cBO01, dcol(dr["obout"]), cDs, slow=True)
        y_dma(cBENT, dr["obent"], cDs)
        y_dma(cE4[0:4, :, :], dr["e4"], cDs)
        NCDS = cDs.n

        def init_consts(v):
            v.memset(cE8[:, :, :], 0.0)
            for j in range(4):
                v.memset(cE8[:, j, j:j + 1], 1.0)
            v.memset(cONE1[:, :], 1.0)
            v.memset(cPOOL[:, 0:1], 1.25)
            v.memset(cPOOL[:, 1:2], 0.1)
            v.memset(cPOOL[:, 2:3], -0.5)
            v.memset(cPOOL[:, 3:4], 1e-38)
            v.memset(cPOOL[:, 4:5], 1.0 + 1e-6)
            v.memset(cPOOL[:, 5:6], 1e-11)
            v.memset(cPOOL[:, 6:7], 1e-5)
            v.memset(cPOOL[:, 7:8], 0.0)
            return v.memset(cNEG1[:, :], -1.0)
        E("v", init_consts, inc=V)
        V_CONST = V.n
        E("v", lambda v: v.tensor_scalar_mul(cBQK[:, 0:16:2], cBQK[:, 0:16:2], 1.25),
          wait=[(cDs, NCDS)], inc=V)
        E("v", lambda v: v.tensor_scalar_mul(cBO01[:, :], cBO01[:, :], 0.1), inc=V)

        def emit_stats(src_f32, src_bf, shifted):
            pre = BART([(xD, 16), (cDg, NCDG), (T, T.n)])
            for c in range(NC):
                if shifted and c >= 4:
                    E("s", lambda s, c=c: s.memzero(src_bf[:, c, 0:2]), wait=pre)
                    E("s", lambda s, c=c: s.activation(
                        src_bf[:, c, 1:S], src_f32[:, c, 0:S - 1], AF.Identity),
                      inc=SG)
                else:
                    E("s", lambda s, c=c: s.activation(
                        src_bf[:, c, :], src_f32[:, c, :], AF.Identity),
                      wait=pre, inc=SG)
            sCAST = SG.n
            firstC = True
            firstB = True
            for c in range(NC):
                E("v", lambda v, c=c: v.tensor_mul(
                    xsq[:, :], src_bf[:, c, :], src_bf[:, c, :]),
                  wait=[(SG, sCAST), (V, V_CONST), (T, T.n)], inc=V)
                vSq = V.n
                for j in range(4):
                    last = (c == NC - 1) and (j == 3)
                    E("t", lambda t, c=c, j=j, st=firstC, last=last: t.matmul(
                        psC[0:4, :], cE8[:, j, :],
                        src_bf[:, c, j * 512:(j + 1) * 512],
                        start=st, stop=last),
                      wait=BART([(SG, sCAST)]) if firstC else [])
                    firstC = False
                for j in range(4):
                    last = (c == NC - 1) and (j == 3)
                    E("t", lambda t, j=j, c=c, st=firstB, last=last: t.matmul(
                        psB[0:4, :], cE8[:, j, :],
                        xsq[:, j * 512:(j + 1) * 512],
                        start=st, stop=last),
                      wait=(BART([(V, vSq)]) if firstB else [(V, vSq)]),
                      inc=T)
                    firstB = False
            tSt = T.n
            E("v", lambda v: v.tensor_copy(cSTAT[:, :], psC[0:4, :]),
              wait=[(T, tSt)])
            E("v", lambda v: v.tensor_copy(cST2[:, :], psB[0:4, :]),
              wait=[(T, tSt)], inc=V)
            return V.n

        def emit_row_cs(vStats, mode):
            E("v", lambda v: v.tensor_scalar_mul(cMU[:, :], cSTAT[:, :], 1.0 / D),
              wait=[(V, vStats)])
            E("v", lambda v: v.tensor_scalar_mul(cROW[:, :], cST2[:, :], 1.0 / D))
            E("v", lambda v: v.tensor_mul(cCS[:, :], cMU[:, :], cMU[:, :]))
            E("v", lambda v: v.tensor_sub(cROW[:, :], cROW[:, :], cCS[:, :]), inc=V)
            vVar = V.n
            if mode == "outer":
                E("s", lambda s: s.activation(cROW[:, :], cROW[:, :], AF.Ln,
                                              bias=cPOOL[0:4, 5:6],
                                              scale=cPOOL[0:4, 4:5]),
                  wait=[(V, vVar)])
            else:
                E("s", lambda s: s.activation(cROW[:, :], cROW[:, :], AF.Ln,
                                              bias=cPOOL[0:4, 6:7], scale=1.0),
                  wait=[(V, vVar)])
            E("s", lambda s: s.activation(cCS[:, :], cROW[:, :], AF.Exp,
                                          scale=cPOOL[0:4, 2:3]), inc=SG)
            return SG.n

        def emit_normalize(sCS, src_f32, dst, shifted, affine, out_f32=None):
            zcol = cPOOL[:, 7:8]
            for blk in range(4):
                E("t", lambda t, blk=blk: t.matmul(
                    psA[:, :], cE4[0:4, blk, :], cMU[:, :], start=True, stop=True),
                  wait=BART([(SG, sCS), (V, V_CONST)]))
                E("t", lambda t, blk=blk: t.matmul(
                    psB[:, :], cE4[0:4, blk, :], cCS[:, :], start=True, stop=True),
                  inc=T)
                tBC = T.n
                for c in range(NC):
                    lo = blk * 512
                    sh = shifted and c >= 4

                    def sub(v, c=c, lo=lo, sh=sh):
                        if not sh:
                            return v.tensor_sub(
                                tscr[:, :], src_f32[:, c, lo:lo + 512], psA[:, :])
                        if lo == 0:
                            v.tensor_sub(tscr[:, 0:1], zcol, psA[:, 0:1])
                            return v.tensor_sub(
                                tscr[:, 1:512], src_f32[:, c, 0:511], psA[:, 1:512])
                        return v.tensor_sub(
                            tscr[:, :], src_f32[:, c, lo - 1:lo + 511], psA[:, :])
                    E("v", sub, wait=[(T, tBC), (SG, SG.n)])
                    if out_f32 is not None:
                        E("v", lambda v, c=c, lo=lo: v.tensor_mul(
                            out_f32[:, c, lo:lo + 512], tscr[:, :], psB[:, :]),
                          inc=V)
                    else:
                        E("v", lambda v: v.tensor_mul(
                            tscr[:, :], tscr[:, :], psB[:, :]), inc=V)
                        vN = V.n
                        if affine is not None:
                            gg, bb = affine
                            E("s", lambda s, c=c, lo=lo, gg=gg, bb=bb: s.activation(
                                dst[:, c, lo:lo + 512], tscr[:, :], AF.Identity,
                                bias=bb[:, c:c + 1], scale=gg[:, c:c + 1]),
                              wait=[(V, vN)], inc=SG)
                        else:
                            E("s", lambda s, c=c, lo=lo: s.activation(
                                dst[:, c, lo:lo + 512], tscr[:, :], AF.Identity),
                              wait=[(V, vN)], inc=SG)
            return V.n, SG.n

        vSt = emit_stats(Y, XA, False)
        sCS = emit_row_cs(vSt, "outer")
        vXN, sXN = emit_normalize(sCS, Y, XA, False, (cGOUT, cBOUT))

        if stage == "xn":
            for c in range(NC):
                E("s", lambda s, c=c: s.activation(
                    Y[:, c, :], XA[:, c, :], AF.Identity),
                  wait=[(SG, sXN), (V, vXN)], inc=SG)
            _finish(mk, E, o_fm, Y, stO, [(SG, SG.n)])
            _replay(mk, nc)
            return nc

        y_dma(spX.rearrange("(c p) t -> p c t", p=128), Y[:, :, :], spXS,
              wait=[(V, vXN), (SG, sXN)])

        for blk in range(4):
            for c in range(NC):
                E("t", lambda t, blk=blk, c=c: t.matmul(
                    psS[0:1, blk * 512:(blk + 1) * 512], cWENT[:, c:c + 1],
                    XA[:, c, blk * 512:(blk + 1) * 512],
                    start=(c == 0), stop=(c == NC - 1)),
                  wait=BART([(SG, sXN), (cDg, NCDG)]),
                  inc=(T if c == NC - 1 else None))
        tGate = T.n
        E("s", lambda s: s.activation(gRow[:, :], psS[0:1, 0:2048], AF.Sigmoid,
                                      bias=cBENT[0:1, 0:1]),
          wait=[(T, tGate), (cDs, NCDS)], inc=SG)
        E("v", lambda v: v.tensor_scalar(gRow[:, :], gRow[:, :], 0.1, 2.0,
                                         ALU.max, ALU.min),
          wait=[(SG, SG.n)], inc=V)
        y_dma(spGate, gRow[0:1, :], gcS, wait=[(V, V.n)])
        y_dma(cGCOL[:, :], spGate.rearrange("(a p) -> p a", p=128), gcS,
              wait=[(gcS, gcS.n)], slow=True)

        g_dma(wvSB, dw3(dr["owqkv"])[:, :, 2048:3072], wvS, wait=[(spXS, 16)])
        y_dma(bvRow, drow(dr["obqkv"][2048:3072]), cDs)
        NCDS2 = cDs.n
        for h in range(2):
            E("t", lambda t, h=h: t.matmul(
                psA[:, :], cONE1[:, :], bvRow[0:1, h * 512:(h + 1) * 512],
                start=True, stop=True),
              wait=BART([(cDs, NCDS2), (V, V_CONST), (spXS, 16)]), inc=T)
            E("v", lambda v, h=h: v.tensor_copy(
                bvR[:, h * 512:(h + 1) * 512], psA[:, :]),
              wait=[(T, T.n)], inc=V)
        vBv = V.n
        for tt in range(16):
            for vb in range(2):
                pp = psA if vb == 0 else psB
                for c in range(NC):
                    E("t", lambda t, tt=tt, vb=vb, c=c, pp=pp: t.matmul(
                        pp[:, :], XA[:, c, tt * 128:(tt + 1) * 128],
                        wvSB[:, c, vb * 512:(vb + 1) * 512],
                        start=(c == 0), stop=(c == NC - 1)),
                      wait=BART([(wvS, 16), (SG, sXN), (V, vBv)]),
                      inc=(T if c == NC - 1 else None))
                tV = T.n
                E("v", lambda v, pp=pp, vb=vb: v.tensor_add(
                    tvs[:, :], pp[:, :], bvR[:, vb * 512:(vb + 1) * 512]),
                  wait=[(T, tV), (SG, SG.n)], inc=V)
                E("s", lambda s, tt=tt, vb=vb: s.activation(
                    vTM[:, tt, vb * 512:(vb + 1) * 512], tvs[:, :], AF.Identity,
                    scale=cGCOL[:, tt:tt + 1]),
                  wait=[(V, V.n), (gcS, 32)], inc=SG)
        sVTM = SG.n

        wq_last = [0, 0]
        NPrun = 1 if stage == "dbg0" else NP
        for p in range(NPrun):
            par = p % 2
            wctr = wS0 if par == 0 else wS1
            wbuf = wqhSB[par]
            g_dma(wbuf, dw3(dr["owqkv"])[:, :, p * 256:(p + 1) * 256], wctr,
                  wait=[(T, wq_last[par]), (spXS, 16)])
            nw = wctr.n
            for qk in range(2):
                dst = qpSB if qk == 0 else kpSB
                for tb in range(4):
                    pp = psA if tb % 2 == 0 else psB
                    for c in range(NC):
                        E("t", lambda t, qk=qk, tb=tb, c=c, pp=pp, wbuf=wbuf:
                          t.matmul(
                            pp[:, :], wbuf[:, c, qk * 128:qk * 128 + 128],
                            XA[:, c, tb * 512:(tb + 1) * 512],
                            start=(c == 0), stop=(c == NC - 1)),
                          wait=BART([(wctr, nw), (SG, sVTM)]),
                          inc=(T if c == NC - 1 else None))
                    tP = T.n

                    def ev(s, qk=qk, tb=tb, pp=pp, dst=dst, p=p):
                        sc = cPOOL[:, 0:1] if qk == 0 else 1.0
                        return s.activation(
                            dst[:, tb * 512:(tb + 1) * 512], pp[:, :], AF.Identity,
                            scale=sc, bias=cBQK[:, 2 * p + qk:2 * p + qk + 1])
                    E("s", ev, wait=[(T, tP)], inc=SG)
            wq_last[par] = T.n
            sProj = SG.n
            for h01 in range(2):
                b0, b1 = 64 * h01, 64 * h01 + 64
                head = 2 * p + h01
                for qt in range(16):
                    nk = (qt + 1) * 128
                    nbank = (nk + 511) // 512
                    for kb in range(nbank):
                        klo = kb * 512
                        kw = min(512, nk - klo)
                        lastb = kb == nbank - 1
                        E("t", lambda t, qt=qt, klo=klo, kw=kw, b0=b0, b1=b1,
                          lastb=lastb: t.matmul(
                            psS[:, klo:klo + kw],
                            qpSB[b0:b1, qt * 128:(qt + 1) * 128],
                            kpSB[b0:b1, klo:klo + kw],
                            start=True, stop=not lastb),
                          wait=BART([(SG, sProj)]))
                        if lastb:
                            E("t", lambda t, qt=qt: t.matmul(
                                psS[:, qt * 128:(qt + 1) * 128], cEYE[:, :],
                                cTRIC[:, :], start=False, stop=True), inc=T)
                    tS = T.n
                    E("v", lambda v, nk=nk: v.reduce_max(
                        cNM[:, :], psS[:, 0:nk], axis=AX.X, negate=True),
                      wait=[(T, tS), (SG, SG.n)], inc=V)
                    E("s", lambda s, nk=nk: s.activation(
                        pSB[:, 0:nk], psS[:, 0:nk], AF.Exp, bias=cNM[:, :]),
                      wait=[(V, V.n)], inc=SG)
                    E("v", lambda v, nk=nk: v.reduce_sum(
                        cELL2[:, :], pSB[:, 0:nk], axis=AX.X),
                      wait=[(SG, SG.n)], inc=V)
                    E("s", lambda s: s.activation(cELL[:, :], cELL2[:, :], AF.Ln),
                      wait=[(V, V.n)])
                    E("s", lambda s: s.activation(cRC[:, :], cELL[:, :], AF.Exp,
                                                  scale=cNEG1[:, :]), inc=SG)
                    E("v", lambda v: v.tensor_scalar_mul(
                        diagO[:, :], cEYE[:, :], cRC[:, :]),
                      wait=[(T, T.n), (SG, SG.n)], inc=V)
                    vDiag = V.n
                    nkc = qt + 1
                    for g0 in range(0, nkc, 4):
                        gn = min(4, nkc - g0)
                        for ki in range(gn):
                            kc = g0 + ki
                            E("t", lambda t, kc=kc, ki=ki: t.matmul(
                                psT[:, ki * 128:(ki + 1) * 128],
                                pSB[:, kc * 128:(kc + 1) * 128], diagO[:, :],
                                start=True, stop=True),
                              wait=BART([(V, vDiag)]),
                              inc=(T if ki == gn - 1 else None))
                        tPT = T.n
                        E("v", lambda v, g0=g0, gn=gn: v.tensor_copy(
                            ptSB[:, g0 * 128:(g0 + gn) * 128], psT[:, 0:gn * 128]),
                          wait=[(T, tPT)], inc=V)
                        vPT = V.n
                        for ki in range(gn):
                            kc = g0 + ki
                            E("t", lambda t, kc=kc, head=head, h01=h01,
                              st=(kc == 0), sp=(kc == nkc - 1): t.matmul(
                                psS[0:64, 0:128] if h01 == 0
                                else psS[64:128, 0:128],
                                vTM[:, kc, head * 64:head * 64 + 64],
                                ptSB[:, kc * 128:(kc + 1) * 128],
                                start=st, stop=sp,
                                tile_position=((0, 64) if h01 == 1 else None)),
                              wait=[(V, vPT)],
                              inc=(T if kc == nkc - 1 else None))
                    tPV = T.n
                    ch, po = (head * 64) // 128, (head * 64) % 128
                    E("v", lambda v, ch=ch, po=po, qt=qt, h01=h01: v.tensor_copy(
                        OF[po:po + 64, ch, qt * 128:(qt + 1) * 128],
                        psS[0:64, 0:128] if h01 == 0 else psS[64:128, 0:128]),
                      wait=[(T, tPV)], inc=V)
                    if stage == "dbg0" and h01 == 0 and qt == 3:
                        E("s", lambda s: s.activation(
                            Y[:, 0, 0:512], pSB[:, 0:512], AF.Identity),
                          wait=[(T, tPV)], inc=SG)
                        E("s", lambda s: s.activation(
                            Y[:, 2, 0:1], cELL[:, :], AF.Identity), inc=SG)
                        E("s", lambda s: s.activation(
                            Y[:, 2, 1:2], cNM[:, :], AF.Identity), inc=SG)
                        E("s", lambda s: s.activation(
                            Y[:, 3, 0:512], ptSB[:, 0:512], AF.Identity), inc=SG)
                        E("v", lambda v: v.tensor_copy(
                            Y[0:64, 5, 0:128], psS[0:64, 0:128]),
                          wait=[(T, tPV)], inc=V)
                        _finish(mk, E, o_fm, Y, stO, [(SG, SG.n), (V, V.n)])
                        _replay(mk, nc)
                        return nc
        vAttO = V.n

        if stage == "of":
            for c in range(NC):
                E("s", lambda s, c=c: s.activation(
                    Y[:, c, :], OF[:, c, :], AF.Identity),
                  wait=[(V, vAttO), (T, T.n), (SG, SG.n)], inc=SG)
            _finish(mk, E, o_fm, Y, stO, [(SG, SG.n)])
            _replay(mk, nc)
            return nc

        g_dma(woSBo, dw3(dr["owout"]), woS, wait=[(T, T.n)])
        nwo = woS.n
        rl_last = [vAttO, vAttO]
        for oc in range(NC):
            for tb in range(4):
                rb = (oc * 4 + tb) % 2
                rctr = rlS0 if rb == 0 else rlS1
                y_dma(xrl[:, rb, :],
                      spX.rearrange("(c p) t -> p c t", p=128)
                      [:, oc, tb * 512:(tb + 1) * 512],
                      rctr, wait=[(spXS, 16), (V, rl_last[rb])])
                nrl = rctr.n
                pp = psA if tb % 2 == 0 else psB
                for c in range(NC):
                    E("t", lambda t, oc=oc, tb=tb, c=c, pp=pp: t.matmul(
                        pp[:, :], woSBo[:, c, oc * 128:(oc + 1) * 128],
                        OF[:, c, tb * 512:(tb + 1) * 512],
                        start=(c == 0), stop=(c == NC - 1)),
                      wait=BART([(woS, nwo), (V, vAttO)]),
                      inc=(T if c == NC - 1 else None))
                tWo = T.n
                E("s", lambda s, pp=pp, oc=oc: s.activation(
                    twx[:, :], pp[:, :], AF.Identity, scale=cPOOL[:, 1:2],
                    bias=cBO01[:, oc:oc + 1]),
                  wait=[(T, tWo), (V, V.n)], inc=SG)
                sWo = SG.n
                E("v", lambda v, oc=oc, tb=tb, rb=rb: v.tensor_add(
                    x1v[:, oc, tb * 512:(tb + 1) * 512], twx[:, :], xrl[:, rb, :]),
                  wait=[(SG, sWo), (rctr, nrl)], inc=V)
                rl_last[rb] = V.n
            y_dma(spX1.rearrange("(c p) t -> p c t", p=128)[:, oc, :],
                  x1v[:, oc, :], spX1S, wait=[(V, V.n)])
        vX1 = V.n

        if stage == "x1":
            _finish(mk, E, o_fm, Y, stO, [(V, vX1)])
            _replay(mk, nc)
            return nc

        vSt2 = emit_stats(x1v, XA, False)
        sCS2 = emit_row_cs(vSt2, "plain")
        vH2, _ = emit_normalize(sCS2, x1v, None, False, None, out_f32=Y)
        iwq_last0_init = T.n
        iwq_vgate0_init = V.n

        if stage == "h2":
            _finish(mk, E, o_fm, Y, stO, [(V, vH2)])
            _replay(mk, nc)
            return nc

        iwq_last = [iwq_last0_init]
        iwq_vgate = [iwq_vgate0_init]
        w1_last = [iwq_last0_init, iwq_last0_init]
        w2_last = [iwq_last0_init, iwq_last0_init]
        wo_last = [0]

        def inner_layer(L):
            nld0 = ldS.n
            y_dma(cLGB[:, 0:8], dcol(dr["ilng"][L]), ldS,
                  wait=[(SG, SG.n), (V, V.n)], slow=True)
            y_dma(cLGB[:, 8:16], dcol(dr["ilnb"][L]), ldS, slow=True)
            y_dma(cB1[:, 0:42],
                  dr["ib1"][L][0:5376].rearrange("(j p) -> p j", p=128),
                  ldS, slow=True)
            y_dma(cB1[0:42, 42:43], drow(dr["ib1"][L][5376:5418]).rearrange(
                "o n -> n o"), ldS, slow=True)
            y_dma(cB1[0:42, 43:44], drow(dr["ib1"][L][5418:5460]).rearrange(
                "o n -> n o"), ldS, slow=True)
            y_dma(cB2[:, :], dcol(dr["ib2"][L]), ldS, slow=True)
            y_dma(cQSR[0:1, 0:64], drow(dr["iqs"][L]), ldS)
            y_dma(cQSR[0:1, 128:192], drow(dr["iks"][L]), ldS)
            nld = ldS.n
            E("v", lambda v: v.tensor_scalar_mul(
                cQSR[0:1, 64:96], cQSR[0:1, 32:64], -1.0),
              wait=[(ldS, nld)])
            E("v", lambda v: v.tensor_copy(cQSR[0:1, 96:128], cQSR[0:1, 0:32]))
            E("v", lambda v: v.tensor_scalar_mul(
                cQSR[0:1, 192:224], cQSR[0:1, 160:192], -1.0))
            E("v", lambda v: v.tensor_copy(cQSR[0:1, 224:256], cQSR[0:1, 128:160]),
              inc=V)
            vQSR = V.n
            for r in range(4):
                E("t", lambda t, r=r: t.matmul(
                    psA[:, 0:64], cONE1[:, :], cQSR[0:1, r * 64:(r + 1) * 64],
                    start=True, stop=True),
                  wait=BART([(V, vQSR)]), inc=T)
                E("v", lambda v, r=r: v.tensor_copy(qsRep[:, r, :], psA[:, 0:64]),
                  wait=[(T, T.n)], inc=V)
            vRep = V.n
            for (tab, trig, r) in ((AQ, cC8, 0), (BQ, cS8, 1), (AK, cCI, 2),
                                   (BK, cSI, 3)):
                E("v", lambda v, tab=tab, trig=trig, r=r: v.tensor_tensor(
                    tab[:, :, :], trig[:, :, :],
                    qsRep[:, r:r + 1, :].broadcast_to([128, 16, 64]), ALU.mult),
                  wait=[(V, vRep), (cDg, NCDG)], inc=V)
            vAB = V.n

            for c in range(NC):
                if c >= 4:
                    E("s", lambda s, c=c: s.memzero(XA[:, c, 0:2]),
                      wait=[(T, T.n)])
                    E("s", lambda s, c=c: s.activation(
                        XA[:, c, 1:S], Y[:, c, 0:S - 1], AF.Identity), inc=SG)
                else:
                    E("s", lambda s, c=c: s.activation(
                        XA[:, c, :], Y[:, c, :], AF.Identity),
                      wait=[(T, T.n)], inc=SG)
            sXAi = SG.n

            for p in range(NP):
                inner_pair(L, p, sXAi, vAB)
            vAtt = V.n

            g_dma(wobuf, dw3(dr["iwo"][L]), woS,
                  wait=[(T, T.n), (SG, SG.n)])
            nwo2 = woS.n
            for oc in range(NC):
                for tb in range(4):
                    pp = psA if tb % 2 == 0 else psB
                    for c in range(NC):
                        E("t", lambda t, oc=oc, tb=tb, c=c, pp=pp: t.matmul(
                            pp[:, :], wobuf[:, c, oc * 128:(oc + 1) * 128],
                            OF[:, c, tb * 512:(tb + 1) * 512],
                            start=(c == 0), stop=(c == NC - 1)),
                          wait=BART([(woS, nwo2), (V, vAtt)]),
                          inc=(T if c == NC - 1 else None))
                    tWo = T.n
                    E("v", lambda v, oc=oc, tb=tb, pp=pp: v.tensor_add(
                        Y[:, oc, tb * 512:(tb + 1) * 512], pp[:, :],
                        Y[:, oc, tb * 512:(tb + 1) * 512]),
                      wait=[(T, tWo)], inc=V)
            ffn_tgate = T.n
            if stage == ("att", L):
                return

            vStf = emit_stats(Y, XA, True)
            if stage == ("fst", L):
                E("s", lambda s: s.activation(
                    Y[:, 0, 0:2], Y[:, 0, 0:2], AF.Identity),
                  wait=[(V, vStf)], inc=SG)
                return "fxn"
            sCSf = emit_row_cs(vStf, "plain")
            if stage == ("fs2", L):
                E("s", lambda s: s.activation(
                    Y[:, 0, 0:2], Y[:, 0, 0:2], AF.Identity),
                  wait=[(SG, sCSf)], inc=SG)
                return "fxn"
            vXN2, sXN2 = emit_normalize(
                sCSf, Y, XA, True, (cLGB[:, 0:8], cLGB[:, 8:16]))
            if stage == ("fxn", L):
                for c in range(NC):
                    E("s", lambda s, c=c: s.activation(
                        Y[:, c, :], XA[:, c, :], AF.Identity),
                      wait=[(SG, sXN2), (V, vXN2)], inc=SG)
                return "fxn"
            w1_last[0] = max(w1_last[0], ffn_tgate)
            w1_last[1] = max(w1_last[1], ffn_tgate)
            w2_last[0] = max(w2_last[0], ffn_tgate)
            w2_last[1] = max(w2_last[1], ffn_tgate)
            for th in range(2):
                ffn_half(L, th, vXN2, sXN2, nld)

        def inner_pair(L, p, sXAi, vAB):
            g_dma(wqp, dw3(dr["iwqkv"][L])[:, :, p * 384:(p + 1) * 384], wS0,
                  wait=[(T, iwq_last[0]), (V, iwq_vgate[0])])
            nw = wS0.n
            for tt in range(16):
                pp = psA if tt % 2 == 0 else psB
                for c in range(NC):
                    E("t", lambda t, tt=tt, c=c, pp=pp: t.matmul(
                        pp[:, 0:384], XA[:, c, tt * 128:(tt + 1) * 128],
                        wqp[:, c, :], start=(c == 0), stop=(c == NC - 1)),
                      wait=BART([(wS0, nw), (SG, sXAi)]),
                      inc=(T if c == NC - 1 else None))
                tQ = T.n
                E("v", lambda v, tt=tt, pp=pp: v.tensor_copy(
                    qkvTM[:, tt, :], pp[:, 0:384]),
                  wait=[(T, tQ)], inc=V)
            iwq_last[0] = T.n
            vQKV = V.n
            E("v", lambda v: v.tensor_mul(
                sqB[:, :, :], qkvTM[:, :, 0:256], qkvTM[:, :, 0:256]),
              wait=[(V, vQKV)])
            E("v", lambda v: v.tensor_reduce(
                cRN[:, :], sq64[:, :, :], axis=AX.X, op=ALU.add), inc=V)
            vNsq = V.n
            E("s", lambda s: s.activation(cRN[:, :], cRN[:, :], AF.Ln,
                                          bias=cPOOL[:, 3:4], scale=1.0),
              wait=[(V, vNsq)])
            E("s", lambda s: s.activation(cRN[:, :], cRN[:, :], AF.Exp,
                                          scale=cPOOL[:, 2:3]), inc=SG)
            E("v", lambda v: v.tensor_scalar_min(cRN[:, :], cRN[:, :], 1e12),
              wait=[(SG, SG.n)], inc=V)
            for qk in range(2):
                Atab = AQ if qk == 0 else AK
                Btab = BQ if qk == 0 else BK
                sub0 = 2 * qk
                E("v", lambda v, sub0=sub0: v.tensor_copy(
                    qrB[:, :, :, 0:32], qv6[:, :, sub0:sub0 + 2, 32:64]),
                  wait=[(V, V.n), (V, vAB)])
                E("v", lambda v, sub0=sub0: v.tensor_copy(
                    qrB[:, :, :, 32:64], qv6[:, :, sub0:sub0 + 2, 0:32]), inc=V)
                E("v", lambda v, qk=qk, Atab=Atab: v.tensor_tensor(
                    m14[:, :, :, :], qv6[:, :, 2 * qk:2 * qk + 2, :],
                    Atab[:, :, :].rearrange("p a (o c) -> p a o c", o=1)
                    .broadcast_to([128, 16, 2, 64]), ALU.mult),
                  wait=[(V, V.n)])
                E("v", lambda v, qk=qk, Btab=Btab: v.tensor_tensor(
                    q2k4[:, :, 2 * qk:2 * qk + 2, :], qrB[:, :, :, :],
                    Btab[:, :, :].rearrange("p a (o c) -> p a o c", o=1)
                    .broadcast_to([128, 16, 2, 64]), ALU.mult))
                E("v", lambda v, qk=qk: v.tensor_add(
                    q2k4[:, :, 2 * qk:2 * qk + 2, :],
                    q2k4[:, :, 2 * qk:2 * qk + 2, :], m14[:, :, :, :]), inc=V)
            vRope = V.n
            for tt in range(16):
                for j in range(4):
                    qk, sub = (0 if j < 2 else 1), j % 2
                    E("v", lambda v, tt=tt, j=j: v.tensor_scalar_mul(
                        diagI[:, :], cEYE[:, :],
                        cRN[:, tt * 4 + j:tt * 4 + j + 1]),
                      wait=[(V, vRope), (T, T.n)], inc=V)
                    vD = V.n
                    E("t", lambda t, tt=tt, qk=qk, sub=sub: t.matmul(
                        psT[0:64, qk * 128:qk * 128 + 128] if sub == 0
                        else psT[64:128, qk * 128:qk * 128 + 128],
                        q2k4[:, tt, 2 * qk + sub, :], diagI[:, :],
                        start=True, stop=True,
                        tile_position=((0, 64) if sub == 1 else None)),
                      wait=BART([(V, vD)]), inc=T)
                tTr = T.n
                E("v", lambda v, tt=tt: v.tensor_copy(
                    qkFM[:, 0, tt * 128:(tt + 1) * 128], psT[:, 0:128]),
                  wait=[(T, tTr)])
                E("v", lambda v, tt=tt: v.tensor_copy(
                    qkFM[:, 1, tt * 128:(tt + 1) * 128], psT[:, 128:256]),
                  inc=V)
            vFM = V.n
            for h01 in range(2):
                b0, b1 = 64 * h01, 64 * h01 + 64
                head = 2 * p + h01
                for qt in range(16):
                    sl = (qt % 2) * 512
                    klo = max(0, qt - 2) * 128
                    n0 = qt * 128 - klo
                    nk = n0 + 128
                    E("t", lambda t, qt=qt, sl=sl, b0=b0, b1=b1, klo=klo, nk=nk:
                      t.matmul(psS[:, sl:sl + nk],
                               qkFM[b0:b1, 0, qt * 128:(qt + 1) * 128],
                               qkFM[b0:b1, 1, klo:klo + nk],
                               start=True, stop=False),
                      wait=BART([(V, vFM)]))
                    if qt >= 2:
                        E("t", lambda t, sl=sl: t.matmul(
                            psS[:, sl:sl + 128], cEYE[:, :], cTRIG[:, :],
                            start=False, stop=False))
                    E("t", lambda t, sl=sl, n0=n0: t.matmul(
                        psS[:, sl + n0:sl + n0 + 128], cEYE[:, :], cTRIC[:, :],
                        start=False, stop=True), inc=T)
                    tS = T.n
                    E("v", lambda v, sl=sl, nk=nk: v.reduce_max(
                        cNM[:, :], psS[:, sl:sl + nk], axis=AX.X, negate=True),
                      wait=[(T, tS), (SG, SG.n)], inc=V)
                    E("s", lambda s, sl=sl, nk=nk, qt=qt: s.activation(
                        pIn[:, qt % 2, 0:nk], psS[:, sl:sl + nk], AF.Exp,
                        bias=cNM[:, :]),
                      wait=[(V, V.n)], inc=SG)
                    E("v", lambda v, nk=nk, qt=qt: v.reduce_sum(
                        cELL2[:, :], pIn[:, qt % 2, 0:nk], axis=AX.X),
                      wait=[(SG, SG.n)], inc=V)
                    E("s", lambda s: s.activation(cELL[:, :], cELL2[:, :], AF.Ln),
                      wait=[(V, V.n)])
                    E("s", lambda s: s.activation(cRC[:, :], cELL[:, :], AF.Exp,
                                                  scale=cNEG1[:, :]), inc=SG)
                    E("v", lambda v: v.tensor_scalar_mul(
                        diagI[:, :], cEYE[:, :], cRC[:, :]),
                      wait=[(T, T.n), (SG, SG.n)], inc=V)
                    vDg = V.n
                    nkc = nk // 128
                    for ki in range(nkc):
                        E("t", lambda t, qt=qt, ki=ki: t.matmul(
                            psT[:, ki * 128:(ki + 1) * 128],
                            pIn[:, qt % 2, ki * 128:(ki + 1) * 128], diagI[:, :],
                            start=True, stop=True),
                          wait=BART([(V, vDg)]),
                          inc=(T if ki == nkc - 1 else None))
                    tPT = T.n
                    E("v", lambda v, qt=qt, nk=nk: v.tensor_copy(
                        ptIn[:, qt % 2, 0:nk], psT[:, 0:nk]),
                      wait=[(T, tPT)], inc=V)
                    vPT = V.n
                    for ki in range(nkc):
                        kc = klo // 128 + ki
                        E("t", lambda t, qt=qt, ki=ki, kc=kc, h01=h01, sl=sl,
                          st=(ki == 0), sp=(ki == nkc - 1): t.matmul(
                            psS[0:64, sl:sl + 128] if h01 == 0
                            else psS[64:128, sl:sl + 128],
                            qv6[:, kc, 4 + h01, :],
                            ptIn[:, qt % 2, ki * 128:(ki + 1) * 128],
                            start=st, stop=sp,
                            tile_position=((0, 64) if h01 == 1 else None)),
                          wait=[(V, vPT)],
                          inc=(T if ki == nkc - 1 else None))
                    tPV = T.n
                    ch, po = (head * 64) // 128, (head * 64) % 128
                    E("v", lambda v, ch=ch, po=po, qt=qt, h01=h01, sl=sl:
                      v.tensor_copy(
                        OF[po:po + 64, ch, qt * 128:(qt + 1) * 128],
                        psS[0:64, sl:sl + 128] if h01 == 0
                        else psS[64:128, sl:sl + 128]),
                      wait=[(T, tPV)], inc=V)

        def ffn_half(L, th, vXN2, sXN2, nld):
            t0 = th * 1024
            for jp in range(NJ):
                wa = JW[jp]
                par = jp % 2
                wbuf = w1b[par]
                wctr = w1S0 if par == 0 else w1S1
                g_dma(wbuf[:, :, 0:2 * wa],
                      dw3(dr["iw1"][L])[:, :, jp * 256:jp * 256 + 2 * wa],
                      wctr, wait=[(T, w1_last[par])])
                nw = wctr.n
                for tb in range(2):
                    lo = t0 + tb * 512
                    hb = tb * 512
                    for c in range(NC):
                        E("t", lambda t, c=c, wa=wa, wbuf=wbuf, lo=lo: t.matmul(
                            psA[0:wa, :], wbuf[:, c, wa:2 * wa],
                            XA[:, c, lo:lo + 512],
                            start=(c == 0), stop=(c == NC - 1)),
                          wait=BART([(wctr, nw), (SG, sXN2), (V, vXN2)]),
                          inc=(T if c == NC - 1 else None))
                    tG = T.n
                    E("s", lambda s, wa=wa, jp=jp: s.activation(
                        gelB[0:wa, :], psA[0:wa, :], AF.Gelu,
                        bias=cB1[0:wa, 2 * jp + 1:2 * jp + 2]),
                      wait=[(T, tG), (ldS, nld), (V, V.n)], inc=SG)
                    sGl = SG.n
                    for c in range(NC):
                        E("t", lambda t, c=c, wa=wa, wbuf=wbuf, lo=lo: t.matmul(
                            psB[0:wa, :], wbuf[:, c, 0:wa],
                            XA[:, c, lo:lo + 512],
                            start=(c == 0), stop=(c == NC - 1)),
                          wait=BART(),
                          inc=(T if c == NC - 1 else None))
                    tA = T.n
                    E("v", lambda v, wa=wa, jp=jp, hb=hb: v.scalar_tensor_tensor(
                        H_ap(jp, hb, 512)[0:wa, :], psB[0:wa, :],
                        cB1[0:wa, 2 * jp:2 * jp + 1], gelB[0:wa, :],
                        op0=ALU.add, op1=ALU.mult),
                      wait=[(T, tA), (SG, sGl)], inc=V)
                w1_last[par] = T.n
            vH = V.n
            for tb in range(2):
                lo = t0 + tb * 512
                hb = tb * 512
                for jp in range(NJ):
                    wa = JW[jp]
                    par = jp % 2
                    wbuf = w2b[par]
                    wctr = w2S0 if par == 0 else w2S1
                    g_dma(wbuf[0:wa, :], dr["iw2"][L][jp * 128:jp * 128 + wa, :],
                          wctr, wait=[(T, w2_last[par])])
                    nw2 = wctr.n
                    for oc in range(8):
                        E("t", lambda t, jp=jp, wa=wa, oc=oc, wbuf=wbuf, hb=hb:
                          t.matmul(
                            ps_oc(oc), wbuf[0:wa, oc * 128:(oc + 1) * 128],
                            H_ap(jp, hb, 512)[0:wa, :],
                            start=(jp == 0), stop=(jp == NJ - 1)),
                          wait=(BART([(wctr, nw2), (V, vH)])
                                if jp == 0 and oc == 0
                                else [(wctr, nw2)]),
                          inc=(T if oc == 7 else None))
                    w2_last[par] = T.n
                tW2 = T.n
                for oc in range(8):
                    E("v", lambda v, oc=oc, lo=lo: v.scalar_tensor_tensor(
                        Y[:, oc, lo:lo + 512], ps_oc(oc), cB2[:, oc:oc + 1],
                        Y[:, oc, lo:lo + 512], op0=ALU.add, op1=ALU.add),
                      wait=[(T, tW2), (ldS, nld)], inc=V)

        for L in range(n_layers):
            r = inner_layer(L)
            iwq_last[0] = T.n
            iwq_vgate[0] = V.n
            w1_last[0] = w1_last[1] = T.n
            w2_last[0] = w2_last[1] = T.n
            if r == "fxn":
                _finish(mk, E, o_fm, Y, stO, [(V, V.n), (T, T.n), (SG, SG.n)])
                _replay(mk, nc)
                return nc
            if stage == ("att", L) or stage == ("ffn", L):
                _finish(mk, E, o_fm, Y, stO, [(V, V.n), (T, T.n), (SG, SG.n)])
                _replay(mk, nc)
                return nc

        rl2_last = [0, 0]
        for c in range(NC):
            rb = c % 2
            rctr = rlS0 if rb == 0 else rlS1
            y_dma(x1rl[:, rb, :],
                  spX1.rearrange("(c p) t -> p c t", p=128)[:, c, :],
                  rctr, wait=[(spX1S, spX1S.n), (V, rl2_last[rb]),
                              (T, T.n), (SG, SG.n)])
            nrl = rctr.n
            E("v", lambda v, c=c, rb=rb: v.tensor_add(
                Y[:, c, :], Y[:, c, :], x1rl[:, rb, :]),
              wait=[(rctr, nrl)], inc=V)
            rl2_last[rb] = V.n
        _finish(mk, E, o_fm, Y, stO, [(V, V.n)])
        _replay(mk, nc)
    return nc


def _finish(mk, E, o_fm, src, stO, wait):
    E("y", lambda y: y.dma_start(
        o_fm.rearrange("(c p) t -> p c t", p=128), src[:, :, :]),
      wait=wait, inc=stO, inck=16)
    E("y", lambda y: None, wait=[(stO, 16)])


def _replay(mk, nc):
    with nc.Block() as block:
        for eng, reg in (("g", block.gpsimd), ("y", block.sync),
                         ("t", block.tensor), ("v", block.vector),
                         ("s", block.scalar)):
            ops = list(mk.q[eng])

            def body(e, ops=ops):
                for fn in ops:
                    fn(e)
            reg(body)


def kernel(**inputs):
    shared = _prep_shared(inputs)
    x = np.asarray(inputs["L0_attn_x"], dtype=np.float32)
    nc = build_program()
    in_maps = []
    for b in range(8):
        m = dict(shared)
        m["x_fm"] = np.ascontiguousarray(x[b].T)
        in_maps.append(m)
    res = run_bass_kernel_spmd(nc, in_maps, core_ids=list(range(8)))
    out = np.stack([np.ascontiguousarray(np.asarray(r["o_fm"]).T)
                    for r in res.results])
    return out.astype(np.float32)


# version 02
# speedup vs baseline: 1.2473x; 1.2473x over previous
import numpy as np
import concourse.bass as bass
import concourse.mybir as mybir
from concourse.bass_utils import run_bass_kernel_spmd
from contextlib import ExitStack

F32 = mybir.dt.float32
BF16 = mybir.dt.bfloat16
U8 = mybir.dt.uint8
AF = mybir.ActivationFunctionType
ALU = mybir.AluOpType
AX = mybir.AxisListType

D = 1024
S = 2048
NC = 8
NH = 16
HD = 64
NP = 8
DEPTH = 6
DI = 2730
NJ = 22
JW = [128] * 21 + [42]
NEG = -1e30


def _prep_shared(inputs):
    g = lambda k: np.ascontiguousarray(np.asarray(inputs[k], dtype=np.float32))
    d = {}
    wqkvT = g("L0_attn_w_qkv").T
    bqkv = g("L0_attn_b_qkv")
    idx = []
    for p in range(NP):
        idx += list(range(2 * p * HD, 2 * p * HD + 2 * HD))
        idx += list(range(D + 2 * p * HD, D + 2 * p * HD + 2 * HD))
    idx += list(range(2 * D, 3 * D))
    d["owqkv"] = np.ascontiguousarray(wqkvT[:, idx])
    d["obqkv"] = np.ascontiguousarray(bqkv[idx])
    d["owout"] = np.ascontiguousarray(g("L0_attn_w_out").T)
    d["obout"] = g("L0_attn_b_out")
    d["owent"] = np.ascontiguousarray(g("L0_attn_w_ent").T)
    d["obent"] = g("L0_attn_b_ent").reshape(1, 1)
    d["olng"] = g("L0_attn_ln_g")
    d["olnb"] = g("L0_attn_ln_b")
    Wq = g("L0_ffn_Wqkv")
    idxi = []
    for p in range(NP):
        idxi += list(range(p * 128, (p + 1) * 128))
        idxi += list(range(D + p * 128, D + (p + 1) * 128))
        idxi += list(range(2 * D + p * 128, 2 * D + (p + 1) * 128))
    d["iwqkv"] = np.ascontiguousarray(Wq[:, :, idxi])
    d["iwo"] = g("L0_ffn_Wo")
    W1 = g("L0_ffn_W1")
    b1 = g("L0_ffn_b1")
    idx1 = []
    for j in range(NJ):
        w = JW[j]
        idx1 += list(range(j * 128, j * 128 + w))
        idx1 += list(range(DI + j * 128, DI + j * 128 + w))
    d["iw1"] = np.ascontiguousarray(W1[:, :, idx1])
    d["ib1"] = np.ascontiguousarray(b1[:, idx1])
    d["iw2"] = g("L0_ffn_W2")
    d["ib2"] = g("L0_ffn_b2")
    d["ilng"] = g("L0_ffn_ln_g")
    d["ilnb"] = g("L0_ffn_ln_b")
    d["iqs"] = g("L0_ffn_q_scale")
    d["iks"] = g("L0_ffn_k_scale")
    d["eye"] = np.eye(128, dtype=np.float32)
    e4 = np.zeros((4, 4, 128), dtype=np.float32)
    for j in range(4):
        e4[j, j, :] = 1.0
    d["e4"] = e4
    u = np.arange(128)[:, None]
    v = np.arange(128)[None, :]
    d["tri_c"] = np.where(v <= u, 0.0, NEG).astype(np.float32)
    d["tri_g"] = np.where(v >= u, 0.0, NEG).astype(np.float32)
    half = np.arange(0, HD, 2, dtype=np.float64)
    inv_freq = 1.0 / 10000.0 ** (half / HD)
    t = np.arange(S, dtype=np.float64)
    fr = t[:, None] * inv_freq[None, :]
    fr = np.concatenate([fr, fr], -1)
    bs = (half + 0.4 * HD) / (1.4 * HD)
    pw = (t - S // 2) / (256 // 2)
    sc = bs[None, :] ** pw[:, None]
    sc = np.concatenate([sc, sc], -1)
    tm = lambda a: np.ascontiguousarray(
        a.reshape(16, 128, HD).transpose(1, 0, 2).astype(np.float32))
    d["cos8"] = tm(np.cos(fr) * sc * 8.0)
    d["sin8"] = tm(np.sin(fr) * sc * 8.0)
    d["cosi"] = tm(np.cos(fr) / sc)
    d["sini"] = tm(np.sin(fr) / sc)
    return d


class Ctr:
    __slots__ = ("sem", "n", "owner")

    def __init__(self, sem, owner=None):
        self.sem = sem
        self.n = 0
        self.owner = owner


class MK:
    ENGS = ("g", "y", "t", "v", "s")

    def __init__(self, nc):
        self.nc = nc
        self.q = {e: [] for e in self.ENGS}
        self.waited = {e: {} for e in self.ENGS}

    def op(self, eng, fn, wait=(), inc=None, inck=1):
        todo = []
        for (c, val) in wait:
            if val <= 0 or c.owner == eng:
                continue
            assert val <= c.n, f"wait on future {val} > {c.n} of {c.sem.name}"
            if self.waited[eng].get(c, -1) < val:
                self.waited[eng][c] = val
                todo.append((c, val))
        if inc is not None:
            inc.n += inck
        sem = inc.sem if inc is not None else None

        def run(e, todo=todo, fn=fn, sem=sem, inck=inck):
            for (c, val) in todo:
                e.wait_ge(c.sem, val)
            r = fn(e)
            if sem is not None:
                if r is None:
                    r = e.drain()
                r.then_inc(sem, inck)
        self.q[eng].append(run)


def view(region, off, dt, shape):
    esz = {F32: 4, BF16: 2, U8: 1}[dt]
    n = 1
    for s in shape[1:]:
        n *= s
    assert off % esz == 0
    a = region.ap().bitcast(dt)[:, off // esz: off // esz + n]
    if len(shape) == 3:
        a = a.rearrange("p (a b) -> p a b", a=shape[1])
    elif len(shape) == 4:
        a = a.rearrange("p (a b c) -> p a b c", a=shape[1], b=shape[2])
    if shape[0] != 128:
        a = a[0: shape[0]]
    return a


def build_program(n_layers=DEPTH, stage="final"):
    nc = bass.Bass("TRN2", target_bir_lowering=False, detect_race_conditions=False)
    dr = {}

    def din(name, shape):
        dr[name] = nc.dram_tensor(name, shape, F32, kind="ExternalInput").ap()

    din("x_fm", [D, S])
    din("owqkv", [D, 3 * D]); din("obqkv", [3 * D]); din("owout", [D, D])
    din("obout", [D]); din("owent", [D, 1]); din("obent", [1, 1])
    din("olng", [D]); din("olnb", [D])
    din("iwqkv", [DEPTH, D, 3 * D]); din("iwo", [DEPTH, D, D])
    din("iw1", [DEPTH, D, 2 * DI]); din("ib1", [DEPTH, 2 * DI])
    din("iw2", [DEPTH, DI, D]); din("ib2", [DEPTH, D])
    din("ilng", [DEPTH, D]); din("ilnb", [DEPTH, D])
    din("iqs", [DEPTH, HD]); din("iks", [DEPTH, HD])
    din("eye", [128, 128]); din("tri_c", [128, 128]); din("tri_g", [128, 128])
    din("e4", [4, 4, 128])
    din("cos8", [128, 16, HD]); din("sin8", [128, 16, HD])
    din("cosi", [128, 16, HD]); din("sini", [128, 16, HD])
    o_fm = nc.dram_tensor("o_fm", [D, S], F32, kind="ExternalOutput").ap()
    spX = nc.dram_tensor("spX", [D, S], F32, kind="Internal").ap()
    spX1 = nc.dram_tensor("spX1", [D, S], F32, kind="Internal").ap()
    spGate = nc.dram_tensor("spGate", [S], F32, kind="Internal").ap()
    d3 = lambda name: dr[name].rearrange("(c p) t -> p c t", p=128)
    dw3 = lambda ap2d: ap2d.rearrange("(c p) n -> p c n", p=128)
    dcol = lambda ap1d: ap1d.rearrange("(c p) -> p c", p=128)
    drow = lambda ap1d: ap1d.rearrange("(o n) -> o n", o=1)

    with ExitStack() as ctx:
        sbuf = lambda n, sh, dt: ctx.enter_context(nc.sbuf_tensor(n, sh, dt))
        psum = lambda n, sh: ctx.enter_context(nc.psum_tensor(n, sh, F32))

        def mksem(n, owner=None):
            return Ctr(ctx.enter_context(nc.semaphore(n)), owner)

        RY = sbuf("RY", [128, 65536], U8)
        RXA = sbuf("RXA", [128, 32768], U8)
        ROF = sbuf("ROF", [128, 32768], U8)
        RSC = sbuf("RSC", [128, 49152], U8)
        RCN = sbuf("RCN", [128, 32032], U8)

        psA = psum("psA", [128, 512])
        psB = psum("psB", [128, 512])
        psC = psum("psC", [128, 512])
        psS = psum("psS", [128, 2048])
        psT = psum("psT", [128, 512])

        T = mksem("T", "t"); V = mksem("V", "v"); SG = mksem("SG", "s")
        cDg = mksem("cDg"); cDs = mksem("cDs"); xD = mksem("xD")
        wS0 = mksem("wS0"); wS1 = mksem("wS1")
        w1S0 = mksem("w1S0"); w1S1 = mksem("w1S1")
        w2S0 = mksem("w2S0"); w2S1 = mksem("w2S1")
        woS = mksem("woS"); wvS = mksem("wvS")
        spXS = mksem("spXS"); spX1S = mksem("spX1S")
        rlS0 = mksem("rlS0"); rlS1 = mksem("rlS1")
        gcS = mksem("gcS"); ldS = mksem("ldS")
        stO = mksem("stO")

        mk = MK(nc)
        E = mk.op

        def BART(extra=()):
            return [(V, V.n), (SG, SG.n)] + list(extra)

        Y = view(RY, 0, F32, [128, NC, S])
        XA = view(RXA, 0, BF16, [128, NC, S])
        OF = view(ROF, 0, BF16, [128, NC, S])
        cEYE = view(RCN, 0, BF16, [128, 128])
        cTRIC = view(RCN, 256, BF16, [128, 128])
        cTRIG = view(RCN, 512, BF16, [128, 128])
        cC8 = view(RCN, 768, BF16, [128, 16, HD])
        cS8 = view(RCN, 2816, BF16, [128, 16, HD])
        cCI = view(RCN, 4864, BF16, [128, 16, HD])
        cSI = view(RCN, 6912, BF16, [128, 16, HD])
        cE8 = view(RCN, 8960, BF16, [128, 8, 4])
        cE4 = view(RCN, 9088, F32, [128, 4, 128])
        cONE1 = view(RCN, 11136, F32, [1, 128])
        cPOOL = view(RCN, 11648, F32, [128, 8])
        cBQK = view(RCN, 11680, F32, [128, 16])
        cGOUT = view(RCN, 11744, F32, [128, 8])
        cBOUT = view(RCN, 11776, F32, [128, 8])
        cBO01 = view(RCN, 11808, F32, [128, 8])
        cWENT = view(RCN, 11840, BF16, [128, 8])
        cBENT = view(RCN, 11856, F32, [1, 1])
        cGCOL = view(RCN, 11872, F32, [128, 16])
        cSTAT = view(RCN, 11936, F32, [4, 512])
        cST2 = view(RCN, 29936, F32, [4, 512])
        cNEG1 = view(RCN, 31984, F32, [128, 1])
        cMU = view(RCN, 13984, F32, [4, 512])
        cCS = view(RCN, 16032, F32, [4, 512])
        cROW = view(RCN, 18080, F32, [4, 512])
        cQSR = view(RCN, 20128, F32, [1, 256])
        cLGB = view(RCN, 21152, F32, [128, 16])
        cB1 = view(RCN, 21216, F32, [128, 44])
        cB2 = view(RCN, 21392, F32, [128, 8])
        cELL = view(RCN, 21424, F32, [128, 1])
        cELL2 = view(RCN, 21472, F32, [128, 1])
        cNM = view(RCN, 21440, F32, [128, 1])
        cRC = view(RCN, 21456, F32, [128, 1])
        cRN = view(RCN, 21488, F32, [128, 64])
        AQ = view(RCN, 21744, BF16, [128, 16, HD])
        BQ = view(RCN, 23792, BF16, [128, 16, HD])
        AK = view(RCN, 25840, BF16, [128, 16, HD])
        BK = view(RCN, 27888, BF16, [128, 16, HD])
        gRow = view(RCN, 11936, F32, [1, 2048])
        qsRep = view(RCN, 11936, BF16, [128, 4, 64])

        xsq = view(RSC, 25600, BF16, [128, S])
        tscr = view(RSC, 29696, F32, [128, 512])
        bvR = view(RSC, 0, F32, [128, D])
        tvs = view(RSC, 4096, F32, [128, 512])
        bvRow = view(RSC, 31744, F32, [1, D])
        vTM = view(RY, 0, BF16, [128, 16, D])
        wvSB = view(RY, 32768, BF16, [128, NC, D])
        qpSB = view(RSC, 0, BF16, [128, S])
        kpSB = view(RSC, 4096, BF16, [128, S])
        pSB = view(RSC, 8192, BF16, [128, S])
        ptSB = view(RSC, 12288, BF16, [128, S])
        wqhSB = [view(RSC, 16384, BF16, [128, NC, 256]),
                 view(RSC, 20480, BF16, [128, NC, 256])]
        diagO = view(RSC, 24576, BF16, [128, 128])
        woSBo = view(RSC, 16384, BF16, [128, NC, D])
        xrl = view(RSC, 8192, F32, [128, 2, 512])
        twx = view(RSC, 4096, F32, [128, 512])
        x1v = Y
        wqp = view(RSC, 0, BF16, [128, NC, 384])
        qkvTM = view(RSC, 6144, BF16, [128, 16, 384])
        qv6 = view(RSC, 6144, BF16, [128, 16, 6, HD])
        q2k2 = view(RSC, 18432, BF16, [128, 16, 256])
        q2k4 = view(RSC, 18432, BF16, [128, 16, 4, HD])
        qkFM = view(RSC, 26624, BF16, [128, 2, S])
        sqB = view(RSC, 34816, BF16, [128, 16, 256])
        sq64 = view(RSC, 34816, BF16, [128, 64, 64])
        qrB = view(RSC, 34816, BF16, [128, 16, 2, HD])
        m1B = view(RSC, 38912, BF16, [128, 16, 128])
        m14 = view(RSC, 38912, BF16, [128, 16, 2, HD])
        pIn = view(RSC, 43008, BF16, [128, 2, 384])
        ptIn = view(RSC, 44544, BF16, [128, 2, 384])
        diagI = view(RSC, 46080, BF16, [128, 128])
        diagT2 = view(RSC, 46080, BF16, [128, 2, 128])
        sxNM = view(RSC, 46592, F32, [128, 2, 1])
        sxL2 = view(RSC, 46608, F32, [128, 2, 1])
        sxLL = view(RSC, 46624, F32, [128, 2, 1])
        sxRC = view(RSC, 46640, F32, [128, 2, 1])
        diagP2 = view(RSC, 46656, BF16, [128, 2, 128])
        wobuf = view(RXA, 0, BF16, [128, NC, D])
        w1b = [view(RSC, 12288, BF16, [128, NC, 256]),
               view(RSC, 16384, BF16, [128, NC, 256])]
        w2b = [view(RSC, 20480, BF16, [128, D]),
               view(RSC, 22528, BF16, [128, D])]
        gelB2 = view(RSC, 24576, BF16, [128, 2, 512])
        Hlo = view(ROF, 0, BF16, [128, 16, 1024])
        Hhi = view(RSC, 0, BF16, [128, 6, 1024])
        x1rl = view(RSC, 0, F32, [128, 2, S])

        def H_ap(j, lo, w):
            return (Hlo[:, j, lo:lo + w] if j < 16
                    else Hhi[:, j - 16, lo:lo + w])

        def ps_oc(oc):
            if oc == 0:
                return psA[:, :]
            if oc == 1:
                return psB[:, :]
            if oc == 2:
                return psC[:, :]
            if oc < 7:
                return psS[:, (oc - 3) * 512:(oc - 2) * 512]
            return psT[:, :]

        def g_dma(dst, src, ctr, wait=(), slow=False):
            def fn(g, d2=dst, s2=src, sl=slow):
                if sl:
                    return g.dma_start(d2, s2, allow_slow_non_contiguous=True)
                return g.dma_start(d2, s2)
            E("g", fn, wait=wait, inc=ctr, inck=16)

        def y_dma(dst, src, ctr, wait=(), slow=False):
            def fn(y, d2=dst, s2=src, sl=slow):
                if sl:
                    return y.dma_start(d2, s2, allow_slow_non_contiguous=True)
                return y.dma_start(d2, s2)
            E("y", fn, wait=wait, inc=ctr, inck=16)

        y_dma(Y[:, :, :], d3("x_fm"), xD)
        g_dma(cEYE, dr["eye"], cDg)
        g_dma(cTRIC, dr["tri_c"], cDg)
        g_dma(cTRIG, dr["tri_g"], cDg)
        g_dma(cWENT, dr["owent"].rearrange("(c p) o -> p (c o)", p=128), cDg,
              slow=True)
        g_dma(cC8, dr["cos8"], cDg)
        g_dma(cS8, dr["sin8"], cDg)
        g_dma(cCI, dr["cosi"], cDg)
        g_dma(cSI, dr["sini"], cDg)
        NCDG = cDg.n
        y_dma(cBQK, dcol(dr["obqkv"][0:2048]), cDs, slow=True)
        y_dma(cGOUT, dcol(dr["olng"]), cDs, slow=True)
        y_dma(cBOUT, dcol(dr["olnb"]), cDs, slow=True)
        y_dma(cBO01, dcol(dr["obout"]), cDs, slow=True)
        y_dma(cBENT, dr["obent"], cDs)
        y_dma(cE4[0:4, :, :], dr["e4"], cDs)
        NCDS = cDs.n

        def init_consts(v):
            v.memset(cE8[:, :, :], 0.0)
            for j in range(4):
                v.memset(cE8[:, j, j:j + 1], 1.0)
            v.memset(cONE1[:, :], 1.0)
            v.memset(cPOOL[:, 0:1], 1.25)
            v.memset(cPOOL[:, 1:2], 0.1)
            v.memset(cPOOL[:, 2:3], -0.5)
            v.memset(cPOOL[:, 3:4], 1e-38)
            v.memset(cPOOL[:, 4:5], 1.0 + 1e-6)
            v.memset(cPOOL[:, 5:6], 1e-11)
            v.memset(cPOOL[:, 6:7], 1e-5)
            v.memset(cPOOL[:, 7:8], 0.0)
            return v.memset(cNEG1[:, :], -1.0)
        E("v", init_consts, inc=V)
        V_CONST = V.n
        E("v", lambda v: v.tensor_scalar_mul(cBQK[:, 0:16:2], cBQK[:, 0:16:2], 1.25),
          wait=[(cDs, NCDS)], inc=V)
        E("v", lambda v: v.tensor_scalar_mul(cBO01[:, :], cBO01[:, :], 0.1), inc=V)

        def emit_stats(src_f32, src_bf, shifted):
            pre = BART([(xD, 16), (cDg, NCDG), (T, T.n)])
            for c in range(NC):
                if shifted and c >= 4:
                    E("s", lambda s, c=c: s.memzero(src_bf[:, c, 0:2]), wait=pre)
                    E("s", lambda s, c=c: s.activation(
                        src_bf[:, c, 1:S], src_f32[:, c, 0:S - 1], AF.Identity),
                      inc=SG)
                else:
                    E("s", lambda s, c=c: s.activation(
                        src_bf[:, c, :], src_f32[:, c, :], AF.Identity),
                      wait=pre, inc=SG)
            sCAST = SG.n
            firstC = True
            firstB = True
            for c in range(NC):
                E("v", lambda v, c=c: v.tensor_mul(
                    xsq[:, :], src_bf[:, c, :], src_bf[:, c, :]),
                  wait=[(SG, sCAST), (V, V_CONST), (T, T.n)], inc=V)
                vSq = V.n
                for j in range(4):
                    last = (c == NC - 1) and (j == 3)
                    E("t", lambda t, c=c, j=j, st=firstC, last=last: t.matmul(
                        psC[0:4, :], cE8[:, j, :],
                        src_bf[:, c, j * 512:(j + 1) * 512],
                        start=st, stop=last),
                      wait=BART([(SG, sCAST)]) if firstC else [])
                    firstC = False
                for j in range(4):
                    last = (c == NC - 1) and (j == 3)
                    E("t", lambda t, j=j, c=c, st=firstB, last=last: t.matmul(
                        psB[0:4, :], cE8[:, j, :],
                        xsq[:, j * 512:(j + 1) * 512],
                        start=st, stop=last),
                      wait=(BART([(V, vSq)]) if firstB else [(V, vSq)]),
                      inc=T)
                    firstB = False
            tSt = T.n
            E("v", lambda v: v.tensor_copy(cSTAT[:, :], psC[0:4, :]),
              wait=[(T, tSt)])
            E("v", lambda v: v.tensor_copy(cST2[:, :], psB[0:4, :]),
              wait=[(T, tSt)], inc=V)
            return V.n

        def emit_row_cs(vStats, mode):
            E("v", lambda v: v.tensor_scalar_mul(cMU[:, :], cSTAT[:, :], 1.0 / D),
              wait=[(V, vStats)])
            E("v", lambda v: v.tensor_scalar_mul(cROW[:, :], cST2[:, :], 1.0 / D))
            E("v", lambda v: v.tensor_mul(cCS[:, :], cMU[:, :], cMU[:, :]))
            E("v", lambda v: v.tensor_sub(cROW[:, :], cROW[:, :], cCS[:, :]), inc=V)
            vVar = V.n
            if mode == "outer":
                E("s", lambda s: s.activation(cROW[:, :], cROW[:, :], AF.Ln,
                                              bias=cPOOL[0:4, 5:6],
                                              scale=cPOOL[0:4, 4:5]),
                  wait=[(V, vVar)])
            else:
                E("s", lambda s: s.activation(cROW[:, :], cROW[:, :], AF.Ln,
                                              bias=cPOOL[0:4, 6:7], scale=1.0),
                  wait=[(V, vVar)])
            E("s", lambda s: s.activation(cCS[:, :], cROW[:, :], AF.Exp,
                                          scale=cPOOL[0:4, 2:3]), inc=SG)
            return SG.n

        def emit_normalize(sCS, src_f32, dst, shifted, affine, out_f32=None):
            zcol = cPOOL[:, 7:8]
            for blk in range(4):
                E("t", lambda t, blk=blk: t.matmul(
                    psA[:, :], cE4[0:4, blk, :], cMU[:, :], start=True, stop=True),
                  wait=BART([(SG, sCS), (V, V_CONST)]))
                E("t", lambda t, blk=blk: t.matmul(
                    psB[:, :], cE4[0:4, blk, :], cCS[:, :], start=True, stop=True),
                  inc=T)
                tBC = T.n
                for c in range(NC):
                    lo = blk * 512
                    sh = shifted and c >= 4

                    def sub(v, c=c, lo=lo, sh=sh):
                        if not sh:
                            return v.tensor_sub(
                                tscr[:, :], src_f32[:, c, lo:lo + 512], psA[:, :])
                        if lo == 0:
                            v.tensor_sub(tscr[:, 0:1], zcol, psA[:, 0:1])
                            return v.tensor_sub(
                                tscr[:, 1:512], src_f32[:, c, 0:511], psA[:, 1:512])
                        return v.tensor_sub(
                            tscr[:, :], src_f32[:, c, lo - 1:lo + 511], psA[:, :])
                    E("v", sub, wait=[(T, tBC), (SG, SG.n)])
                    if out_f32 is not None:
                        E("v", lambda v, c=c, lo=lo: v.tensor_mul(
                            out_f32[:, c, lo:lo + 512], tscr[:, :], psB[:, :]),
                          inc=V)
                    else:
                        E("v", lambda v: v.tensor_mul(
                            tscr[:, :], tscr[:, :], psB[:, :]), inc=V)
                        vN = V.n
                        if affine is not None:
                            gg, bb = affine
                            E("s", lambda s, c=c, lo=lo, gg=gg, bb=bb: s.activation(
                                dst[:, c, lo:lo + 512], tscr[:, :], AF.Identity,
                                bias=bb[:, c:c + 1], scale=gg[:, c:c + 1]),
                              wait=[(V, vN)], inc=SG)
                        else:
                            E("s", lambda s, c=c, lo=lo: s.activation(
                                dst[:, c, lo:lo + 512], tscr[:, :], AF.Identity),
                              wait=[(V, vN)], inc=SG)
            return V.n, SG.n

        vSt = emit_stats(Y, XA, False)
        sCS = emit_row_cs(vSt, "outer")
        vXN, sXN = emit_normalize(sCS, Y, XA, False, (cGOUT, cBOUT))

        if stage == "xn":
            for c in range(NC):
                E("s", lambda s, c=c: s.activation(
                    Y[:, c, :], XA[:, c, :], AF.Identity),
                  wait=[(SG, sXN), (V, vXN)], inc=SG)
            _finish(mk, E, o_fm, Y, stO, [(SG, SG.n)])
            _replay(mk, nc)
            return nc

        y_dma(spX.rearrange("(c p) t -> p c t", p=128), Y[:, :, :], spXS,
              wait=[(V, vXN), (SG, sXN)])

        for blk in range(4):
            for c in range(NC):
                E("t", lambda t, blk=blk, c=c: t.matmul(
                    psS[0:1, blk * 512:(blk + 1) * 512], cWENT[:, c:c + 1],
                    XA[:, c, blk * 512:(blk + 1) * 512],
                    start=(c == 0), stop=(c == NC - 1)),
                  wait=BART([(SG, sXN), (cDg, NCDG)]),
                  inc=(T if c == NC - 1 else None))
        tGate = T.n
        E("s", lambda s: s.activation(gRow[:, :], psS[0:1, 0:2048], AF.Sigmoid,
                                      bias=cBENT[0:1, 0:1]),
          wait=[(T, tGate), (cDs, NCDS)], inc=SG)
        E("v", lambda v: v.tensor_scalar(gRow[:, :], gRow[:, :], 0.1, 2.0,
                                         ALU.max, ALU.min),
          wait=[(SG, SG.n)], inc=V)
        y_dma(spGate, gRow[0:1, :], gcS, wait=[(V, V.n)])
        y_dma(cGCOL[:, :], spGate.rearrange("(a p) -> p a", p=128), gcS,
              wait=[(gcS, gcS.n)], slow=True)

        g_dma(wvSB, dw3(dr["owqkv"])[:, :, 2048:3072], wvS, wait=[(spXS, 16)])
        y_dma(bvRow, drow(dr["obqkv"][2048:3072]), cDs)
        NCDS2 = cDs.n
        for h in range(2):
            E("t", lambda t, h=h: t.matmul(
                psA[:, :], cONE1[:, :], bvRow[0:1, h * 512:(h + 1) * 512],
                start=True, stop=True),
              wait=BART([(cDs, NCDS2), (V, V_CONST), (spXS, 16)]), inc=T)
            E("v", lambda v, h=h: v.tensor_copy(
                bvR[:, h * 512:(h + 1) * 512], psA[:, :]),
              wait=[(T, T.n)], inc=V)
        vBv = V.n
        for tt in range(16):
            for vb in range(2):
                pp = psA if vb == 0 else psB
                for c in range(NC):
                    E("t", lambda t, tt=tt, vb=vb, c=c, pp=pp: t.matmul(
                        pp[:, :], XA[:, c, tt * 128:(tt + 1) * 128],
                        wvSB[:, c, vb * 512:(vb + 1) * 512],
                        start=(c == 0), stop=(c == NC - 1)),
                      wait=BART([(wvS, 16), (SG, sXN), (V, vBv)]),
                      inc=(T if c == NC - 1 else None))
                tV = T.n
                E("v", lambda v, pp=pp, vb=vb: v.tensor_add(
                    tvs[:, :], pp[:, :], bvR[:, vb * 512:(vb + 1) * 512]),
                  wait=[(T, tV), (SG, SG.n)], inc=V)
                E("s", lambda s, tt=tt, vb=vb: s.activation(
                    vTM[:, tt, vb * 512:(vb + 1) * 512], tvs[:, :], AF.Identity,
                    scale=cGCOL[:, tt:tt + 1]),
                  wait=[(V, V.n), (gcS, 32)], inc=SG)
        sVTM = SG.n

        wq_last = [0, 0]
        NPrun = 1 if stage == "dbg0" else NP
        for p in range(NPrun):
            par = p % 2
            wctr = wS0 if par == 0 else wS1
            wbuf = wqhSB[par]
            g_dma(wbuf, dw3(dr["owqkv"])[:, :, p * 256:(p + 1) * 256], wctr,
                  wait=[(T, wq_last[par]), (spXS, 16)])
            nw = wctr.n
            for qk in range(2):
                dst = qpSB if qk == 0 else kpSB
                for tb in range(4):
                    pp = psA if tb % 2 == 0 else psB
                    for c in range(NC):
                        E("t", lambda t, qk=qk, tb=tb, c=c, pp=pp, wbuf=wbuf:
                          t.matmul(
                            pp[:, :], wbuf[:, c, qk * 128:qk * 128 + 128],
                            XA[:, c, tb * 512:(tb + 1) * 512],
                            start=(c == 0), stop=(c == NC - 1)),
                          wait=BART([(wctr, nw), (SG, sVTM)]),
                          inc=(T if c == NC - 1 else None))
                    tP = T.n

                    def ev(s, qk=qk, tb=tb, pp=pp, dst=dst, p=p):
                        sc = cPOOL[:, 0:1] if qk == 0 else 1.0
                        return s.activation(
                            dst[:, tb * 512:(tb + 1) * 512], pp[:, :], AF.Identity,
                            scale=sc, bias=cBQK[:, 2 * p + qk:2 * p + qk + 1])
                    E("s", ev, wait=[(T, tP)], inc=SG)
            wq_last[par] = T.n
            sProj = SG.n
            for h01 in range(2):
                b0, b1 = 64 * h01, 64 * h01 + 64
                head = 2 * p + h01
                for qt in range(16):
                    nk = (qt + 1) * 128
                    nbank = (nk + 511) // 512
                    for kb in range(nbank):
                        klo = kb * 512
                        kw = min(512, nk - klo)
                        lastb = kb == nbank - 1
                        E("t", lambda t, qt=qt, klo=klo, kw=kw, b0=b0, b1=b1,
                          lastb=lastb: t.matmul(
                            psS[:, klo:klo + kw],
                            qpSB[b0:b1, qt * 128:(qt + 1) * 128],
                            kpSB[b0:b1, klo:klo + kw],
                            start=True, stop=not lastb),
                          wait=BART([(SG, sProj)]))
                        if lastb:
                            E("t", lambda t, qt=qt: t.matmul(
                                psS[:, qt * 128:(qt + 1) * 128], cEYE[:, :],
                                cTRIC[:, :], start=False, stop=True), inc=T)
                    tS = T.n
                    E("v", lambda v, nk=nk: v.reduce_max(
                        cNM[:, :], psS[:, 0:nk], axis=AX.X, negate=True),
                      wait=[(T, tS), (SG, SG.n)], inc=V)
                    E("s", lambda s, nk=nk: s.activation(
                        pSB[:, 0:nk], psS[:, 0:nk], AF.Exp, bias=cNM[:, :]),
                      wait=[(V, V.n)], inc=SG)
                    E("v", lambda v, nk=nk: v.reduce_sum(
                        cELL2[:, :], pSB[:, 0:nk], axis=AX.X),
                      wait=[(SG, SG.n)], inc=V)
                    E("s", lambda s: s.activation(cELL[:, :], cELL2[:, :], AF.Ln),
                      wait=[(V, V.n)])
                    E("s", lambda s: s.activation(cRC[:, :], cELL[:, :], AF.Exp,
                                                  scale=cNEG1[:, :]), inc=SG)
                    E("v", lambda v: v.tensor_scalar_mul(
                        diagO[:, :], cEYE[:, :], cRC[:, :]),
                      wait=[(T, T.n), (SG, SG.n)], inc=V)
                    vDiag = V.n
                    nkc = qt + 1
                    for g0 in range(0, nkc, 4):
                        gn = min(4, nkc - g0)
                        for ki in range(gn):
                            kc = g0 + ki
                            E("t", lambda t, kc=kc, ki=ki: t.matmul(
                                psT[:, ki * 128:(ki + 1) * 128],
                                pSB[:, kc * 128:(kc + 1) * 128], diagO[:, :],
                                start=True, stop=True),
                              wait=BART([(V, vDiag)]),
                              inc=(T if ki == gn - 1 else None))
                        tPT = T.n
                        E("v", lambda v, g0=g0, gn=gn: v.tensor_copy(
                            ptSB[:, g0 * 128:(g0 + gn) * 128], psT[:, 0:gn * 128]),
                          wait=[(T, tPT)], inc=V)
                        vPT = V.n
                        for ki in range(gn):
                            kc = g0 + ki
                            E("t", lambda t, kc=kc, head=head, h01=h01,
                              st=(kc == 0), sp=(kc == nkc - 1): t.matmul(
                                psS[0:64, 0:128] if h01 == 0
                                else psS[64:128, 0:128],
                                vTM[:, kc, head * 64:head * 64 + 64],
                                ptSB[:, kc * 128:(kc + 1) * 128],
                                start=st, stop=sp,
                                tile_position=((0, 64) if h01 == 1 else None)),
                              wait=[(V, vPT)],
                              inc=(T if kc == nkc - 1 else None))
                    tPV = T.n
                    ch, po = (head * 64) // 128, (head * 64) % 128
                    E("v", lambda v, ch=ch, po=po, qt=qt, h01=h01: v.tensor_copy(
                        OF[po:po + 64, ch, qt * 128:(qt + 1) * 128],
                        psS[0:64, 0:128] if h01 == 0 else psS[64:128, 0:128]),
                      wait=[(T, tPV)], inc=V)
                    if stage == "dbg0" and h01 == 0 and qt == 3:
                        E("s", lambda s: s.activation(
                            Y[:, 0, 0:512], pSB[:, 0:512], AF.Identity),
                          wait=[(T, tPV)], inc=SG)
                        E("s", lambda s: s.activation(
                            Y[:, 2, 0:1], cELL[:, :], AF.Identity), inc=SG)
                        E("s", lambda s: s.activation(
                            Y[:, 2, 1:2], cNM[:, :], AF.Identity), inc=SG)
                        E("s", lambda s: s.activation(
                            Y[:, 3, 0:512], ptSB[:, 0:512], AF.Identity), inc=SG)
                        E("v", lambda v: v.tensor_copy(
                            Y[0:64, 5, 0:128], psS[0:64, 0:128]),
                          wait=[(T, tPV)], inc=V)
                        _finish(mk, E, o_fm, Y, stO, [(SG, SG.n), (V, V.n)])
                        _replay(mk, nc)
                        return nc
        vAttO = V.n

        if stage == "of":
            for c in range(NC):
                E("s", lambda s, c=c: s.activation(
                    Y[:, c, :], OF[:, c, :], AF.Identity),
                  wait=[(V, vAttO), (T, T.n), (SG, SG.n)], inc=SG)
            _finish(mk, E, o_fm, Y, stO, [(SG, SG.n)])
            _replay(mk, nc)
            return nc

        g_dma(woSBo, dw3(dr["owout"]), woS, wait=[(T, T.n)])
        nwo = woS.n
        rl_last = [vAttO, vAttO]
        for oc in range(NC):
            for tb in range(4):
                rb = (oc * 4 + tb) % 2
                rctr = rlS0 if rb == 0 else rlS1
                y_dma(xrl[:, rb, :],
                      spX.rearrange("(c p) t -> p c t", p=128)
                      [:, oc, tb * 512:(tb + 1) * 512],
                      rctr, wait=[(spXS, 16), (V, rl_last[rb])])
                nrl = rctr.n
                pp = psA if tb % 2 == 0 else psB
                for c in range(NC):
                    E("t", lambda t, oc=oc, tb=tb, c=c, pp=pp: t.matmul(
                        pp[:, :], woSBo[:, c, oc * 128:(oc + 1) * 128],
                        OF[:, c, tb * 512:(tb + 1) * 512],
                        start=(c == 0), stop=(c == NC - 1)),
                      wait=BART([(woS, nwo), (V, vAttO)]),
                      inc=(T if c == NC - 1 else None))
                tWo = T.n
                E("s", lambda s, pp=pp, oc=oc: s.activation(
                    twx[:, :], pp[:, :], AF.Identity, scale=cPOOL[:, 1:2],
                    bias=cBO01[:, oc:oc + 1]),
                  wait=[(T, tWo), (V, V.n)], inc=SG)
                sWo = SG.n
                E("v", lambda v, oc=oc, tb=tb, rb=rb: v.tensor_add(
                    x1v[:, oc, tb * 512:(tb + 1) * 512], twx[:, :], xrl[:, rb, :]),
                  wait=[(SG, sWo), (rctr, nrl)], inc=V)
                rl_last[rb] = V.n
            y_dma(spX1.rearrange("(c p) t -> p c t", p=128)[:, oc, :],
                  x1v[:, oc, :], spX1S, wait=[(V, V.n)])
        vX1 = V.n

        if stage == "x1":
            _finish(mk, E, o_fm, Y, stO, [(V, vX1)])
            _replay(mk, nc)
            return nc

        vSt2 = emit_stats(x1v, XA, False)
        sCS2 = emit_row_cs(vSt2, "plain")
        vH2, _ = emit_normalize(sCS2, x1v, None, False, None, out_f32=Y)
        iwq_last0_init = T.n
        iwq_vgate0_init = V.n

        if stage == "h2":
            _finish(mk, E, o_fm, Y, stO, [(V, vH2)])
            _replay(mk, nc)
            return nc

        iwq_last = [iwq_last0_init]
        iwq_vgate = [iwq_vgate0_init]
        w1_last = [iwq_last0_init, iwq_last0_init]
        w2_last = [iwq_last0_init, iwq_last0_init]
        wo_last = [0]

        def inner_layer(L):
            nld0 = ldS.n
            y_dma(cLGB[:, 0:8], dcol(dr["ilng"][L]), ldS,
                  wait=[(SG, SG.n), (V, V.n)], slow=True)
            y_dma(cLGB[:, 8:16], dcol(dr["ilnb"][L]), ldS, slow=True)
            y_dma(cB1[:, 0:42],
                  dr["ib1"][L][0:5376].rearrange("(j p) -> p j", p=128),
                  ldS, slow=True)
            y_dma(cB1[0:42, 42:43], drow(dr["ib1"][L][5376:5418]).rearrange(
                "o n -> n o"), ldS, slow=True)
            y_dma(cB1[0:42, 43:44], drow(dr["ib1"][L][5418:5460]).rearrange(
                "o n -> n o"), ldS, slow=True)
            y_dma(cB2[:, :], dcol(dr["ib2"][L]), ldS, slow=True)
            y_dma(cQSR[0:1, 0:64], drow(dr["iqs"][L]), ldS)
            y_dma(cQSR[0:1, 128:192], drow(dr["iks"][L]), ldS)
            nld = ldS.n
            E("v", lambda v: v.tensor_scalar_mul(
                cQSR[0:1, 64:96], cQSR[0:1, 32:64], -1.0),
              wait=[(ldS, nld)])
            E("v", lambda v: v.tensor_copy(cQSR[0:1, 96:128], cQSR[0:1, 0:32]))
            E("v", lambda v: v.tensor_scalar_mul(
                cQSR[0:1, 192:224], cQSR[0:1, 160:192], -1.0))
            E("v", lambda v: v.tensor_copy(cQSR[0:1, 224:256], cQSR[0:1, 128:160]),
              inc=V)
            vQSR = V.n
            for r in range(4):
                E("t", lambda t, r=r: t.matmul(
                    psA[:, 0:64], cONE1[:, :], cQSR[0:1, r * 64:(r + 1) * 64],
                    start=True, stop=True),
                  wait=BART([(V, vQSR)]), inc=T)
                E("v", lambda v, r=r: v.tensor_copy(qsRep[:, r, :], psA[:, 0:64]),
                  wait=[(T, T.n)], inc=V)
            vRep = V.n
            for (tab, trig, r) in ((AQ, cC8, 0), (BQ, cS8, 1), (AK, cCI, 2),
                                   (BK, cSI, 3)):
                E("v", lambda v, tab=tab, trig=trig, r=r: v.tensor_tensor(
                    tab[:, :, :], trig[:, :, :],
                    qsRep[:, r:r + 1, :].broadcast_to([128, 16, 64]), ALU.mult),
                  wait=[(V, vRep), (cDg, NCDG)], inc=V)
            vAB = V.n

            for c in range(NC):
                if c >= 4:
                    E("s", lambda s, c=c: s.memzero(XA[:, c, 0:2]),
                      wait=[(T, T.n)])
                    E("s", lambda s, c=c: s.activation(
                        XA[:, c, 1:S], Y[:, c, 0:S - 1], AF.Identity), inc=SG)
                else:
                    E("s", lambda s, c=c: s.activation(
                        XA[:, c, :], Y[:, c, :], AF.Identity),
                      wait=[(T, T.n)], inc=SG)
            sXAi = SG.n

            for p in range(NP):
                inner_pair(L, p, sXAi, vAB)
            vAtt = V.n

            g_dma(wobuf, dw3(dr["iwo"][L]), woS,
                  wait=[(T, T.n), (SG, SG.n)])
            nwo2 = woS.n
            addg = [None, None]
            for oc in range(NC):
                for tb in range(4):
                    pp = psA if tb % 2 == 0 else psB
                    par = tb % 2
                    for c in range(NC):
                        if c == 0:
                            w = ([(woS, nwo2), (V, vAtt)]
                                 + ([(V, addg[par])] if addg[par] else []))
                            if oc == 0 and tb == 0:
                                w = BART(w)
                        else:
                            w = []
                        E("t", lambda t, oc=oc, tb=tb, c=c, pp=pp: t.matmul(
                            pp[:, :], wobuf[:, c, oc * 128:(oc + 1) * 128],
                            OF[:, c, tb * 512:(tb + 1) * 512],
                            start=(c == 0), stop=(c == NC - 1)),
                          wait=w, inc=(T if c == NC - 1 else None))
                    tWo = T.n
                    E("v", lambda v, oc=oc, tb=tb, pp=pp: v.tensor_add(
                        Y[:, oc, tb * 512:(tb + 1) * 512], pp[:, :],
                        Y[:, oc, tb * 512:(tb + 1) * 512]),
                      wait=[(T, tWo)], inc=V)
                    addg[par] = V.n
            ffn_tgate = T.n
            if stage == ("att", L):
                return

            vStf = emit_stats(Y, XA, True)
            if stage == ("fst", L):
                E("s", lambda s: s.activation(
                    Y[:, 0, 0:2], Y[:, 0, 0:2], AF.Identity),
                  wait=[(V, vStf)], inc=SG)
                return "fxn"
            sCSf = emit_row_cs(vStf, "plain")
            if stage == ("fs2", L):
                E("s", lambda s: s.activation(
                    Y[:, 0, 0:2], Y[:, 0, 0:2], AF.Identity),
                  wait=[(SG, sCSf)], inc=SG)
                return "fxn"
            vXN2, sXN2 = emit_normalize(
                sCSf, Y, XA, True, (cLGB[:, 0:8], cLGB[:, 8:16]))
            if stage == ("fxn", L):
                for c in range(NC):
                    E("s", lambda s, c=c: s.activation(
                        Y[:, c, :], XA[:, c, :], AF.Identity),
                      wait=[(SG, sXN2), (V, vXN2)], inc=SG)
                return "fxn"
            w1_last[0] = max(w1_last[0], ffn_tgate)
            w1_last[1] = max(w1_last[1], ffn_tgate)
            w2_last[0] = max(w2_last[0], ffn_tgate)
            w2_last[1] = max(w2_last[1], ffn_tgate)
            for th in range(2):
                ffn_half(L, th, vXN2, sXN2, nld)

        def inner_pair(L, p, sXAi, vAB):
            g_dma(wqp, dw3(dr["iwqkv"][L])[:, :, p * 384:(p + 1) * 384], wS0,
                  wait=[(T, iwq_last[0]), (V, iwq_vgate[0])])
            nw = wS0.n
            qev = [V.n, V.n]
            for tt in range(16):
                pp = psA if tt % 2 == 0 else psB
                for c in range(NC):
                    w = ((BART([(wS0, nw), (SG, sXAi)]) if tt == 0
                          else [(wS0, nw), (SG, sXAi), (V, qev[tt % 2])])
                         if c == 0 else [])
                    E("t", lambda t, tt=tt, c=c, pp=pp: t.matmul(
                        pp[:, 0:384], XA[:, c, tt * 128:(tt + 1) * 128],
                        wqp[:, c, :], start=(c == 0), stop=(c == NC - 1)),
                      wait=w, inc=(T if c == NC - 1 else None))
                tQ = T.n
                E("v", lambda v, tt=tt, pp=pp: v.tensor_copy(
                    qkvTM[:, tt, :], pp[:, 0:384]),
                  wait=[(T, tQ)], inc=V)
                qev[tt % 2] = V.n
            iwq_last[0] = T.n
            vQKV = V.n
            E("v", lambda v: v.tensor_mul(
                sqB[:, :, :], qkvTM[:, :, 0:256], qkvTM[:, :, 0:256]),
              wait=[(V, vQKV)])
            E("v", lambda v: v.tensor_reduce(
                cRN[:, :], sq64[:, :, :], axis=AX.X, op=ALU.add), inc=V)
            vNsq = V.n
            E("s", lambda s: s.activation(cRN[:, :], cRN[:, :], AF.Ln,
                                          bias=cPOOL[:, 3:4], scale=1.0),
              wait=[(V, vNsq)])
            E("s", lambda s: s.activation(cRN[:, :], cRN[:, :], AF.Exp,
                                          scale=cPOOL[:, 2:3]), inc=SG)
            E("v", lambda v: v.tensor_scalar_min(cRN[:, :], cRN[:, :], 1e12),
              wait=[(SG, SG.n)], inc=V)
            for qk in range(2):
                Atab = AQ if qk == 0 else AK
                Btab = BQ if qk == 0 else BK
                sub0 = 2 * qk
                E("v", lambda v, sub0=sub0: v.tensor_copy(
                    qrB[:, :, :, 0:32], qv6[:, :, sub0:sub0 + 2, 32:64]),
                  wait=[(V, V.n), (V, vAB)])
                E("v", lambda v, sub0=sub0: v.tensor_copy(
                    qrB[:, :, :, 32:64], qv6[:, :, sub0:sub0 + 2, 0:32]), inc=V)
                E("v", lambda v, qk=qk, Atab=Atab: v.tensor_tensor(
                    m14[:, :, :, :], qv6[:, :, 2 * qk:2 * qk + 2, :],
                    Atab[:, :, :].rearrange("p a (o c) -> p a o c", o=1)
                    .broadcast_to([128, 16, 2, 64]), ALU.mult),
                  wait=[(V, V.n)])
                E("v", lambda v, qk=qk, Btab=Btab: v.tensor_tensor(
                    q2k4[:, :, 2 * qk:2 * qk + 2, :], qrB[:, :, :, :],
                    Btab[:, :, :].rearrange("p a (o c) -> p a o c", o=1)
                    .broadcast_to([128, 16, 2, 64]), ALU.mult))
                E("v", lambda v, qk=qk: v.tensor_add(
                    q2k4[:, :, 2 * qk:2 * qk + 2, :],
                    q2k4[:, :, 2 * qk:2 * qk + 2, :], m14[:, :, :, :]), inc=V)
            vRope = V.n
            tTr_, vQF_ = {}, {}
            for k in range(64):
                tt, j = k // 4, k % 4
                qk, sub = (0 if j < 2 else 1), j % 2
                pr = k % 2
                w = [(V, vRope)]
                if k >= 2:
                    w.append((T, tTr_[k - 2]))
                E("v", lambda v, tt=tt, j=j, pr=pr: v.tensor_scalar_mul(
                    diagT2[:, pr, :], cEYE[:, :],
                    cRN[:, tt * 4 + j:tt * 4 + j + 1]),
                  wait=w, inc=V)
                vD = V.n
                wt = [(V, vD)]
                if j == 0 and tt >= 1:
                    wt.append((V, vQF_[tt - 1]))
                E("t", lambda t, tt=tt, qk=qk, sub=sub, pr=pr: t.matmul(
                    psT[0:64, qk * 128:qk * 128 + 128] if sub == 0
                    else psT[64:128, qk * 128:qk * 128 + 128],
                    q2k4[:, tt, 2 * qk + sub, :], diagT2[:, pr, :],
                    start=True, stop=True,
                    tile_position=((0, 64) if sub == 1 else None)),
                  wait=(BART(wt) if k == 0 else wt), inc=T)
                tTr_[k] = T.n
                if j == 3:
                    E("v", lambda v, tt=tt: v.tensor_copy(
                        qkFM[:, 0, tt * 128:(tt + 1) * 128], psT[:, 0:128]),
                      wait=[(T, tTr_[k])])
                    E("v", lambda v, tt=tt: v.tensor_copy(
                        qkFM[:, 1, tt * 128:(tt + 1) * 128], psT[:, 128:256]),
                      inc=V)
                    vQF_[tt] = V.n
            vFM = V.n
            units = [(h01, qt) for h01 in range(2) for qt in range(16)]
            NU = len(units)
            tS_, tPT_, tPV_ = {}, {}, {}
            sE1_, sRC_ = {}, {}
            vDg_, vPT_, vOF_ = {}, {}, {}

            def att_S(i):
                h01, qt = units[i]
                b0, b1 = 64 * h01, 64 * h01 + 64
                sl = (i % 4) * 512
                klo = max(0, qt - 2) * 128
                n0 = qt * 128 - klo
                nk = n0 + 128
                w = [(V, vFM)]
                if i >= 4:
                    w.append((V, vOF_[i - 4]))
                E("t", lambda t, qt=qt, sl=sl, b0=b0, b1=b1, klo=klo, nk=nk:
                  t.matmul(psS[:, sl:sl + nk],
                           qkFM[b0:b1, 0, qt * 128:(qt + 1) * 128],
                           qkFM[b0:b1, 1, klo:klo + nk],
                           start=True, stop=False),
                  wait=(BART(w) if i == 0 else w))
                if qt >= 2:
                    E("t", lambda t, sl=sl: t.matmul(
                        psS[:, sl:sl + 128], cEYE[:, :], cTRIG[:, :],
                        start=False, stop=False))
                E("t", lambda t, sl=sl, n0=n0: t.matmul(
                    psS[:, sl + n0:sl + n0 + 128], cEYE[:, :], cTRIC[:, :],
                    start=False, stop=True), inc=T)
                tS_[i] = T.n

            def att_soft(i):
                h01, qt = units[i]
                head = 2 * p + h01
                sl = (i % 4) * 512
                par = i % 2
                klo = max(0, qt - 2) * 128
                nk = qt * 128 - klo + 128
                w = [(T, tS_[i])]
                if i >= 2:
                    w.append((SG, sE1_[i - 2]))
                E("v", lambda v, sl=sl, nk=nk, par=par: v.reduce_max(
                    sxNM[:, par, :], psS[:, sl:sl + nk], axis=AX.X, negate=True),
                  wait=w, inc=V)
                E("s", lambda s, sl=sl, nk=nk, par=par: s.activation(
                    pIn[:, par, 0:nk], psS[:, sl:sl + nk], AF.Exp,
                    bias=sxNM[:, par, :]),
                  wait=([(V, V.n)]
                        + ([(T, tPT_[i - 2])] if i >= 2 else [])), inc=SG)
                sE1_[i] = SG.n
                w2 = [(SG, sE1_[i])]
                if i >= 2:
                    w2.append((SG, sRC_[i - 2]))
                E("v", lambda v, nk=nk, par=par: v.reduce_sum(
                    sxL2[:, par, :], pIn[:, par, 0:nk], axis=AX.X),
                  wait=w2, inc=V)
                E("s", lambda s, par=par: s.activation(
                    sxLL[:, par, :], sxL2[:, par, :], AF.Ln),
                  wait=[(V, V.n)])
                E("s", lambda s, par=par: s.activation(
                    sxRC[:, par, :], sxLL[:, par, :], AF.Exp,
                    scale=cNEG1[:, :]),
                  wait=([(V, vDg_[i - 2])] if i >= 2 else []), inc=SG)
                sRC_[i] = SG.n
                w3 = [(SG, sRC_[i])]
                if i >= 2:
                    w3.append((T, tPT_[i - 2]))
                E("v", lambda v, par=par: v.tensor_scalar_mul(
                    diagP2[:, par, :], cEYE[:, :], sxRC[:, par, :]),
                  wait=w3, inc=V)
                vDg_[i] = V.n
                pt = psT if par == 0 else psC
                nkc = nk // 128
                for ki in range(nkc):
                    E("t", lambda t, ki=ki, par=par, pt=pt: t.matmul(
                        pt[:, ki * 128:(ki + 1) * 128],
                        pIn[:, par, ki * 128:(ki + 1) * 128],
                        diagP2[:, par, :], start=True, stop=True),
                      wait=(([(V, vDg_[i])]
                             + ([(V, vPT_[i - 2])] if i >= 2 else []))
                            if ki == 0 else []),
                      inc=(T if ki == nkc - 1 else None))
                tPT_[i] = T.n
                E("v", lambda v, nk=nk, par=par, pt=pt: v.tensor_copy(
                    ptIn[:, par, 0:nk], pt[:, 0:nk]),
                  wait=([(T, tPT_[i])]
                        + ([(T, tPV_[i - 2])] if i >= 2 else [])), inc=V)
                vPT_[i] = V.n
                for ki in range(nkc):
                    kc = klo // 128 + ki
                    E("t", lambda t, ki=ki, kc=kc, h01=h01, sl=sl, par=par,
                      st=(ki == 0), sp=(ki == nkc - 1): t.matmul(
                        psS[0:64, sl:sl + 128] if h01 == 0
                        else psS[64:128, sl:sl + 128],
                        qv6[:, kc, 4 + h01, :],
                        ptIn[:, par, ki * 128:(ki + 1) * 128],
                        start=st, stop=sp,
                        tile_position=((0, 64) if h01 == 1 else None)),
                      wait=([(V, vPT_[i])] if ki == 0 else []),
                      inc=(T if ki == nkc - 1 else None))
                tPV_[i] = T.n
                ch, po = (head * 64) // 128, (head * 64) % 128
                E("v", lambda v, ch=ch, po=po, qt=qt, h01=h01, sl=sl:
                  v.tensor_copy(
                    OF[po:po + 64, ch, qt * 128:(qt + 1) * 128],
                    psS[0:64, sl:sl + 128] if h01 == 0
                    else psS[64:128, sl:sl + 128]),
                  wait=[(T, tPV_[i])], inc=V)
                vOF_[i] = V.n

            att_S(0)
            att_S(1)
            for i in range(NU):
                att_soft(i)
                if i + 2 < NU:
                    att_S(i + 2)

        def ffn_half(L, th, vXN2, sXN2, nld):
            t0 = th * 1024
            gelg = [None, None]
            sttg = [None, None]
            for jp in range(NJ):
                wa = JW[jp]
                par = jp % 2
                wbuf = w1b[par]
                wctr = w1S0 if par == 0 else w1S1
                g_dma(wbuf[:, :, 0:2 * wa],
                      dw3(dr["iw1"][L])[:, :, jp * 256:jp * 256 + 2 * wa],
                      wctr, wait=[(T, w1_last[par])])
                nw = wctr.n
                for tb in range(2):
                    pu = tb
                    gb = psA if pu == 0 else psC
                    ab = psB if pu == 0 else psT
                    lo = t0 + tb * 512
                    hb = tb * 512
                    for c in range(NC):
                        if c == 0:
                            w = [(wctr, nw), (SG, sXN2), (V, vXN2)]
                            if gelg[pu]:
                                w.append((SG, gelg[pu]))
                            if jp == 0 and tb == 0:
                                w = BART(w)
                        else:
                            w = []
                        E("t", lambda t, c=c, wa=wa, wbuf=wbuf, lo=lo, gb=gb:
                          t.matmul(
                            gb[0:wa, :], wbuf[:, c, wa:2 * wa],
                            XA[:, c, lo:lo + 512],
                            start=(c == 0), stop=(c == NC - 1)),
                          wait=w, inc=(T if c == NC - 1 else None))
                    tG = T.n
                    E("s", lambda s, wa=wa, jp=jp, pu=pu, gb=gb: s.activation(
                        gelB2[0:wa, pu, :], gb[0:wa, :], AF.Gelu,
                        bias=cB1[0:wa, 2 * jp + 1:2 * jp + 2]),
                      wait=([(T, tG), (ldS, nld)]
                            + [(V, sttg[pu]) if sttg[pu] else (V, V.n)]),
                      inc=SG)
                    gelg[pu] = SG.n
                    sGl = SG.n
                    for c in range(NC):
                        w = (([(wctr, nw)]
                              + [(V, sttg[pu]) if sttg[pu] else (V, V.n)])
                             if c == 0 else [])
                        E("t", lambda t, c=c, wa=wa, wbuf=wbuf, lo=lo, ab=ab:
                          t.matmul(
                            ab[0:wa, :], wbuf[:, c, 0:wa],
                            XA[:, c, lo:lo + 512],
                            start=(c == 0), stop=(c == NC - 1)),
                          wait=w, inc=(T if c == NC - 1 else None))
                    tA = T.n
                    E("v", lambda v, wa=wa, jp=jp, hb=hb, pu=pu, ab=ab:
                      v.scalar_tensor_tensor(
                        H_ap(jp, hb, 512)[0:wa, :], ab[0:wa, :],
                        cB1[0:wa, 2 * jp:2 * jp + 1], gelB2[0:wa, pu, :],
                        op0=ALU.add, op1=ALU.mult),
                      wait=[(T, tA), (SG, sGl)], inc=V)
                    sttg[pu] = V.n
                w1_last[par] = T.n
            vH = V.n
            for tb in range(2):
                lo = t0 + tb * 512
                hb = tb * 512
                for jp in range(NJ):
                    wa = JW[jp]
                    par = jp % 2
                    wbuf = w2b[par]
                    wctr = w2S0 if par == 0 else w2S1
                    g_dma(wbuf[0:wa, :], dr["iw2"][L][jp * 128:jp * 128 + wa, :],
                          wctr, wait=[(T, w2_last[par])])
                    nw2 = wctr.n
                    for oc in range(8):
                        E("t", lambda t, jp=jp, wa=wa, oc=oc, wbuf=wbuf, hb=hb:
                          t.matmul(
                            ps_oc(oc), wbuf[0:wa, oc * 128:(oc + 1) * 128],
                            H_ap(jp, hb, 512)[0:wa, :],
                            start=(jp == 0), stop=(jp == NJ - 1)),
                          wait=(BART([(wctr, nw2), (V, vH)])
                                if jp == 0 and oc == 0
                                else [(wctr, nw2)]),
                          inc=(T if oc == 7 else None))
                    w2_last[par] = T.n
                tW2 = T.n
                for oc in range(8):
                    E("v", lambda v, oc=oc, lo=lo: v.scalar_tensor_tensor(
                        Y[:, oc, lo:lo + 512], ps_oc(oc), cB2[:, oc:oc + 1],
                        Y[:, oc, lo:lo + 512], op0=ALU.add, op1=ALU.add),
                      wait=[(T, tW2), (ldS, nld)], inc=V)

        for L in range(n_layers):
            r = inner_layer(L)
            iwq_last[0] = T.n
            iwq_vgate[0] = V.n
            w1_last[0] = w1_last[1] = T.n
            w2_last[0] = w2_last[1] = T.n
            if r == "fxn":
                _finish(mk, E, o_fm, Y, stO, [(V, V.n), (T, T.n), (SG, SG.n)])
                _replay(mk, nc)
                return nc
            if stage == ("att", L) or stage == ("ffn", L):
                _finish(mk, E, o_fm, Y, stO, [(V, V.n), (T, T.n), (SG, SG.n)])
                _replay(mk, nc)
                return nc

        rl2_last = [0, 0]
        for c in range(NC):
            rb = c % 2
            rctr = rlS0 if rb == 0 else rlS1
            y_dma(x1rl[:, rb, :],
                  spX1.rearrange("(c p) t -> p c t", p=128)[:, c, :],
                  rctr, wait=[(spX1S, spX1S.n), (V, rl2_last[rb]),
                              (T, T.n), (SG, SG.n)])
            nrl = rctr.n
            E("v", lambda v, c=c, rb=rb: v.tensor_add(
                Y[:, c, :], Y[:, c, :], x1rl[:, rb, :]),
              wait=[(rctr, nrl)], inc=V)
            rl2_last[rb] = V.n
        _finish(mk, E, o_fm, Y, stO, [(V, V.n)])
        _replay(mk, nc)
    return nc


def _finish(mk, E, o_fm, src, stO, wait):
    E("y", lambda y: y.dma_start(
        o_fm.rearrange("(c p) t -> p c t", p=128), src[:, :, :]),
      wait=wait, inc=stO, inck=16)
    E("y", lambda y: None, wait=[(stO, 16)])


def _replay(mk, nc):
    with nc.Block() as block:
        for eng, reg in (("g", block.gpsimd), ("y", block.sync),
                         ("t", block.tensor), ("v", block.vector),
                         ("s", block.scalar)):
            ops = list(mk.q[eng])

            def body(e, ops=ops):
                for fn in ops:
                    fn(e)
            reg(body)


def kernel(**inputs):
    shared = _prep_shared(inputs)
    x = np.asarray(inputs["L0_attn_x"], dtype=np.float32)
    nc = build_program()
    in_maps = []
    for b in range(8):
        m = dict(shared)
        m["x_fm"] = np.ascontiguousarray(x[b].T)
        in_maps.append(m)
    res = run_bass_kernel_spmd(nc, in_maps, core_ids=list(range(8)))
    out = np.stack([np.ascontiguousarray(np.asarray(r["o_fm"]).T)
                    for r in res.results])
    return out.astype(np.float32)


# version 03
# speedup vs baseline: 1.3643x; 1.0939x over previous
import numpy as np
import concourse.bass as bass
import concourse.mybir as mybir
from concourse.bass_utils import run_bass_kernel_spmd
from contextlib import ExitStack

F32 = mybir.dt.float32
BF16 = mybir.dt.bfloat16
U8 = mybir.dt.uint8
AF = mybir.ActivationFunctionType
ALU = mybir.AluOpType
AX = mybir.AxisListType

D = 1024
S = 2048
NC = 8
NH = 16
HD = 64
NP = 8
DEPTH = 6
DI = 2730
NJ = 22
JW = [128] * 21 + [42]
NEG = -1e30


def _prep_shared(inputs):
    g = lambda k: np.ascontiguousarray(np.asarray(inputs[k], dtype=np.float32))
    d = {}
    wqkvT = g("L0_attn_w_qkv").T
    bqkv = g("L0_attn_b_qkv")
    idx = []
    for p in range(NP):
        idx += list(range(2 * p * HD, 2 * p * HD + 2 * HD))
        idx += list(range(D + 2 * p * HD, D + 2 * p * HD + 2 * HD))
    idx += list(range(2 * D, 3 * D))
    d["owqkv"] = np.ascontiguousarray(wqkvT[:, idx])
    d["obqkv"] = np.ascontiguousarray(bqkv[idx])
    d["owout"] = np.ascontiguousarray(g("L0_attn_w_out").T)
    d["obout"] = g("L0_attn_b_out")
    d["owent"] = np.ascontiguousarray(g("L0_attn_w_ent").T)
    d["obent"] = g("L0_attn_b_ent").reshape(1, 1)
    d["olng"] = g("L0_attn_ln_g")
    d["olnb"] = g("L0_attn_ln_b")
    Wq = g("L0_ffn_Wqkv")
    idxi = []
    for p in range(NP):
        idxi += list(range(p * 128, (p + 1) * 128))
        idxi += list(range(D + p * 128, D + (p + 1) * 128))
        idxi += list(range(2 * D + p * 128, 2 * D + (p + 1) * 128))
    d["iwqkv"] = np.ascontiguousarray(Wq[:, :, idxi])
    d["iwo"] = g("L0_ffn_Wo")
    W1 = g("L0_ffn_W1")
    b1 = g("L0_ffn_b1")
    idx1 = []
    for j in range(NJ):
        w = JW[j]
        idx1 += list(range(j * 128, j * 128 + w))
        idx1 += list(range(DI + j * 128, DI + j * 128 + w))
    d["iw1"] = np.ascontiguousarray(W1[:, :, idx1])
    d["ib1"] = np.ascontiguousarray(b1[:, idx1])
    d["iw2"] = g("L0_ffn_W2")
    d["ib2"] = g("L0_ffn_b2")
    d["ilng"] = g("L0_ffn_ln_g")
    d["ilnb"] = g("L0_ffn_ln_b")
    d["iqs"] = g("L0_ffn_q_scale")
    d["iks"] = g("L0_ffn_k_scale")
    d["eye"] = np.eye(128, dtype=np.float32)
    e4 = np.zeros((4, 4, 128), dtype=np.float32)
    for j in range(4):
        e4[j, j, :] = 1.0
    d["e4"] = e4
    u = np.arange(128)[:, None]
    v = np.arange(128)[None, :]
    d["tri_c"] = np.where(v <= u, 0.0, NEG).astype(np.float32)
    d["tri_g"] = np.where(v >= u, 0.0, NEG).astype(np.float32)
    half = np.arange(0, HD, 2, dtype=np.float64)
    inv_freq = 1.0 / 10000.0 ** (half / HD)
    t = np.arange(S, dtype=np.float64)
    fr = t[:, None] * inv_freq[None, :]
    fr = np.concatenate([fr, fr], -1)
    bs = (half + 0.4 * HD) / (1.4 * HD)
    pw = (t - S // 2) / (256 // 2)
    sc = bs[None, :] ** pw[:, None]
    sc = np.concatenate([sc, sc], -1)
    tm = lambda a: np.ascontiguousarray(
        a.reshape(16, 128, HD).transpose(1, 0, 2).astype(np.float32))
    d["cos8"] = tm(np.cos(fr) * sc * 8.0)
    d["sin8"] = tm(np.sin(fr) * sc * 8.0)
    d["cosi"] = tm(np.cos(fr) / sc)
    d["sini"] = tm(np.sin(fr) / sc)
    return d


class Ctr:
    __slots__ = ("sem", "n", "owner")

    def __init__(self, sem, owner=None):
        self.sem = sem
        self.n = 0
        self.owner = owner


class MK:
    ENGS = ("g", "y", "t", "v", "s")

    def __init__(self, nc):
        self.nc = nc
        self.q = {e: [] for e in self.ENGS}
        self.waited = {e: {} for e in self.ENGS}

    def op(self, eng, fn, wait=(), inc=None, inck=1):
        todo = []
        for (c, val) in wait:
            if val <= 0 or c.owner == eng:
                continue
            assert val <= c.n, f"wait on future {val} > {c.n} of {c.sem.name}"
            if self.waited[eng].get(c, -1) < val:
                self.waited[eng][c] = val
                todo.append((c, val))
        if inc is not None:
            inc.n += inck
        sem = inc.sem if inc is not None else None

        def run(e, todo=todo, fn=fn, sem=sem, inck=inck):
            for (c, val) in todo:
                e.wait_ge(c.sem, val)
            r = fn(e)
            if sem is not None:
                if r is None:
                    r = e.drain()
                r.then_inc(sem, inck)
        self.q[eng].append(run)


def view(region, off, dt, shape):
    esz = {F32: 4, BF16: 2, U8: 1}[dt]
    n = 1
    for s in shape[1:]:
        n *= s
    assert off % esz == 0
    a = region.ap().bitcast(dt)[:, off // esz: off // esz + n]
    if len(shape) == 3:
        a = a.rearrange("p (a b) -> p a b", a=shape[1])
    elif len(shape) == 4:
        a = a.rearrange("p (a b c) -> p a b c", a=shape[1], b=shape[2])
    if shape[0] != 128:
        a = a[0: shape[0]]
    return a


def build_program(n_layers=DEPTH, stage="final"):
    nc = bass.Bass("TRN2", target_bir_lowering=False, detect_race_conditions=False)
    dr = {}

    def din(name, shape):
        dr[name] = nc.dram_tensor(name, shape, F32, kind="ExternalInput").ap()

    din("x_fm", [D, S])
    din("owqkv", [D, 3 * D]); din("obqkv", [3 * D]); din("owout", [D, D])
    din("obout", [D]); din("owent", [D, 1]); din("obent", [1, 1])
    din("olng", [D]); din("olnb", [D])
    din("iwqkv", [DEPTH, D, 3 * D]); din("iwo", [DEPTH, D, D])
    din("iw1", [DEPTH, D, 2 * DI]); din("ib1", [DEPTH, 2 * DI])
    din("iw2", [DEPTH, DI, D]); din("ib2", [DEPTH, D])
    din("ilng", [DEPTH, D]); din("ilnb", [DEPTH, D])
    din("iqs", [DEPTH, HD]); din("iks", [DEPTH, HD])
    din("eye", [128, 128]); din("tri_c", [128, 128]); din("tri_g", [128, 128])
    din("e4", [4, 4, 128])
    din("cos8", [128, 16, HD]); din("sin8", [128, 16, HD])
    din("cosi", [128, 16, HD]); din("sini", [128, 16, HD])
    o_fm = nc.dram_tensor("o_fm", [D, S], F32, kind="ExternalOutput").ap()
    spX = nc.dram_tensor("spX", [D, S], F32, kind="Internal").ap()
    spX1 = nc.dram_tensor("spX1", [D, S], F32, kind="Internal").ap()
    spGate = nc.dram_tensor("spGate", [S], F32, kind="Internal").ap()
    d3 = lambda name: dr[name].rearrange("(c p) t -> p c t", p=128)
    dw3 = lambda ap2d: ap2d.rearrange("(c p) n -> p c n", p=128)
    dcol = lambda ap1d: ap1d.rearrange("(c p) -> p c", p=128)
    drow = lambda ap1d: ap1d.rearrange("(o n) -> o n", o=1)

    with ExitStack() as ctx:
        sbuf = lambda n, sh, dt: ctx.enter_context(nc.sbuf_tensor(n, sh, dt))
        psum = lambda n, sh: ctx.enter_context(nc.psum_tensor(n, sh, F32))

        def mksem(n, owner=None):
            return Ctr(ctx.enter_context(nc.semaphore(n)), owner)

        RY = sbuf("RY", [128, 65536], U8)
        RXA = sbuf("RXA", [128, 32768], U8)
        ROF = sbuf("ROF", [128, 32768], U8)
        RSC = sbuf("RSC", [128, 49152], U8)
        RCN = sbuf("RCN", [128, 32032], U8)

        psS = psum("psS", [128, 2048])
        psSB = psum("psSB", [128, 2048])
        psA = psSB[:, 0:512]
        psB = psSB[:, 512:1024]
        psC = psSB[:, 1024:1536]
        psT = psSB[:, 1536:2048]

        T = mksem("T", "t"); V = mksem("V", "v"); SG = mksem("SG", "s")
        cDg = mksem("cDg"); cDs = mksem("cDs"); xD = mksem("xD")
        wS0 = mksem("wS0"); wS1 = mksem("wS1")
        w1S0 = mksem("w1S0"); w1S1 = mksem("w1S1")
        w2S0 = mksem("w2S0"); w2S1 = mksem("w2S1")
        woS = mksem("woS"); wvS = mksem("wvS")
        spXS = mksem("spXS"); spX1S = mksem("spX1S")
        rlS0 = mksem("rlS0"); rlS1 = mksem("rlS1")
        gcS = mksem("gcS"); ldS = mksem("ldS")
        stO = mksem("stO")

        mk = MK(nc)
        E = mk.op

        def BART(extra=()):
            return [(V, V.n), (SG, SG.n)] + list(extra)

        Y = view(RY, 0, F32, [128, NC, S])
        XA = view(RXA, 0, BF16, [128, NC, S])
        OF = view(ROF, 0, BF16, [128, NC, S])
        cEYE = view(RCN, 0, BF16, [128, 128])
        cTRIC = view(RCN, 256, BF16, [128, 128])
        cTRIG = view(RCN, 512, BF16, [128, 128])
        cC8 = view(RCN, 768, BF16, [128, 16, HD])
        cS8 = view(RCN, 2816, BF16, [128, 16, HD])
        cCI = view(RCN, 4864, BF16, [128, 16, HD])
        cSI = view(RCN, 6912, BF16, [128, 16, HD])
        cE8 = view(RCN, 8960, BF16, [128, 8, 4])
        cE4 = view(RCN, 9088, F32, [128, 4, 128])
        cONE1 = view(RCN, 11136, F32, [1, 128])
        cPOOL = view(RCN, 11648, F32, [128, 8])
        cBQK = view(RCN, 11680, F32, [128, 16])
        cGOUT = view(RCN, 11744, F32, [128, 8])
        cBOUT = view(RCN, 11776, F32, [128, 8])
        cBO01 = view(RCN, 11808, F32, [128, 8])
        cWENT = view(RCN, 11840, BF16, [128, 8])
        cBENT = view(RCN, 11856, F32, [1, 1])
        cGCOL = view(RCN, 11872, F32, [128, 16])
        cSTAT = view(RCN, 11936, F32, [4, 512])
        cST2 = view(RCN, 29936, F32, [4, 512])
        cNEG1 = view(RCN, 31984, F32, [128, 1])
        cMU = view(RCN, 13984, F32, [4, 512])
        cCS = view(RCN, 16032, F32, [4, 512])
        cROW = view(RCN, 18080, F32, [4, 512])
        cQSR = view(RCN, 20128, F32, [1, 256])
        cLGB = view(RCN, 21152, F32, [128, 16])
        cB1 = view(RCN, 21216, F32, [128, 44])
        cB2 = view(RCN, 21392, F32, [128, 8])
        cELL = view(RCN, 21424, F32, [128, 1])
        cELL2 = view(RCN, 21472, F32, [128, 1])
        cNM = view(RCN, 21440, F32, [128, 1])
        cRC = view(RCN, 21456, F32, [128, 1])
        cRN = view(RCN, 21488, F32, [128, 64])
        AQ = view(RCN, 21744, BF16, [128, 16, HD])
        BQ = view(RCN, 23792, BF16, [128, 16, HD])
        AK = view(RCN, 25840, BF16, [128, 16, HD])
        BK = view(RCN, 27888, BF16, [128, 16, HD])
        gRow = view(RCN, 11936, F32, [1, 2048])
        qsRep = view(RCN, 11936, BF16, [128, 4, 64])

        xsq = view(RSC, 25600, BF16, [128, S])
        tscr = view(RSC, 29696, F32, [128, 512])
        bvR = view(RSC, 0, F32, [128, D])
        tvs = view(RSC, 4096, F32, [128, 512])
        bvRow = view(RSC, 41264, F32, [1, D])
        vTM = view(RY, 0, BF16, [128, 16, D])
        wvSB = view(RY, 32768, BF16, [128, NC, D])
        qpSB = view(RSC, 0, BF16, [128, S])
        kpSB = view(RSC, 4096, BF16, [128, S])
        pSBp = view(RSC, 8192, BF16, [128, 2, S])
        ptSBp = view(RSC, 16384, BF16, [128, 2, S])
        wqhSB = [view(RSC, 24576, BF16, [128, NC, 256]),
                 view(RSC, 28672, BF16, [128, NC, 256])]
        pSB2p = view(RSC, 32768, BF16, [128, 2, S])
        oMX = view(RSC, 40960, F32, [128, 2, 4, 4])
        oNM = view(RSC, 41088, F32, [128, 2, 4])
        oLL = view(RSC, 41120, F32, [128, 2, 4])
        oRC = view(RSC, 41152, F32, [128, 2, 4])
        oEL = view(RSC, 41184, F32, [128, 2, 4])
        oJK = view(RSC, 41216, F32, [128, 4])
        oTa = view(RSC, 41232, F32, [128, 1])
        oTb = view(RSC, 41248, F32, [128, 1])
        woSBo = view(RSC, 16384, BF16, [128, NC, D])
        xrl = view(RSC, 8192, F32, [128, 2, 512])
        twx = view(RSC, 4096, F32, [128, 512])
        x1v = Y
        wqp = view(RSC, 0, BF16, [128, NC, 384])
        qkvTM = view(RSC, 6144, BF16, [128, 16, 384])
        qv6 = view(RSC, 6144, BF16, [128, 16, 6, HD])
        q2k2 = view(RSC, 18432, BF16, [128, 16, 256])
        q2k4 = view(RSC, 18432, BF16, [128, 16, 4, HD])
        qkFM = view(RSC, 26624, BF16, [128, 2, S])
        sqB = view(RSC, 34816, BF16, [128, 16, 256])
        sq64 = view(RSC, 34816, BF16, [128, 64, 64])
        qrB = view(RSC, 34816, BF16, [128, 16, 2, HD])
        m1B = view(RSC, 38912, BF16, [128, 16, 128])
        m14 = view(RSC, 38912, BF16, [128, 16, 2, HD])
        pIn = view(RSC, 43008, BF16, [128, 2, 384])
        ptIn = view(RSC, 44544, BF16, [128, 2, 384])
        diagI = view(RSC, 46080, BF16, [128, 128])
        diagT2 = view(RSC, 46080, BF16, [128, 2, 128])
        sxNM = view(RSC, 46592, F32, [128, 2, 4])
        sxL2 = view(RSC, 46624, F32, [128, 2, 4])
        sxLL = view(RSC, 46656, F32, [128, 2, 4])
        sxRC = view(RSC, 46688, F32, [128, 2, 4])
        pNr = view(RSC, 47168, BF16, [128, 2, 384])
        wobuf = view(RXA, 0, BF16, [128, NC, D])
        w1b = [view(RSC, 12288, BF16, [128, NC, 256]),
               view(RSC, 16384, BF16, [128, NC, 256])]
        w2b = [view(RSC, 20480, BF16, [128, D]),
               view(RSC, 22528, BF16, [128, D])]
        gelB2 = view(RSC, 24576, BF16, [128, 2, 512])
        Hlo = view(ROF, 0, BF16, [128, 16, 1024])
        Hhi = view(RSC, 0, BF16, [128, 6, 1024])
        x1rl = view(RSC, 0, F32, [128, 2, S])

        def H_ap(j, lo, w):
            return (Hlo[:, j, lo:lo + w] if j < 16
                    else Hhi[:, j - 16, lo:lo + w])

        def ps_oc(oc):
            if oc == 0:
                return psA[:, :]
            if oc == 1:
                return psB[:, :]
            if oc == 2:
                return psC[:, :]
            if oc < 7:
                return psS[:, (oc - 3) * 512:(oc - 2) * 512]
            return psT[:, :]

        def g_dma(dst, src, ctr, wait=(), slow=False):
            def fn(g, d2=dst, s2=src, sl=slow):
                if sl:
                    return g.dma_start(d2, s2, allow_slow_non_contiguous=True)
                return g.dma_start(d2, s2)
            E("g", fn, wait=wait, inc=ctr, inck=16)

        def y_dma(dst, src, ctr, wait=(), slow=False):
            def fn(y, d2=dst, s2=src, sl=slow):
                if sl:
                    return y.dma_start(d2, s2, allow_slow_non_contiguous=True)
                return y.dma_start(d2, s2)
            E("y", fn, wait=wait, inc=ctr, inck=16)

        y_dma(Y[:, :, :], d3("x_fm"), xD)
        g_dma(cEYE, dr["eye"], cDg)
        g_dma(cTRIC, dr["tri_c"], cDg)
        g_dma(cTRIG, dr["tri_g"], cDg)
        g_dma(cWENT, dr["owent"].rearrange("(c p) o -> p (c o)", p=128), cDg,
              slow=True)
        g_dma(cC8, dr["cos8"], cDg)
        g_dma(cS8, dr["sin8"], cDg)
        g_dma(cCI, dr["cosi"], cDg)
        g_dma(cSI, dr["sini"], cDg)
        NCDG = cDg.n
        y_dma(cBQK, dcol(dr["obqkv"][0:2048]), cDs, slow=True)
        y_dma(cGOUT, dcol(dr["olng"]), cDs, slow=True)
        y_dma(cBOUT, dcol(dr["olnb"]), cDs, slow=True)
        y_dma(cBO01, dcol(dr["obout"]), cDs, slow=True)
        y_dma(cBENT, dr["obent"], cDs)
        y_dma(cE4[0:4, :, :], dr["e4"], cDs)
        NCDS = cDs.n

        def init_consts(v):
            v.memset(cE8[:, :, :], 0.0)
            for j in range(4):
                v.memset(cE8[:, j, j:j + 1], 1.0)
            v.memset(cONE1[:, :], 1.0)
            v.memset(cPOOL[:, 0:1], 1.25)
            v.memset(cPOOL[:, 1:2], 0.1)
            v.memset(cPOOL[:, 2:3], -0.5)
            v.memset(cPOOL[:, 3:4], 1e-38)
            v.memset(cPOOL[:, 4:5], 1.0 + 1e-6)
            v.memset(cPOOL[:, 5:6], 1e-11)
            v.memset(cPOOL[:, 6:7], 1e-5)
            v.memset(cPOOL[:, 7:8], 0.0)
            return v.memset(cNEG1[:, :], -1.0)
        E("v", init_consts, inc=V)
        V_CONST = V.n
        E("v", lambda v: v.tensor_scalar_mul(cBQK[:, 0:16:2], cBQK[:, 0:16:2], 1.25),
          wait=[(cDs, NCDS)], inc=V)
        E("v", lambda v: v.tensor_scalar_mul(cBO01[:, :], cBO01[:, :], 0.1), inc=V)

        def emit_stats(src_f32, src_bf, shifted):
            pre = BART([(xD, 16), (cDg, NCDG), (T, T.n)])
            for c in range(NC):
                if shifted and c >= 4:
                    E("s", lambda s, c=c: s.memzero(src_bf[:, c, 0:2]), wait=pre)
                    E("s", lambda s, c=c: s.activation(
                        src_bf[:, c, 1:S], src_f32[:, c, 0:S - 1], AF.Identity),
                      inc=SG)
                else:
                    E("s", lambda s, c=c: s.activation(
                        src_bf[:, c, :], src_f32[:, c, :], AF.Identity),
                      wait=pre, inc=SG)
            sCAST = SG.n
            firstC = True
            firstB = True
            for c in range(NC):
                E("v", lambda v, c=c: v.tensor_mul(
                    xsq[:, :], src_bf[:, c, :], src_bf[:, c, :]),
                  wait=[(SG, sCAST), (V, V_CONST), (T, T.n)], inc=V)
                vSq = V.n
                for j in range(4):
                    last = (c == NC - 1) and (j == 3)
                    E("t", lambda t, c=c, j=j, st=firstC, last=last: t.matmul(
                        psC[0:4, :], cE8[:, j, :],
                        src_bf[:, c, j * 512:(j + 1) * 512],
                        start=st, stop=last),
                      wait=BART([(SG, sCAST)]) if firstC else [])
                    firstC = False
                for j in range(4):
                    last = (c == NC - 1) and (j == 3)
                    E("t", lambda t, j=j, c=c, st=firstB, last=last: t.matmul(
                        psB[0:4, :], cE8[:, j, :],
                        xsq[:, j * 512:(j + 1) * 512],
                        start=st, stop=last),
                      wait=(BART([(V, vSq)]) if firstB else [(V, vSq)]),
                      inc=T)
                    firstB = False
            tSt = T.n
            E("v", lambda v: v.tensor_copy(cSTAT[:, :], psC[0:4, :]),
              wait=[(T, tSt)])
            E("v", lambda v: v.tensor_copy(cST2[:, :], psB[0:4, :]),
              wait=[(T, tSt)], inc=V)
            return V.n

        def emit_row_cs(vStats, mode):
            E("v", lambda v: v.tensor_scalar_mul(cMU[:, :], cSTAT[:, :], 1.0 / D),
              wait=[(V, vStats)])
            E("v", lambda v: v.tensor_scalar_mul(cROW[:, :], cST2[:, :], 1.0 / D))
            E("v", lambda v: v.tensor_mul(cCS[:, :], cMU[:, :], cMU[:, :]))
            E("v", lambda v: v.tensor_sub(cROW[:, :], cROW[:, :], cCS[:, :]), inc=V)
            vVar = V.n
            if mode == "outer":
                E("s", lambda s: s.activation(cROW[:, :], cROW[:, :], AF.Ln,
                                              bias=cPOOL[0:4, 5:6],
                                              scale=cPOOL[0:4, 4:5]),
                  wait=[(V, vVar)])
            else:
                E("s", lambda s: s.activation(cROW[:, :], cROW[:, :], AF.Ln,
                                              bias=cPOOL[0:4, 6:7], scale=1.0),
                  wait=[(V, vVar)])
            E("s", lambda s: s.activation(cCS[:, :], cROW[:, :], AF.Exp,
                                          scale=cPOOL[0:4, 2:3]), inc=SG)
            return SG.n

        def emit_normalize(sCS, src_f32, dst, shifted, affine, out_f32=None):
            zcol = cPOOL[:, 7:8]
            for blk in range(4):
                E("t", lambda t, blk=blk: t.matmul(
                    psA[:, :], cE4[0:4, blk, :], cMU[:, :], start=True, stop=True),
                  wait=BART([(SG, sCS), (V, V_CONST)]))
                E("t", lambda t, blk=blk: t.matmul(
                    psB[:, :], cE4[0:4, blk, :], cCS[:, :], start=True, stop=True),
                  inc=T)
                tBC = T.n
                for c in range(NC):
                    lo = blk * 512
                    sh = shifted and c >= 4

                    def sub(v, c=c, lo=lo, sh=sh):
                        if not sh:
                            return v.tensor_sub(
                                tscr[:, :], src_f32[:, c, lo:lo + 512], psA[:, :])
                        if lo == 0:
                            v.tensor_sub(tscr[:, 0:1], zcol, psA[:, 0:1])
                            return v.tensor_sub(
                                tscr[:, 1:512], src_f32[:, c, 0:511], psA[:, 1:512])
                        return v.tensor_sub(
                            tscr[:, :], src_f32[:, c, lo - 1:lo + 511], psA[:, :])
                    E("v", sub, wait=[(T, tBC), (SG, SG.n)])
                    if out_f32 is not None:
                        E("v", lambda v, c=c, lo=lo: v.tensor_mul(
                            out_f32[:, c, lo:lo + 512], tscr[:, :], psB[:, :]),
                          inc=V)
                    else:
                        E("v", lambda v: v.tensor_mul(
                            tscr[:, :], tscr[:, :], psB[:, :]), inc=V)
                        vN = V.n
                        if affine is not None:
                            gg, bb = affine
                            E("s", lambda s, c=c, lo=lo, gg=gg, bb=bb: s.activation(
                                dst[:, c, lo:lo + 512], tscr[:, :], AF.Identity,
                                bias=bb[:, c:c + 1], scale=gg[:, c:c + 1]),
                              wait=[(V, vN)], inc=SG)
                        else:
                            E("s", lambda s, c=c, lo=lo: s.activation(
                                dst[:, c, lo:lo + 512], tscr[:, :], AF.Identity),
                              wait=[(V, vN)], inc=SG)
            return V.n, SG.n

        vSt = emit_stats(Y, XA, False)
        sCS = emit_row_cs(vSt, "outer")
        vXN, sXN = emit_normalize(sCS, Y, XA, False, (cGOUT, cBOUT))

        if stage == "xn":
            for c in range(NC):
                E("s", lambda s, c=c: s.activation(
                    Y[:, c, :], XA[:, c, :], AF.Identity),
                  wait=[(SG, sXN), (V, vXN)], inc=SG)
            _finish(mk, E, o_fm, Y, stO, [(SG, SG.n)])
            _replay(mk, nc)
            return nc

        y_dma(spX.rearrange("(c p) t -> p c t", p=128), Y[:, :, :], spXS,
              wait=[(V, vXN), (SG, sXN)])

        for blk in range(4):
            for c in range(NC):
                E("t", lambda t, blk=blk, c=c: t.matmul(
                    psS[0:1, blk * 512:(blk + 1) * 512], cWENT[:, c:c + 1],
                    XA[:, c, blk * 512:(blk + 1) * 512],
                    start=(c == 0), stop=(c == NC - 1)),
                  wait=BART([(SG, sXN), (cDg, NCDG)]),
                  inc=(T if c == NC - 1 else None))
        tGate = T.n
        E("s", lambda s: s.activation(gRow[:, :], psS[0:1, 0:2048], AF.Sigmoid,
                                      bias=cBENT[0:1, 0:1]),
          wait=[(T, tGate), (cDs, NCDS)], inc=SG)
        E("v", lambda v: v.tensor_scalar(gRow[:, :], gRow[:, :], 0.1, 2.0,
                                         ALU.max, ALU.min),
          wait=[(SG, SG.n)], inc=V)
        y_dma(spGate, gRow[0:1, :], gcS, wait=[(V, V.n)])
        y_dma(cGCOL[:, :], spGate.rearrange("(a p) -> p a", p=128), gcS,
              wait=[(gcS, gcS.n)], slow=True)

        g_dma(wvSB, dw3(dr["owqkv"])[:, :, 2048:3072], wvS, wait=[(spXS, 16)])
        y_dma(bvRow, drow(dr["obqkv"][2048:3072]), cDs)
        NCDS2 = cDs.n
        for h in range(2):
            E("t", lambda t, h=h: t.matmul(
                psA[:, :], cONE1[:, :], bvRow[0:1, h * 512:(h + 1) * 512],
                start=True, stop=True),
              wait=BART([(cDs, NCDS2), (V, V_CONST), (spXS, 16)]), inc=T)
            E("v", lambda v, h=h: v.tensor_copy(
                bvR[:, h * 512:(h + 1) * 512], psA[:, :]),
              wait=[(T, T.n)], inc=V)
        vBv = V.n
        for tt in range(16):
            for vb in range(2):
                pp = psA if vb == 0 else psB
                for c in range(NC):
                    E("t", lambda t, tt=tt, vb=vb, c=c, pp=pp: t.matmul(
                        pp[:, :], XA[:, c, tt * 128:(tt + 1) * 128],
                        wvSB[:, c, vb * 512:(vb + 1) * 512],
                        start=(c == 0), stop=(c == NC - 1)),
                      wait=BART([(wvS, 16), (SG, sXN), (V, vBv)]),
                      inc=(T if c == NC - 1 else None))
                tV = T.n
                E("v", lambda v, pp=pp, vb=vb: v.tensor_add(
                    tvs[:, :], pp[:, :], bvR[:, vb * 512:(vb + 1) * 512]),
                  wait=[(T, tV), (SG, SG.n)], inc=V)
                E("s", lambda s, tt=tt, vb=vb: s.activation(
                    vTM[:, tt, vb * 512:(vb + 1) * 512], tvs[:, :], AF.Identity,
                    scale=cGCOL[:, tt:tt + 1]),
                  wait=[(V, V.n), (gcS, 32)], inc=SG)
        sVTM = SG.n

        wq_last = [0, 0]
        NPrun = 1 if stage == "dbg0" else NP
        for p in range(NPrun):
            par = p % 2
            wctr = wS0 if par == 0 else wS1
            wbuf = wqhSB[par]
            g_dma(wbuf, dw3(dr["owqkv"])[:, :, p * 256:(p + 1) * 256], wctr,
                  wait=[(T, wq_last[par]), (spXS, 16)])
            nw = wctr.n
            for qk in range(2):
                dst = qpSB if qk == 0 else kpSB
                for tb in range(4):
                    pp = psA if tb % 2 == 0 else psB
                    for c in range(NC):
                        E("t", lambda t, qk=qk, tb=tb, c=c, pp=pp, wbuf=wbuf:
                          t.matmul(
                            pp[:, :], wbuf[:, c, qk * 128:qk * 128 + 128],
                            XA[:, c, tb * 512:(tb + 1) * 512],
                            start=(c == 0), stop=(c == NC - 1)),
                          wait=BART([(wctr, nw), (SG, sVTM)]),
                          inc=(T if c == NC - 1 else None))
                    tP = T.n

                    def ev(s, qk=qk, tb=tb, pp=pp, dst=dst, p=p):
                        sc = cPOOL[:, 0:1] if qk == 0 else 1.0
                        return s.activation(
                            dst[:, tb * 512:(tb + 1) * 512], pp[:, :], AF.Identity,
                            scale=sc, bias=cBQK[:, 2 * p + qk:2 * p + qk + 1])
                    E("s", ev, wait=[(T, tP)], inc=SG)
            wq_last[par] = T.n
            sProj = SG.n

            def sbank(pu, kb):
                if pu == 0:
                    return psS[:, kb * 512:(kb + 1) * 512]
                return (psA, psB, psC, psT)[kb][:, :]

            ounits = [(h01, qt) for h01 in range(2) for qt in range(16)]
            tSo_, tPTo_, tPVo_ = {}, {}, {}
            sE1o_, sE2o_, sRCo_ = {}, {}, {}
            vRSo_, vPTo_, vOFo_ = {}, {}, {}

            def out_S(i):
                h01, qt = ounits[i]
                b0, b1 = 64 * h01, 64 * h01 + 64
                pu = i % 2
                nk = (qt + 1) * 128
                nbank = (nk + 511) // 512
                for kb in range(nbank):
                    klo = kb * 512
                    kw = min(512, nk - klo)
                    if kb == 0:
                        w = [(SG, sProj)]
                        if i >= 2:
                            w.append((V, vOFo_[i - 2]))
                        if i == 0:
                            w = BART(w)
                    else:
                        w = []
                    E("t", lambda t, qt=qt, klo=klo, kw=kw, b0=b0, b1=b1,
                      pu=pu, kb=kb, sp=(kb != nbank - 1): t.matmul(
                        sbank(pu, kb)[:, 0:kw],
                        qpSB[b0:b1, qt * 128:(qt + 1) * 128],
                        kpSB[b0:b1, klo:klo + kw],
                        start=True, stop=sp),
                      wait=w)
                mo = qt * 128 - (nbank - 1) * 512
                E("t", lambda t, pu=pu, nbank=nbank, mo=mo: t.matmul(
                    sbank(pu, nbank - 1)[:, mo:mo + 128], cEYE[:, :],
                    cTRIC[:, :], start=False, stop=True), inc=T)
                tSo_[i] = T.n

            def out_soft(i):
                h01, qt = ounits[i]
                head = 2 * p + h01
                pu = i % 2
                nk = (qt + 1) * 128
                nbank = (nk + 511) // 512
                w = [(T, tSo_[i])]
                if i >= 2:
                    w.append((SG, sE1o_[i - 2]))
                if pu == 0:
                    E("v", lambda v, nk=nk, pu=pu: v.reduce_max(
                        oNM[:, pu, 0:1], psS[:, 0:nk], axis=AX.X, negate=True),
                      wait=w, inc=V)
                else:
                    E("v", lambda v, nk=nk, pu=pu: v.reduce_max(
                        oNM[:, pu, 0:1], psSB[:, 0:nk], axis=AX.X,
                        negate=True),
                      wait=w, inc=V)
                vNM = V.n
                for kb in range(nbank):
                    klo = kb * 512
                    kw = min(512, nk - klo)
                    w = [(V, vNM)]
                    if kb == 0 and i >= 2:
                        w.append((V, vRSo_[i - 2]))
                    E("s", lambda s, kb=kb, klo=klo, kw=kw, pu=pu: s.activation(
                        pSBp[:, pu, klo:klo + kw], sbank(pu, kb)[:, 0:kw],
                        AF.Exp, bias=oNM[:, pu, 0:1]),
                      wait=(w if kb == 0 else []),
                      inc=(SG if kb == nbank - 1 else None))
                sE1o_[i] = SG.n
                if stage == "dbg0" and i == 7:
                    E("v", lambda v: v.tensor_copy(Y[:, 3, 0:512], psA[:, :]),
                      wait=[(SG, sE1o_[i])])
                    E("v", lambda v: v.tensor_copy(Y[:, 4, 0:512], psB[:, :]))
                    E("v", lambda v: v.tensor_copy(
                        Y[:, 2, 8:9], oNM[:, 1, 0:1]))
                    for kb in range(4):
                        E("v", lambda v, kb=kb: v.tensor_copy(
                            Y[:, 2, 4 + kb:5 + kb], oMX[:, 1, kb, 0:1]))
                    E("s", lambda s: s.activation(
                        Y[:, 0, 0:512], pSBp[:, 1, 0:512], AF.Identity),
                      wait=[(SG, sE1o_[i])], inc=SG)
                    E("s", lambda s: s.activation(
                        Y[:, 1, 0:512], pSBp[:, 1, 512:1024], AF.Identity),
                      inc=SG)
                w2 = [(SG, sE1o_[i])]
                if i >= 2:
                    w2.append((SG, sRCo_[i - 2]))
                E("v", lambda v, nk=nk, pu=pu: v.reduce_sum(
                    oEL[:, pu, 0:1], pSBp[:, pu, 0:nk], axis=AX.X),
                  wait=w2, inc=V)
                vRSo_[i] = V.n
                E("s", lambda s, pu=pu: s.activation(
                    oLL[:, pu, 0:1], oEL[:, pu, 0:1], AF.Ln),
                  wait=[(V, vRSo_[i])])
                E("s", lambda s, pu=pu: s.activation(
                    oRC[:, pu, 0:1], oLL[:, pu, 0:1], AF.Exp, scale=cNEG1[:, :]),
                  inc=SG)
                sRCo_[i] = SG.n
                if stage == "dbg0" and i == 7:
                    E("v", lambda v: v.tensor_copy(Y[:, 2, 0:1], oEL[:, 1, 0:1]),
                      wait=[(SG, sRCo_[i])])
                    E("v", lambda v: v.tensor_copy(Y[:, 2, 1:2], oLL[:, 1, 0:1]))
                    E("v", lambda v: v.tensor_copy(Y[:, 2, 2:3], oRC[:, 1, 0:1]),
                      inc=V)
                E("s", lambda s, nk=nk, pu=pu: s.activation(
                    pSB2p[:, pu, 0:nk], pSBp[:, pu, 0:nk], AF.Identity,
                    scale=oRC[:, pu, 0:1]),
                  wait=([(T, tPTo_[i - 2])] if i >= 2 else []), inc=SG)
                sE2o_[i] = SG.n
                ptb = psS[:, 512:1024] if pu == 0 else psB[:, :]
                pvb = psS if pu == 0 else psA
                nkc = qt + 1
                vploc = None
                for g0 in range(0, nkc, 4):
                    gn = min(4, nkc - g0)
                    for ki in range(gn):
                        kc = g0 + ki
                        if ki == 0:
                            if g0 == 0:
                                w = [(SG, sE2o_[i])]
                                if i >= 2:
                                    w.append((V, vPTo_[i - 2]))
                            else:
                                w = [(V, vploc)]
                        else:
                            w = []
                        E("t", lambda t, kc=kc, ki=ki, pu=pu, ptb=ptb: t.matmul(
                            ptb[:, ki * 128:(ki + 1) * 128],
                            pSB2p[:, pu, kc * 128:(kc + 1) * 128],
                            cEYE[:, :], start=True, stop=True),
                          wait=w, inc=(T if ki == gn - 1 else None))
                    tg = T.n
                    E("v", lambda v, g0=g0, gn=gn, pu=pu, ptb=ptb: v.tensor_copy(
                        ptSBp[:, pu, g0 * 128:(g0 + gn) * 128],
                        ptb[:, 0:gn * 128]),
                      wait=([(T, tg)]
                            + ([(T, tPVo_[i - 2])]
                               if (g0 == 0 and i >= 2) else [])), inc=V)
                    vploc = V.n
                tPTo_[i] = T.n
                vPTo_[i] = vploc
                for kc in range(nkc):
                    E("t", lambda t, kc=kc, head=head, h01=h01, pu=pu, pvb=pvb,
                      st=(kc == 0), sp=(kc == nkc - 1): t.matmul(
                        pvb[0:64, 0:128] if h01 == 0
                        else pvb[64:128, 0:128],
                        vTM[:, kc, head * 64:head * 64 + 64],
                        ptSBp[:, pu, kc * 128:(kc + 1) * 128],
                        start=st, stop=sp,
                        tile_position=((0, 64) if h01 == 1 else None)),
                      wait=([(V, vPTo_[i])] if kc == 0 else []),
                      inc=(T if kc == nkc - 1 else None))
                tPVo_[i] = T.n
                ch, po = (head * 64) // 128, (head * 64) % 128
                E("v", lambda v, ch=ch, po=po, qt=qt, h01=h01, pvb=pvb:
                  v.tensor_copy(
                    OF[po:po + 64, ch, qt * 128:(qt + 1) * 128],
                    pvb[0:64, 0:128] if h01 == 0 else pvb[64:128, 0:128]),
                  wait=[(T, tPVo_[i])], inc=V)
                vOFo_[i] = V.n

            OPIPE = True
            if OPIPE:
                out_S(0)
                out_S(1)
                for i in range(32):
                    out_soft(i)
                    if i + 2 < 32:
                        out_S(i + 2)
            else:
                for i in range(32):
                    out_S(i)
                    out_soft(i)
        vAttO = V.n

        if stage == "dbg0":
            _finish(mk, E, o_fm, Y, stO, [(V, V.n), (SG, SG.n), (T, T.n)])
            _replay(mk, nc)
            return nc

        if stage == "of":
            for c in range(NC):
                E("s", lambda s, c=c: s.activation(
                    Y[:, c, :], OF[:, c, :], AF.Identity),
                  wait=[(V, vAttO), (T, T.n), (SG, SG.n)], inc=SG)
            _finish(mk, E, o_fm, Y, stO, [(SG, SG.n)])
            _replay(mk, nc)
            return nc

        g_dma(woSBo, dw3(dr["owout"]), woS, wait=[(T, T.n)])
        nwo = woS.n
        rl_last = [vAttO, vAttO]
        for oc in range(NC):
            for tb in range(4):
                rb = (oc * 4 + tb) % 2
                rctr = rlS0 if rb == 0 else rlS1
                y_dma(xrl[:, rb, :],
                      spX.rearrange("(c p) t -> p c t", p=128)
                      [:, oc, tb * 512:(tb + 1) * 512],
                      rctr, wait=[(spXS, 16), (V, rl_last[rb])])
                nrl = rctr.n
                pp = psA if tb % 2 == 0 else psB
                for c in range(NC):
                    E("t", lambda t, oc=oc, tb=tb, c=c, pp=pp: t.matmul(
                        pp[:, :], woSBo[:, c, oc * 128:(oc + 1) * 128],
                        OF[:, c, tb * 512:(tb + 1) * 512],
                        start=(c == 0), stop=(c == NC - 1)),
                      wait=BART([(woS, nwo), (V, vAttO)]),
                      inc=(T if c == NC - 1 else None))
                tWo = T.n
                E("s", lambda s, pp=pp, oc=oc: s.activation(
                    twx[:, :], pp[:, :], AF.Identity, scale=cPOOL[:, 1:2],
                    bias=cBO01[:, oc:oc + 1]),
                  wait=[(T, tWo), (V, V.n)], inc=SG)
                sWo = SG.n
                E("v", lambda v, oc=oc, tb=tb, rb=rb: v.tensor_add(
                    x1v[:, oc, tb * 512:(tb + 1) * 512], twx[:, :], xrl[:, rb, :]),
                  wait=[(SG, sWo), (rctr, nrl)], inc=V)
                rl_last[rb] = V.n
            y_dma(spX1.rearrange("(c p) t -> p c t", p=128)[:, oc, :],
                  x1v[:, oc, :], spX1S, wait=[(V, V.n)])
        vX1 = V.n

        if stage == "x1":
            _finish(mk, E, o_fm, Y, stO, [(V, vX1)])
            _replay(mk, nc)
            return nc

        vSt2 = emit_stats(x1v, XA, False)
        sCS2 = emit_row_cs(vSt2, "plain")
        vH2, _ = emit_normalize(sCS2, x1v, None, False, None, out_f32=Y)
        iwq_last0_init = T.n
        iwq_vgate0_init = V.n

        if stage == "h2":
            _finish(mk, E, o_fm, Y, stO, [(V, vH2)])
            _replay(mk, nc)
            return nc

        iwq_last = [iwq_last0_init]
        iwq_vgate = [iwq_vgate0_init]
        w1_last = [iwq_last0_init, iwq_last0_init]
        w2_last = [iwq_last0_init, iwq_last0_init]
        wo_last = [0]

        def inner_layer(L):
            nld0 = ldS.n
            y_dma(cLGB[:, 0:8], dcol(dr["ilng"][L]), ldS,
                  wait=[(SG, SG.n), (V, V.n)], slow=True)
            y_dma(cLGB[:, 8:16], dcol(dr["ilnb"][L]), ldS, slow=True)
            y_dma(cB1[:, 0:42],
                  dr["ib1"][L][0:5376].rearrange("(j p) -> p j", p=128),
                  ldS, slow=True)
            y_dma(cB1[0:42, 42:43], drow(dr["ib1"][L][5376:5418]).rearrange(
                "o n -> n o"), ldS, slow=True)
            y_dma(cB1[0:42, 43:44], drow(dr["ib1"][L][5418:5460]).rearrange(
                "o n -> n o"), ldS, slow=True)
            y_dma(cB2[:, :], dcol(dr["ib2"][L]), ldS, slow=True)
            y_dma(cQSR[0:1, 0:64], drow(dr["iqs"][L]), ldS)
            y_dma(cQSR[0:1, 128:192], drow(dr["iks"][L]), ldS)
            nld = ldS.n
            E("v", lambda v: v.tensor_scalar_mul(
                cQSR[0:1, 64:96], cQSR[0:1, 32:64], -1.0),
              wait=[(ldS, nld)])
            E("v", lambda v: v.tensor_copy(cQSR[0:1, 96:128], cQSR[0:1, 0:32]))
            E("v", lambda v: v.tensor_scalar_mul(
                cQSR[0:1, 192:224], cQSR[0:1, 160:192], -1.0))
            E("v", lambda v: v.tensor_copy(cQSR[0:1, 224:256], cQSR[0:1, 128:160]),
              inc=V)
            vQSR = V.n
            for r in range(4):
                E("t", lambda t, r=r: t.matmul(
                    psA[:, 0:64], cONE1[:, :], cQSR[0:1, r * 64:(r + 1) * 64],
                    start=True, stop=True),
                  wait=BART([(V, vQSR)]), inc=T)
                E("v", lambda v, r=r: v.tensor_copy(qsRep[:, r, :], psA[:, 0:64]),
                  wait=[(T, T.n)], inc=V)
            vRep = V.n
            for (tab, trig, r) in ((AQ, cC8, 0), (BQ, cS8, 1), (AK, cCI, 2),
                                   (BK, cSI, 3)):
                E("v", lambda v, tab=tab, trig=trig, r=r: v.tensor_tensor(
                    tab[:, :, :], trig[:, :, :],
                    qsRep[:, r:r + 1, :].broadcast_to([128, 16, 64]), ALU.mult),
                  wait=[(V, vRep), (cDg, NCDG)], inc=V)
            vAB = V.n

            for c in range(NC):
                if c >= 4:
                    E("s", lambda s, c=c: s.memzero(XA[:, c, 0:2]),
                      wait=[(T, T.n)])
                    E("s", lambda s, c=c: s.activation(
                        XA[:, c, 1:S], Y[:, c, 0:S - 1], AF.Identity), inc=SG)
                else:
                    E("s", lambda s, c=c: s.activation(
                        XA[:, c, :], Y[:, c, :], AF.Identity),
                      wait=[(T, T.n)], inc=SG)
            sXAi = SG.n

            for p in range(NP):
                inner_pair(L, p, sXAi, vAB)
            vAtt = V.n

            g_dma(wobuf, dw3(dr["iwo"][L]), woS,
                  wait=[(T, T.n), (SG, SG.n)])
            nwo2 = woS.n
            addg = [None, None]
            for oc in range(NC):
                for tb in range(4):
                    pp = psA if tb % 2 == 0 else psB
                    par = tb % 2
                    for c in range(NC):
                        if c == 0:
                            w = ([(woS, nwo2), (V, vAtt)]
                                 + ([(V, addg[par])] if addg[par] else []))
                            if oc == 0 and tb == 0:
                                w = BART(w)
                        else:
                            w = []
                        E("t", lambda t, oc=oc, tb=tb, c=c, pp=pp: t.matmul(
                            pp[:, :], wobuf[:, c, oc * 128:(oc + 1) * 128],
                            OF[:, c, tb * 512:(tb + 1) * 512],
                            start=(c == 0), stop=(c == NC - 1)),
                          wait=w, inc=(T if c == NC - 1 else None))
                    tWo = T.n
                    E("v", lambda v, oc=oc, tb=tb, pp=pp: v.tensor_add(
                        Y[:, oc, tb * 512:(tb + 1) * 512], pp[:, :],
                        Y[:, oc, tb * 512:(tb + 1) * 512]),
                      wait=[(T, tWo)], inc=V)
                    addg[par] = V.n
            ffn_tgate = T.n
            if stage == ("att", L):
                return

            vStf = emit_stats(Y, XA, True)
            if stage == ("fst", L):
                E("s", lambda s: s.activation(
                    Y[:, 0, 0:2], Y[:, 0, 0:2], AF.Identity),
                  wait=[(V, vStf)], inc=SG)
                return "fxn"
            sCSf = emit_row_cs(vStf, "plain")
            if stage == ("fs2", L):
                E("s", lambda s: s.activation(
                    Y[:, 0, 0:2], Y[:, 0, 0:2], AF.Identity),
                  wait=[(SG, sCSf)], inc=SG)
                return "fxn"
            vXN2, sXN2 = emit_normalize(
                sCSf, Y, XA, True, (cLGB[:, 0:8], cLGB[:, 8:16]))
            if stage == ("fxn", L):
                for c in range(NC):
                    E("s", lambda s, c=c: s.activation(
                        Y[:, c, :], XA[:, c, :], AF.Identity),
                      wait=[(SG, sXN2), (V, vXN2)], inc=SG)
                return "fxn"
            w1_last[0] = max(w1_last[0], ffn_tgate)
            w1_last[1] = max(w1_last[1], ffn_tgate)
            w2_last[0] = max(w2_last[0], ffn_tgate)
            w2_last[1] = max(w2_last[1], ffn_tgate)
            for th in range(2):
                ffn_half(L, th, vXN2, sXN2, nld)

        def inner_pair(L, p, sXAi, vAB):
            g_dma(wqp, dw3(dr["iwqkv"][L])[:, :, p * 384:(p + 1) * 384], wS0,
                  wait=[(T, iwq_last[0]), (V, iwq_vgate[0])])
            nw = wS0.n
            qev = [V.n, V.n]
            for tt in range(16):
                pp = psA if tt % 2 == 0 else psB
                for c in range(NC):
                    w = ((BART([(wS0, nw), (SG, sXAi)]) if tt == 0
                          else [(wS0, nw), (SG, sXAi), (V, qev[tt % 2])])
                         if c == 0 else [])
                    E("t", lambda t, tt=tt, c=c, pp=pp: t.matmul(
                        pp[:, 0:384], XA[:, c, tt * 128:(tt + 1) * 128],
                        wqp[:, c, :], start=(c == 0), stop=(c == NC - 1)),
                      wait=w, inc=(T if c == NC - 1 else None))
                tQ = T.n
                E("v", lambda v, tt=tt, pp=pp: v.tensor_copy(
                    qkvTM[:, tt, :], pp[:, 0:384]),
                  wait=[(T, tQ)], inc=V)
                qev[tt % 2] = V.n
            iwq_last[0] = T.n
            vQKV = V.n
            E("v", lambda v: v.tensor_mul(
                sqB[:, :, :], qkvTM[:, :, 0:256], qkvTM[:, :, 0:256]),
              wait=[(V, vQKV)])
            E("v", lambda v: v.tensor_reduce(
                cRN[:, :], sq64[:, :, :], axis=AX.X, op=ALU.add), inc=V)
            vNsq = V.n
            E("s", lambda s: s.activation(cRN[:, :], cRN[:, :], AF.Ln,
                                          bias=cPOOL[:, 3:4], scale=1.0),
              wait=[(V, vNsq)])
            E("s", lambda s: s.activation(cRN[:, :], cRN[:, :], AF.Exp,
                                          scale=cPOOL[:, 2:3]), inc=SG)
            E("v", lambda v: v.tensor_scalar_min(cRN[:, :], cRN[:, :], 1e12),
              wait=[(SG, SG.n)], inc=V)
            for qk in range(2):
                Atab = AQ if qk == 0 else AK
                Btab = BQ if qk == 0 else BK
                sub0 = 2 * qk
                E("v", lambda v, sub0=sub0: v.tensor_copy(
                    qrB[:, :, :, 0:32], qv6[:, :, sub0:sub0 + 2, 32:64]),
                  wait=[(V, V.n), (V, vAB)])
                E("v", lambda v, sub0=sub0: v.tensor_copy(
                    qrB[:, :, :, 32:64], qv6[:, :, sub0:sub0 + 2, 0:32]), inc=V)
                E("v", lambda v, qk=qk, Atab=Atab: v.tensor_tensor(
                    m14[:, :, :, :], qv6[:, :, 2 * qk:2 * qk + 2, :],
                    Atab[:, :, :].rearrange("p a (o c) -> p a o c", o=1)
                    .broadcast_to([128, 16, 2, 64]), ALU.mult),
                  wait=[(V, V.n)])
                E("v", lambda v, qk=qk, Btab=Btab: v.tensor_tensor(
                    q2k4[:, :, 2 * qk:2 * qk + 2, :], qrB[:, :, :, :],
                    Btab[:, :, :].rearrange("p a (o c) -> p a o c", o=1)
                    .broadcast_to([128, 16, 2, 64]), ALU.mult))
                E("v", lambda v, qk=qk: v.tensor_add(
                    q2k4[:, :, 2 * qk:2 * qk + 2, :],
                    q2k4[:, :, 2 * qk:2 * qk + 2, :], m14[:, :, :, :]), inc=V)
            vRope = V.n
            tTr_, vQF_ = {}, {}
            for k in range(64):
                tt, j = k // 4, k % 4
                qk, sub = (0 if j < 2 else 1), j % 2
                pr = k % 2
                w = [(V, vRope)]
                if k >= 2:
                    w.append((T, tTr_[k - 2]))
                E("v", lambda v, tt=tt, j=j, pr=pr: v.tensor_scalar_mul(
                    diagT2[:, pr, :], cEYE[:, :],
                    cRN[:, tt * 4 + j:tt * 4 + j + 1]),
                  wait=w, inc=V)
                vD = V.n
                wt = [(V, vD)]
                if j == 0 and tt >= 1:
                    wt.append((V, vQF_[tt - 1]))
                E("t", lambda t, tt=tt, qk=qk, sub=sub, pr=pr: t.matmul(
                    psT[0:64, qk * 128:qk * 128 + 128] if sub == 0
                    else psT[64:128, qk * 128:qk * 128 + 128],
                    q2k4[:, tt, 2 * qk + sub, :], diagT2[:, pr, :],
                    start=True, stop=True,
                    tile_position=((0, 64) if sub == 1 else None)),
                  wait=(BART(wt) if k == 0 else wt), inc=T)
                tTr_[k] = T.n
                if j == 3:
                    E("v", lambda v, tt=tt: v.tensor_copy(
                        qkFM[:, 0, tt * 128:(tt + 1) * 128], psT[:, 0:128]),
                      wait=[(T, tTr_[k])])
                    E("v", lambda v, tt=tt: v.tensor_copy(
                        qkFM[:, 1, tt * 128:(tt + 1) * 128], psT[:, 128:256]),
                      inc=V)
                    vQF_[tt] = V.n
            vFM = V.n
            units = [(h01, qt) for h01 in range(2) for qt in range(16)]
            NU = len(units)
            tS_, tPT_, tPV_ = {}, {}, {}
            sE1_, sRC_, sE2_ = {}, {}, {}
            vRS_, vPT_, vOF_ = {}, {}, {}

            def att_S(i):
                h01, qt = units[i]
                b0, b1 = 64 * h01, 64 * h01 + 64
                sl = (i % 4) * 512
                klo = max(0, qt - 2) * 128
                n0 = qt * 128 - klo
                nk = n0 + 128
                w = [(V, vFM)]
                if i >= 4:
                    w.append((V, vOF_[i - 4]))
                E("t", lambda t, qt=qt, sl=sl, b0=b0, b1=b1, klo=klo, nk=nk:
                  t.matmul(psS[:, sl:sl + nk],
                           qkFM[b0:b1, 0, qt * 128:(qt + 1) * 128],
                           qkFM[b0:b1, 1, klo:klo + nk],
                           start=True, stop=False),
                  wait=(BART(w) if i == 0 else w))
                if qt >= 2:
                    E("t", lambda t, sl=sl: t.matmul(
                        psS[:, sl:sl + 128], cEYE[:, :], cTRIG[:, :],
                        start=False, stop=False))
                E("t", lambda t, sl=sl, n0=n0: t.matmul(
                    psS[:, sl + n0:sl + n0 + 128], cEYE[:, :], cTRIC[:, :],
                    start=False, stop=True), inc=T)
                tS_[i] = T.n

            def att_soft(i):
                h01, qt = units[i]
                head = 2 * p + h01
                sl = (i % 4) * 512
                par = i % 2
                klo = max(0, qt - 2) * 128
                nk = qt * 128 - klo + 128
                w = [(T, tS_[i])]
                if i >= 2:
                    w.append((V, vRS_[i - 2]))
                E("s", lambda s, sl=sl, nk=nk, par=par: s.activation(
                    pIn[:, par, 0:nk], psS[:, sl:sl + nk], AF.Exp),
                  wait=w, inc=SG)
                sE1_[i] = SG.n
                w2 = [(SG, sE1_[i])]
                if i >= 2:
                    w2.append((SG, sRC_[i - 2]))
                E("v", lambda v, nk=nk, par=par: v.reduce_sum(
                    sxL2[:, par, 0:1], pIn[:, par, 0:nk], axis=AX.X),
                  wait=w2, inc=V)
                vRS_[i] = V.n
                E("s", lambda s, par=par: s.activation(
                    sxLL[:, par, 0:1], sxL2[:, par, 0:1], AF.Ln),
                  wait=[(V, vRS_[i])])
                E("s", lambda s, par=par: s.activation(
                    sxRC[:, par, 0:1], sxLL[:, par, 0:1], AF.Exp,
                    scale=cNEG1[:, :]), inc=SG)
                sRC_[i] = SG.n
                E("s", lambda s, nk=nk, par=par: s.activation(
                    pNr[:, par, 0:nk], pIn[:, par, 0:nk], AF.Identity,
                    scale=sxRC[:, par, 0:1]),
                  wait=([(T, tPT_[i - 2])] if i >= 2 else []), inc=SG)
                sE2_[i] = SG.n
                pt = psT if par == 0 else psC
                nkc = nk // 128
                for ki in range(nkc):
                    E("t", lambda t, ki=ki, par=par, pt=pt: t.matmul(
                        pt[:, ki * 128:(ki + 1) * 128],
                        pNr[:, par, ki * 128:(ki + 1) * 128],
                        cEYE[:, :], start=True, stop=True),
                      wait=(([(SG, sE2_[i])]
                             + ([(V, vPT_[i - 2])] if i >= 2 else []))
                            if ki == 0 else []),
                      inc=(T if ki == nkc - 1 else None))
                tPT_[i] = T.n
                E("v", lambda v, nk=nk, par=par, pt=pt: v.tensor_copy(
                    ptIn[:, par, 0:nk], pt[:, 0:nk]),
                  wait=([(T, tPT_[i])]
                        + ([(T, tPV_[i - 2])] if i >= 2 else [])), inc=V)
                vPT_[i] = V.n
                for ki in range(nkc):
                    kc = klo // 128 + ki
                    E("t", lambda t, ki=ki, kc=kc, h01=h01, sl=sl, par=par,
                      st=(ki == 0), sp=(ki == nkc - 1): t.matmul(
                        psS[0:64, sl:sl + 128] if h01 == 0
                        else psS[64:128, sl:sl + 128],
                        qv6[:, kc, 4 + h01, :],
                        ptIn[:, par, ki * 128:(ki + 1) * 128],
                        start=st, stop=sp,
                        tile_position=((0, 64) if h01 == 1 else None)),
                      wait=([(V, vPT_[i])] if ki == 0 else []),
                      inc=(T if ki == nkc - 1 else None))
                tPV_[i] = T.n
                ch, po = (head * 64) // 128, (head * 64) % 128
                E("v", lambda v, ch=ch, po=po, qt=qt, h01=h01, sl=sl:
                  v.tensor_copy(
                    OF[po:po + 64, ch, qt * 128:(qt + 1) * 128],
                    psS[0:64, sl:sl + 128] if h01 == 0
                    else psS[64:128, sl:sl + 128]),
                  wait=[(T, tPV_[i])], inc=V)
                vOF_[i] = V.n

            att_S(0)
            att_S(1)
            for i in range(NU):
                att_soft(i)
                if i + 2 < NU:
                    att_S(i + 2)

        def ffn_half(L, th, vXN2, sXN2, nld):
            t0 = th * 1024
            gelg = [None, None]
            sttg = [None, None]
            for jp in range(NJ):
                wa = JW[jp]
                par = jp % 2
                wbuf = w1b[par]
                wctr = w1S0 if par == 0 else w1S1
                g_dma(wbuf[:, :, 0:2 * wa],
                      dw3(dr["iw1"][L])[:, :, jp * 256:jp * 256 + 2 * wa],
                      wctr, wait=[(T, w1_last[par])])
                nw = wctr.n
                for tb in range(2):
                    pu = tb
                    gb = psA if pu == 0 else psC
                    ab = psB if pu == 0 else psT
                    lo = t0 + tb * 512
                    hb = tb * 512
                    for c in range(NC):
                        if c == 0:
                            w = [(wctr, nw), (SG, sXN2), (V, vXN2)]
                            if gelg[pu]:
                                w.append((SG, gelg[pu]))
                            if jp == 0 and tb == 0:
                                w = BART(w)
                        else:
                            w = []
                        E("t", lambda t, c=c, wa=wa, wbuf=wbuf, lo=lo, gb=gb:
                          t.matmul(
                            gb[0:wa, :], wbuf[:, c, wa:2 * wa],
                            XA[:, c, lo:lo + 512],
                            start=(c == 0), stop=(c == NC - 1)),
                          wait=w, inc=(T if c == NC - 1 else None))
                    tG = T.n
                    E("s", lambda s, wa=wa, jp=jp, pu=pu, gb=gb: s.activation(
                        gelB2[0:wa, pu, :], gb[0:wa, :], AF.Gelu,
                        bias=cB1[0:wa, 2 * jp + 1:2 * jp + 2]),
                      wait=([(T, tG), (ldS, nld)]
                            + [(V, sttg[pu]) if sttg[pu] else (V, V.n)]),
                      inc=SG)
                    gelg[pu] = SG.n
                    sGl = SG.n
                    for c in range(NC):
                        w = (([(wctr, nw)]
                              + [(V, sttg[pu]) if sttg[pu] else (V, V.n)])
                             if c == 0 else [])
                        E("t", lambda t, c=c, wa=wa, wbuf=wbuf, lo=lo, ab=ab:
                          t.matmul(
                            ab[0:wa, :], wbuf[:, c, 0:wa],
                            XA[:, c, lo:lo + 512],
                            start=(c == 0), stop=(c == NC - 1)),
                          wait=w, inc=(T if c == NC - 1 else None))
                    tA = T.n
                    E("v", lambda v, wa=wa, jp=jp, hb=hb, pu=pu, ab=ab:
                      v.scalar_tensor_tensor(
                        H_ap(jp, hb, 512)[0:wa, :], ab[0:wa, :],
                        cB1[0:wa, 2 * jp:2 * jp + 1], gelB2[0:wa, pu, :],
                        op0=ALU.add, op1=ALU.mult),
                      wait=[(T, tA), (SG, sGl)], inc=V)
                    sttg[pu] = V.n
                w1_last[par] = T.n
            vH = V.n
            for tb in range(2):
                lo = t0 + tb * 512
                hb = tb * 512
                for jp in range(NJ):
                    wa = JW[jp]
                    par = jp % 2
                    wbuf = w2b[par]
                    wctr = w2S0 if par == 0 else w2S1
                    g_dma(wbuf[0:wa, :], dr["iw2"][L][jp * 128:jp * 128 + wa, :],
                          wctr, wait=[(T, w2_last[par])])
                    nw2 = wctr.n
                    for oc in range(8):
                        E("t", lambda t, jp=jp, wa=wa, oc=oc, wbuf=wbuf, hb=hb:
                          t.matmul(
                            ps_oc(oc), wbuf[0:wa, oc * 128:(oc + 1) * 128],
                            H_ap(jp, hb, 512)[0:wa, :],
                            start=(jp == 0), stop=(jp == NJ - 1)),
                          wait=(BART([(wctr, nw2), (V, vH)])
                                if jp == 0 and oc == 0
                                else [(wctr, nw2)]),
                          inc=(T if oc == 7 else None))
                    w2_last[par] = T.n
                tW2 = T.n
                for oc in range(8):
                    E("v", lambda v, oc=oc, lo=lo: v.scalar_tensor_tensor(
                        Y[:, oc, lo:lo + 512], ps_oc(oc), cB2[:, oc:oc + 1],
                        Y[:, oc, lo:lo + 512], op0=ALU.add, op1=ALU.add),
                      wait=[(T, tW2), (ldS, nld)], inc=V)

        for L in range(n_layers):
            r = inner_layer(L)
            iwq_last[0] = T.n
            iwq_vgate[0] = V.n
            w1_last[0] = w1_last[1] = T.n
            w2_last[0] = w2_last[1] = T.n
            if r == "fxn":
                _finish(mk, E, o_fm, Y, stO, [(V, V.n), (T, T.n), (SG, SG.n)])
                _replay(mk, nc)
                return nc
            if stage == ("att", L) or stage == ("ffn", L):
                _finish(mk, E, o_fm, Y, stO, [(V, V.n), (T, T.n), (SG, SG.n)])
                _replay(mk, nc)
                return nc

        rl2_last = [0, 0]
        for c in range(NC):
            rb = c % 2
            rctr = rlS0 if rb == 0 else rlS1
            y_dma(x1rl[:, rb, :],
                  spX1.rearrange("(c p) t -> p c t", p=128)[:, c, :],
                  rctr, wait=[(spX1S, spX1S.n), (V, rl2_last[rb]),
                              (T, T.n), (SG, SG.n)])
            nrl = rctr.n
            E("v", lambda v, c=c, rb=rb: v.tensor_add(
                Y[:, c, :], Y[:, c, :], x1rl[:, rb, :]),
              wait=[(rctr, nrl)], inc=V)
            rl2_last[rb] = V.n
        _finish(mk, E, o_fm, Y, stO, [(V, V.n)])
        _replay(mk, nc)
    return nc


def _finish(mk, E, o_fm, src, stO, wait):
    E("y", lambda y: y.dma_start(
        o_fm.rearrange("(c p) t -> p c t", p=128), src[:, :, :]),
      wait=wait, inc=stO, inck=16)
    E("y", lambda y: None, wait=[(stO, 16)])


def _replay(mk, nc):
    with nc.Block() as block:
        for eng, reg in (("g", block.gpsimd), ("y", block.sync),
                         ("t", block.tensor), ("v", block.vector),
                         ("s", block.scalar)):
            ops = list(mk.q[eng])

            def body(e, ops=ops):
                for fn in ops:
                    fn(e)
            reg(body)


def kernel(**inputs):
    shared = _prep_shared(inputs)
    x = np.asarray(inputs["L0_attn_x"], dtype=np.float32)
    nc = build_program()
    in_maps = []
    for b in range(8):
        m = dict(shared)
        m["x_fm"] = np.ascontiguousarray(x[b].T)
        in_maps.append(m)
    res = run_bass_kernel_spmd(nc, in_maps, core_ids=list(range(8)))
    out = np.stack([np.ascontiguousarray(np.asarray(r["o_fm"]).T)
                    for r in res.results])
    return out.astype(np.float32)


# version 05
# speedup vs baseline: 1.3930x; 1.0211x over previous
import numpy as np
import concourse.bass as bass
import concourse.mybir as mybir
from concourse.bass_utils import run_bass_kernel_spmd
from contextlib import ExitStack

F32 = mybir.dt.float32
BF16 = mybir.dt.bfloat16
U8 = mybir.dt.uint8
AF = mybir.ActivationFunctionType
ALU = mybir.AluOpType
AX = mybir.AxisListType

D = 1024
S = 2048
NC = 8
NH = 16
HD = 64
NP = 8
DEPTH = 6
DI = 2730
NJ = 22
JW = [128] * 21 + [42]
NEG = -1e30


def _prep_shared(inputs):
    g = lambda k: np.ascontiguousarray(np.asarray(inputs[k], dtype=np.float32))
    d = {}
    wqkvT = g("L0_attn_w_qkv").T
    bqkv = g("L0_attn_b_qkv")
    idx = []
    for p in range(NP):
        idx += list(range(2 * p * HD, 2 * p * HD + 2 * HD))
        idx += list(range(D + 2 * p * HD, D + 2 * p * HD + 2 * HD))
    idx += list(range(2 * D, 3 * D))
    d["owqkv"] = np.ascontiguousarray(wqkvT[:, idx])
    d["obqkv"] = np.ascontiguousarray(bqkv[idx])
    d["owout"] = np.ascontiguousarray(g("L0_attn_w_out").T)
    d["obout"] = g("L0_attn_b_out")
    d["owent"] = np.ascontiguousarray(g("L0_attn_w_ent").T)
    d["obent"] = g("L0_attn_b_ent").reshape(1, 1)
    d["olng"] = g("L0_attn_ln_g")
    d["olnb"] = g("L0_attn_ln_b")
    Wq = g("L0_ffn_Wqkv")
    idxi = []
    for p in range(NP):
        idxi += list(range(p * 128, (p + 1) * 128))
        idxi += list(range(D + p * 128, D + (p + 1) * 128))
        idxi += list(range(2 * D + p * 128, 2 * D + (p + 1) * 128))
    d["iwqkv"] = np.ascontiguousarray(Wq[:, :, idxi])
    d["iwo"] = g("L0_ffn_Wo")
    W1 = g("L0_ffn_W1")
    b1 = g("L0_ffn_b1")
    idx1 = []
    for j in range(NJ):
        w = JW[j]
        idx1 += list(range(j * 128, j * 128 + w))
        idx1 += list(range(DI + j * 128, DI + j * 128 + w))
    d["iw1"] = np.ascontiguousarray(W1[:, :, idx1])
    d["ib1"] = np.ascontiguousarray(b1[:, idx1])
    d["iw2"] = g("L0_ffn_W2")
    d["ib2"] = g("L0_ffn_b2")
    d["ilng"] = g("L0_ffn_ln_g")
    d["ilnb"] = g("L0_ffn_ln_b")
    d["iqs"] = g("L0_ffn_q_scale")
    d["iks"] = g("L0_ffn_k_scale")
    d["eye"] = np.eye(128, dtype=np.float32)
    e4 = np.zeros((4, 4, 128), dtype=np.float32)
    for j in range(4):
        e4[j, j, :] = 1.0
    d["e4"] = e4
    u = np.arange(128)[:, None]
    v = np.arange(128)[None, :]
    d["tri_c"] = np.where(v <= u, 0.0, NEG).astype(np.float32)
    d["tri_g"] = np.where(v >= u, 0.0, NEG).astype(np.float32)
    half = np.arange(0, HD, 2, dtype=np.float64)
    inv_freq = 1.0 / 10000.0 ** (half / HD)
    t = np.arange(S, dtype=np.float64)
    fr = t[:, None] * inv_freq[None, :]
    fr = np.concatenate([fr, fr], -1)
    bs = (half + 0.4 * HD) / (1.4 * HD)
    pw = (t - S // 2) / (256 // 2)
    sc = bs[None, :] ** pw[:, None]
    sc = np.concatenate([sc, sc], -1)
    tm = lambda a: np.ascontiguousarray(
        a.reshape(16, 128, HD).transpose(1, 0, 2).astype(np.float32))
    d["cos8"] = tm(np.cos(fr) * sc * 8.0)
    d["sin8"] = tm(np.sin(fr) * sc * 8.0)
    d["cosi"] = tm(np.cos(fr) / sc)
    d["sini"] = tm(np.sin(fr) / sc)
    return d


class Ctr:
    __slots__ = ("sem", "n", "owner")

    def __init__(self, sem, owner=None):
        self.sem = sem
        self.n = 0
        self.owner = owner


class MK:
    ENGS = ("g", "y", "t", "v", "s")

    def __init__(self, nc):
        self.nc = nc
        self.q = {e: [] for e in self.ENGS}
        self.waited = {e: {} for e in self.ENGS}

    def op(self, eng, fn, wait=(), inc=None, inck=1):
        todo = []
        for (c, val) in wait:
            if val <= 0 or c.owner == eng:
                continue
            assert val <= c.n, f"wait on future {val} > {c.n} of {c.sem.name}"
            if self.waited[eng].get(c, -1) < val:
                self.waited[eng][c] = val
                todo.append((c, val))
        if inc is not None:
            inc.n += inck
        sem = inc.sem if inc is not None else None

        def run(e, todo=todo, fn=fn, sem=sem, inck=inck):
            for (c, val) in todo:
                e.wait_ge(c.sem, val)
            r = fn(e)
            if sem is not None:
                if r is None:
                    r = e.drain()
                r.then_inc(sem, inck)
        self.q[eng].append(run)


def view(region, off, dt, shape):
    esz = {F32: 4, BF16: 2, U8: 1}[dt]
    n = 1
    for s in shape[1:]:
        n *= s
    assert off % esz == 0
    a = region.ap().bitcast(dt)[:, off // esz: off // esz + n]
    if len(shape) == 3:
        a = a.rearrange("p (a b) -> p a b", a=shape[1])
    elif len(shape) == 4:
        a = a.rearrange("p (a b c) -> p a b c", a=shape[1], b=shape[2])
    if shape[0] != 128:
        a = a[0: shape[0]]
    return a


def build_program(n_layers=DEPTH, stage="final"):
    nc = bass.Bass("TRN2", target_bir_lowering=False, detect_race_conditions=False)
    dr = {}

    def din(name, shape):
        dr[name] = nc.dram_tensor(name, shape, F32, kind="ExternalInput").ap()

    din("x_fm", [D, S])
    din("owqkv", [D, 3 * D]); din("obqkv", [3 * D]); din("owout", [D, D])
    din("obout", [D]); din("owent", [D, 1]); din("obent", [1, 1])
    din("olng", [D]); din("olnb", [D])
    din("iwqkv", [DEPTH, D, 3 * D]); din("iwo", [DEPTH, D, D])
    din("iw1", [DEPTH, D, 2 * DI]); din("ib1", [DEPTH, 2 * DI])
    din("iw2", [DEPTH, DI, D]); din("ib2", [DEPTH, D])
    din("ilng", [DEPTH, D]); din("ilnb", [DEPTH, D])
    din("iqs", [DEPTH, HD]); din("iks", [DEPTH, HD])
    din("eye", [128, 128]); din("tri_c", [128, 128]); din("tri_g", [128, 128])
    din("e4", [4, 4, 128])
    din("cos8", [128, 16, HD]); din("sin8", [128, 16, HD])
    din("cosi", [128, 16, HD]); din("sini", [128, 16, HD])
    o_fm = nc.dram_tensor("o_fm", [D, S], F32, kind="ExternalOutput").ap()
    spX = nc.dram_tensor("spX", [D, S], F32, kind="Internal").ap()
    spX1 = nc.dram_tensor("spX1", [D, S], F32, kind="Internal").ap()
    spGate = nc.dram_tensor("spGate", [S], F32, kind="Internal").ap()
    d3 = lambda name: dr[name].rearrange("(c p) t -> p c t", p=128)
    dw3 = lambda ap2d: ap2d.rearrange("(c p) n -> p c n", p=128)
    dcol = lambda ap1d: ap1d.rearrange("(c p) -> p c", p=128)
    drow = lambda ap1d: ap1d.rearrange("(o n) -> o n", o=1)

    with ExitStack() as ctx:
        sbuf = lambda n, sh, dt: ctx.enter_context(nc.sbuf_tensor(n, sh, dt))
        psum = lambda n, sh: ctx.enter_context(nc.psum_tensor(n, sh, F32))

        def mksem(n, owner=None):
            return Ctr(ctx.enter_context(nc.semaphore(n)), owner)

        RY = sbuf("RY", [128, 65536], U8)
        RXA = sbuf("RXA", [128, 32768], U8)
        ROF = sbuf("ROF", [128, 32768], U8)
        RSC = sbuf("RSC", [128, 49152], U8)
        RCN = sbuf("RCN", [128, 32032], U8)

        psS = psum("psS", [128, 2048])
        psSB = psum("psSB", [128, 2048])
        psA = psSB[:, 0:512]
        psB = psSB[:, 512:1024]
        psC = psSB[:, 1024:1536]
        psT = psSB[:, 1536:2048]

        T = mksem("T", "t"); V = mksem("V", "v"); SG = mksem("SG", "s")
        cDg = mksem("cDg"); cDs = mksem("cDs"); xD = mksem("xD")
        wS0 = mksem("wS0"); wS1 = mksem("wS1")
        w1S0 = mksem("w1S0"); w1S1 = mksem("w1S1")
        w2S0 = mksem("w2S0"); w2S1 = mksem("w2S1")
        woS = mksem("woS"); wvS = mksem("wvS")
        spXS = mksem("spXS"); spX1S = mksem("spX1S")
        rlS0 = mksem("rlS0"); rlS1 = mksem("rlS1")
        gcS = mksem("gcS"); ldS = mksem("ldS")
        stO = mksem("stO")

        mk = MK(nc)
        E = mk.op

        def BART(extra=()):
            return [(V, V.n), (SG, SG.n)] + list(extra)

        Y = view(RY, 0, F32, [128, NC, S])
        XA = view(RXA, 0, BF16, [128, NC, S])
        OF = view(ROF, 0, BF16, [128, NC, S])
        cEYE = view(RCN, 0, BF16, [128, 128])
        cTRIC = view(RCN, 256, BF16, [128, 128])
        cTRIG = view(RCN, 512, BF16, [128, 128])
        cC8 = view(RCN, 768, BF16, [128, 16, HD])
        cS8 = view(RCN, 2816, BF16, [128, 16, HD])
        cCI = view(RCN, 4864, BF16, [128, 16, HD])
        cSI = view(RCN, 6912, BF16, [128, 16, HD])
        cE8 = view(RCN, 8960, BF16, [128, 8, 4])
        cE4 = view(RCN, 9088, F32, [128, 4, 128])
        cONE1 = view(RCN, 11136, F32, [1, 128])
        cPOOL = view(RCN, 11648, F32, [128, 8])
        cBQK = view(RCN, 11680, F32, [128, 16])
        cGOUT = view(RCN, 11744, F32, [128, 8])
        cBOUT = view(RCN, 11776, F32, [128, 8])
        cBO01 = view(RCN, 11808, F32, [128, 8])
        cWENT = view(RCN, 11840, BF16, [128, 8])
        cBENT = view(RCN, 11856, F32, [1, 1])
        cGCOL = view(RCN, 11872, F32, [128, 16])
        cSTAT = view(RCN, 11936, F32, [4, 512])
        cST2 = view(RCN, 29936, F32, [4, 512])
        cNEG1 = view(RCN, 31984, F32, [128, 1])
        cMU = view(RCN, 13984, F32, [4, 512])
        cCS = view(RCN, 16032, F32, [4, 512])
        cROW = view(RCN, 18080, F32, [4, 512])
        cQSR = view(RCN, 20128, F32, [1, 256])
        cLGB = view(RCN, 21152, F32, [128, 16])
        cB1 = view(RCN, 21216, F32, [128, 44])
        cB2 = view(RCN, 21392, F32, [128, 8])
        cELL = view(RCN, 21424, F32, [128, 1])
        cELL2 = view(RCN, 21472, F32, [128, 1])
        cNM = view(RCN, 21440, F32, [128, 1])
        cRC = view(RCN, 21456, F32, [128, 1])
        cRN = view(RCN, 21488, F32, [128, 64])
        AQ = view(RCN, 21744, BF16, [128, 16, HD])
        BQ = view(RCN, 23792, BF16, [128, 16, HD])
        AK = view(RCN, 25840, BF16, [128, 16, HD])
        BK = view(RCN, 27888, BF16, [128, 16, HD])
        gRow = view(RCN, 11936, F32, [1, 2048])
        qsRep = view(RCN, 11936, BF16, [128, 4, 64])

        xsq = view(RSC, 25600, BF16, [128, S])
        tscr = view(RSC, 29696, F32, [128, 512])
        bvR = view(RSC, 0, F32, [128, D])
        tvs = view(RSC, 4096, F32, [128, 512])
        bvRow = view(RSC, 41264, F32, [1, D])
        vTM = view(RY, 0, BF16, [128, 16, D])
        wvSB = view(RY, 32768, BF16, [128, NC, D])
        qpSB = view(RSC, 0, BF16, [128, S])
        kpSB = view(RSC, 4096, BF16, [128, S])
        pSBp = view(RSC, 8192, BF16, [128, 2, S])
        ptSBp = view(RSC, 16384, BF16, [128, 2, S])
        wqhSB = [view(RSC, 24576, BF16, [128, NC, 256]),
                 view(RSC, 28672, BF16, [128, NC, 256])]
        pSB2p = view(RSC, 32768, BF16, [128, 2, S])
        oMX = view(RSC, 40960, F32, [128, 2, 4, 4])
        oNM = view(RSC, 41088, F32, [128, 2, 4])
        oLL = view(RSC, 41120, F32, [128, 2, 4])
        oRC = view(RSC, 41152, F32, [128, 2, 4])
        oEL = view(RSC, 41184, F32, [128, 2, 4])
        oJK = view(RSC, 41216, F32, [128, 4])
        oTa = view(RSC, 41232, F32, [128, 1])
        oTb = view(RSC, 41248, F32, [128, 1])
        woSBo = view(RSC, 16384, BF16, [128, NC, D])
        xrl = view(RSC, 8192, F32, [128, 2, 512])
        twx = view(RSC, 4096, F32, [128, 512])
        x1v = Y
        wqp = view(RSC, 0, BF16, [128, NC, 384])
        qkvTM = view(RSC, 6144, BF16, [128, 16, 384])
        qv6 = view(RSC, 6144, BF16, [128, 16, 6, HD])
        q2k2 = view(RSC, 18432, BF16, [128, 16, 256])
        q2k4 = view(RSC, 18432, BF16, [128, 16, 4, HD])
        qkFM = view(RSC, 26624, BF16, [128, 2, S])
        sqB = view(RSC, 34816, BF16, [128, 16, 256])
        sq64 = view(RSC, 34816, BF16, [128, 64, 64])
        qrB = view(RSC, 34816, BF16, [128, 16, 2, HD])
        m1B = view(RSC, 38912, BF16, [128, 16, 128])
        m14 = view(RSC, 38912, BF16, [128, 16, 2, HD])
        pIn = view(RSC, 43008, BF16, [128, 2, 384])
        ptIn = view(RSC, 44544, BF16, [128, 2, 384])
        diagI = view(RSC, 46080, BF16, [128, 128])
        diagT2 = view(RSC, 46080, BF16, [128, 2, 128])
        sxNM = view(RSC, 46592, F32, [128, 2, 4])
        sxL2 = view(RSC, 46624, F32, [128, 2, 4])
        sxLL = view(RSC, 46656, F32, [128, 2, 4])
        sxRC = view(RSC, 46688, F32, [128, 2, 4])
        pNr = view(RSC, 47168, BF16, [128, 2, 384])
        wobuf = view(RXA, 0, BF16, [128, NC, D])
        w1b = [view(RSC, 12288, BF16, [128, NC, 256]),
               view(RSC, 16384, BF16, [128, NC, 256])]
        w2b = [view(RSC, 20480, BF16, [128, D]),
               view(RSC, 22528, BF16, [128, D])]
        gelB2 = view(RSC, 24576, BF16, [128, 2, 512])
        Hlo = view(ROF, 0, BF16, [128, 16, 1024])
        Hhi = view(RSC, 0, BF16, [128, 6, 1024])
        x1rl = view(RSC, 0, F32, [128, 2, S])

        def H_ap(j, lo, w):
            return (Hlo[:, j, lo:lo + w] if j < 16
                    else Hhi[:, j - 16, lo:lo + w])

        def ps_oc(oc):
            if oc == 0:
                return psA[:, :]
            if oc == 1:
                return psB[:, :]
            if oc == 2:
                return psC[:, :]
            if oc < 7:
                return psS[:, (oc - 3) * 512:(oc - 2) * 512]
            return psT[:, :]

        def g_dma(dst, src, ctr, wait=(), slow=False):
            def fn(g, d2=dst, s2=src, sl=slow):
                if sl:
                    return g.dma_start(d2, s2, allow_slow_non_contiguous=True)
                return g.dma_start(d2, s2)
            E("g", fn, wait=wait, inc=ctr, inck=16)

        def y_dma(dst, src, ctr, wait=(), slow=False):
            def fn(y, d2=dst, s2=src, sl=slow):
                if sl:
                    return y.dma_start(d2, s2, allow_slow_non_contiguous=True)
                return y.dma_start(d2, s2)
            E("y", fn, wait=wait, inc=ctr, inck=16)

        y_dma(Y[:, :, :], d3("x_fm"), xD)
        g_dma(cEYE, dr["eye"], cDg)
        g_dma(cTRIC, dr["tri_c"], cDg)
        g_dma(cTRIG, dr["tri_g"], cDg)
        g_dma(cWENT, dr["owent"].rearrange("(c p) o -> p (c o)", p=128), cDg,
              slow=True)
        g_dma(cC8, dr["cos8"], cDg)
        g_dma(cS8, dr["sin8"], cDg)
        g_dma(cCI, dr["cosi"], cDg)
        g_dma(cSI, dr["sini"], cDg)
        NCDG = cDg.n
        y_dma(cBQK, dcol(dr["obqkv"][0:2048]), cDs, slow=True)
        y_dma(cGOUT, dcol(dr["olng"]), cDs, slow=True)
        y_dma(cBOUT, dcol(dr["olnb"]), cDs, slow=True)
        y_dma(cBO01, dcol(dr["obout"]), cDs, slow=True)
        y_dma(cBENT, dr["obent"], cDs)
        y_dma(cE4[0:4, :, :], dr["e4"], cDs)
        NCDS = cDs.n

        def init_consts(v):
            v.memset(cE8[:, :, :], 0.0)
            for j in range(4):
                v.memset(cE8[:, j, j:j + 1], 1.0)
            v.memset(cONE1[:, :], 1.0)
            v.memset(cPOOL[:, 0:1], 1.25)
            v.memset(cPOOL[:, 1:2], 0.1)
            v.memset(cPOOL[:, 2:3], -0.5)
            v.memset(cPOOL[:, 3:4], 1e-38)
            v.memset(cPOOL[:, 4:5], 1.0 + 1e-6)
            v.memset(cPOOL[:, 5:6], 1e-11)
            v.memset(cPOOL[:, 6:7], 1e-5)
            v.memset(cPOOL[:, 7:8], 0.0)
            return v.memset(cNEG1[:, :], -1.0)
        E("v", init_consts, inc=V)
        V_CONST = V.n
        E("v", lambda v: v.tensor_scalar_mul(cBQK[:, 0:16:2], cBQK[:, 0:16:2], 1.25),
          wait=[(cDs, NCDS)], inc=V)
        E("v", lambda v: v.tensor_scalar_mul(cBO01[:, :], cBO01[:, :], 0.1), inc=V)

        def emit_stats(src_f32, src_bf, shifted):
            pre = BART([(xD, 16), (cDg, NCDG), (T, T.n)])
            for c in range(NC):
                if shifted and c >= 4:
                    E("s", lambda s, c=c: s.memzero(src_bf[:, c, 0:2]), wait=pre)
                    E("s", lambda s, c=c: s.activation(
                        src_bf[:, c, 1:S], src_f32[:, c, 0:S - 1], AF.Identity),
                      inc=SG)
                else:
                    E("s", lambda s, c=c: s.activation(
                        src_bf[:, c, :], src_f32[:, c, :], AF.Identity),
                      wait=pre, inc=SG)
            sCAST = SG.n
            firstC = True
            firstB = True
            for c in range(NC):
                E("v", lambda v, c=c: v.tensor_mul(
                    xsq[:, :], src_bf[:, c, :], src_bf[:, c, :]),
                  wait=[(SG, sCAST), (V, V_CONST), (T, T.n)], inc=V)
                vSq = V.n
                for j in range(4):
                    last = (c == NC - 1) and (j == 3)
                    E("t", lambda t, c=c, j=j, st=firstC, last=last: t.matmul(
                        psC[0:4, :], cE8[:, j, :],
                        src_bf[:, c, j * 512:(j + 1) * 512],
                        start=st, stop=last),
                      wait=BART([(SG, sCAST)]) if firstC else [])
                    firstC = False
                for j in range(4):
                    last = (c == NC - 1) and (j == 3)
                    E("t", lambda t, j=j, c=c, st=firstB, last=last: t.matmul(
                        psB[0:4, :], cE8[:, j, :],
                        xsq[:, j * 512:(j + 1) * 512],
                        start=st, stop=last),
                      wait=(BART([(V, vSq)]) if firstB else [(V, vSq)]),
                      inc=T)
                    firstB = False
            tSt = T.n
            E("v", lambda v: v.tensor_copy(cSTAT[:, :], psC[0:4, :]),
              wait=[(T, tSt)])
            E("v", lambda v: v.tensor_copy(cST2[:, :], psB[0:4, :]),
              wait=[(T, tSt)], inc=V)
            return V.n

        def emit_row_cs(vStats, mode):
            E("v", lambda v: v.tensor_scalar_mul(cMU[:, :], cSTAT[:, :], 1.0 / D),
              wait=[(V, vStats)])
            E("v", lambda v: v.tensor_scalar_mul(cROW[:, :], cST2[:, :], 1.0 / D))
            E("v", lambda v: v.tensor_mul(cCS[:, :], cMU[:, :], cMU[:, :]))
            E("v", lambda v: v.tensor_sub(cROW[:, :], cROW[:, :], cCS[:, :]), inc=V)
            vVar = V.n
            if mode == "outer":
                E("s", lambda s: s.activation(cROW[:, :], cROW[:, :], AF.Ln,
                                              bias=cPOOL[0:4, 5:6],
                                              scale=cPOOL[0:4, 4:5]),
                  wait=[(V, vVar)])
            else:
                E("s", lambda s: s.activation(cROW[:, :], cROW[:, :], AF.Ln,
                                              bias=cPOOL[0:4, 6:7], scale=1.0),
                  wait=[(V, vVar)])
            E("s", lambda s: s.activation(cCS[:, :], cROW[:, :], AF.Exp,
                                          scale=cPOOL[0:4, 2:3]), inc=SG)
            return SG.n

        def emit_normalize(sCS, src_f32, dst, shifted, affine, out_f32=None):
            zcol = cPOOL[:, 7:8]
            for blk in range(4):
                E("t", lambda t, blk=blk: t.matmul(
                    psA[:, :], cE4[0:4, blk, :], cMU[:, :], start=True, stop=True),
                  wait=BART([(SG, sCS), (V, V_CONST)]))
                E("t", lambda t, blk=blk: t.matmul(
                    psB[:, :], cE4[0:4, blk, :], cCS[:, :], start=True, stop=True),
                  inc=T)
                tBC = T.n
                for c in range(NC):
                    lo = blk * 512
                    sh = shifted and c >= 4

                    def sub(v, c=c, lo=lo, sh=sh):
                        if not sh:
                            return v.tensor_sub(
                                tscr[:, :], src_f32[:, c, lo:lo + 512], psA[:, :])
                        if lo == 0:
                            v.tensor_sub(tscr[:, 0:1], zcol, psA[:, 0:1])
                            return v.tensor_sub(
                                tscr[:, 1:512], src_f32[:, c, 0:511], psA[:, 1:512])
                        return v.tensor_sub(
                            tscr[:, :], src_f32[:, c, lo - 1:lo + 511], psA[:, :])
                    E("v", sub, wait=[(T, tBC), (SG, SG.n)])
                    if out_f32 is not None:
                        E("v", lambda v, c=c, lo=lo: v.tensor_mul(
                            out_f32[:, c, lo:lo + 512], tscr[:, :], psB[:, :]),
                          inc=V)
                    else:
                        E("v", lambda v: v.tensor_mul(
                            tscr[:, :], tscr[:, :], psB[:, :]), inc=V)
                        vN = V.n
                        if affine is not None:
                            gg, bb = affine
                            E("s", lambda s, c=c, lo=lo, gg=gg, bb=bb: s.activation(
                                dst[:, c, lo:lo + 512], tscr[:, :], AF.Identity,
                                bias=bb[:, c:c + 1], scale=gg[:, c:c + 1]),
                              wait=[(V, vN)], inc=SG)
                        else:
                            E("s", lambda s, c=c, lo=lo: s.activation(
                                dst[:, c, lo:lo + 512], tscr[:, :], AF.Identity),
                              wait=[(V, vN)], inc=SG)
            return V.n, SG.n

        vSt = emit_stats(Y, XA, False)
        sCS = emit_row_cs(vSt, "outer")
        vXN, sXN = emit_normalize(sCS, Y, XA, False, (cGOUT, cBOUT))

        if stage == "xn":
            for c in range(NC):
                E("s", lambda s, c=c: s.activation(
                    Y[:, c, :], XA[:, c, :], AF.Identity),
                  wait=[(SG, sXN), (V, vXN)], inc=SG)
            _finish(mk, E, o_fm, Y, stO, [(SG, SG.n)])
            _replay(mk, nc)
            return nc

        y_dma(spX.rearrange("(c p) t -> p c t", p=128), Y[:, :, :], spXS,
              wait=[(V, vXN), (SG, sXN)])

        for blk in range(4):
            for c in range(NC):
                E("t", lambda t, blk=blk, c=c: t.matmul(
                    psS[0:1, blk * 512:(blk + 1) * 512], cWENT[:, c:c + 1],
                    XA[:, c, blk * 512:(blk + 1) * 512],
                    start=(c == 0), stop=(c == NC - 1)),
                  wait=BART([(SG, sXN), (cDg, NCDG)]),
                  inc=(T if c == NC - 1 else None))
        tGate = T.n
        E("s", lambda s: s.activation(gRow[:, :], psS[0:1, 0:2048], AF.Sigmoid,
                                      bias=cBENT[0:1, 0:1]),
          wait=[(T, tGate), (cDs, NCDS)], inc=SG)
        E("v", lambda v: v.tensor_scalar(gRow[:, :], gRow[:, :], 0.1, 2.0,
                                         ALU.max, ALU.min),
          wait=[(SG, SG.n)], inc=V)
        y_dma(spGate, gRow[0:1, :], gcS, wait=[(V, V.n)])
        y_dma(cGCOL[:, :], spGate.rearrange("(a p) -> p a", p=128), gcS,
              wait=[(gcS, gcS.n)], slow=True)

        g_dma(wvSB, dw3(dr["owqkv"])[:, :, 2048:3072], wvS, wait=[(spXS, 16)])
        y_dma(bvRow, drow(dr["obqkv"][2048:3072]), cDs)
        NCDS2 = cDs.n
        for h in range(2):
            E("t", lambda t, h=h: t.matmul(
                psA[:, :], cONE1[:, :], bvRow[0:1, h * 512:(h + 1) * 512],
                start=True, stop=True),
              wait=BART([(cDs, NCDS2), (V, V_CONST), (spXS, 16)]), inc=T)
            E("v", lambda v, h=h: v.tensor_copy(
                bvR[:, h * 512:(h + 1) * 512], psA[:, :]),
              wait=[(T, T.n)], inc=V)
        vBv = V.n
        for tt in range(16):
            for vb in range(2):
                pp = psA if vb == 0 else psB
                for c in range(NC):
                    E("t", lambda t, tt=tt, vb=vb, c=c, pp=pp: t.matmul(
                        pp[:, :], XA[:, c, tt * 128:(tt + 1) * 128],
                        wvSB[:, c, vb * 512:(vb + 1) * 512],
                        start=(c == 0), stop=(c == NC - 1)),
                      wait=BART([(wvS, 16), (SG, sXN), (V, vBv)]),
                      inc=(T if c == NC - 1 else None))
                tV = T.n
                E("v", lambda v, pp=pp, vb=vb: v.tensor_add(
                    tvs[:, :], pp[:, :], bvR[:, vb * 512:(vb + 1) * 512]),
                  wait=[(T, tV), (SG, SG.n)], inc=V)
                E("s", lambda s, tt=tt, vb=vb: s.activation(
                    vTM[:, tt, vb * 512:(vb + 1) * 512], tvs[:, :], AF.Identity,
                    scale=cGCOL[:, tt:tt + 1]),
                  wait=[(V, V.n), (gcS, 32)], inc=SG)
        sVTM = SG.n

        wq_last = [0, 0]
        NPrun = 1 if stage == "dbg0" else NP
        for p in range(NPrun):
            par = p % 2
            wctr = wS0 if par == 0 else wS1
            wbuf = wqhSB[par]
            g_dma(wbuf, dw3(dr["owqkv"])[:, :, p * 256:(p + 1) * 256], wctr,
                  wait=[(T, wq_last[par]), (spXS, 16)])
            nw = wctr.n
            for qk in range(2):
                dst = qpSB if qk == 0 else kpSB
                for tb in range(4):
                    pp = psA if tb % 2 == 0 else psB
                    for c in range(NC):
                        E("t", lambda t, qk=qk, tb=tb, c=c, pp=pp, wbuf=wbuf:
                          t.matmul(
                            pp[:, :], wbuf[:, c, qk * 128:qk * 128 + 128],
                            XA[:, c, tb * 512:(tb + 1) * 512],
                            start=(c == 0), stop=(c == NC - 1)),
                          wait=BART([(wctr, nw), (SG, sVTM)]),
                          inc=(T if c == NC - 1 else None))
                    tP = T.n

                    def ev(s, qk=qk, tb=tb, pp=pp, dst=dst, p=p):
                        sc = cPOOL[:, 0:1] if qk == 0 else 1.0
                        return s.activation(
                            dst[:, tb * 512:(tb + 1) * 512], pp[:, :], AF.Identity,
                            scale=sc, bias=cBQK[:, 2 * p + qk:2 * p + qk + 1])
                    E("s", ev, wait=[(T, tP)], inc=SG)
            wq_last[par] = T.n
            sProj = SG.n

            def sbank(pu, kb):
                if pu == 0:
                    return psS[:, kb * 512:(kb + 1) * 512]
                return (psA, psB, psC, psT)[kb][:, :]

            ounits = [(h01, qt) for h01 in range(2) for qt in range(16)]
            tSo_, tPTo_, tPVo_ = {}, {}, {}
            sE1o_, sE2o_, sRCo_ = {}, {}, {}
            vRSo_, vPTo_, vOFo_ = {}, {}, {}

            def out_S(i):
                h01, qt = ounits[i]
                b0, b1 = 64 * h01, 64 * h01 + 64
                pu = i % 2
                nk = (qt + 1) * 128
                nbank = (nk + 511) // 512
                for kb in range(nbank):
                    klo = kb * 512
                    kw = min(512, nk - klo)
                    if kb == 0:
                        w = [(SG, sProj)]
                        if i >= 2:
                            w.append((V, vOFo_[i - 2]))
                        if i == 0:
                            w = BART(w)
                    else:
                        w = []
                    E("t", lambda t, qt=qt, klo=klo, kw=kw, b0=b0, b1=b1,
                      pu=pu, kb=kb, sp=(kb != nbank - 1): t.matmul(
                        sbank(pu, kb)[:, 0:kw],
                        qpSB[b0:b1, qt * 128:(qt + 1) * 128],
                        kpSB[b0:b1, klo:klo + kw],
                        start=True, stop=sp),
                      wait=w)
                mo = qt * 128 - (nbank - 1) * 512
                E("t", lambda t, pu=pu, nbank=nbank, mo=mo: t.matmul(
                    sbank(pu, nbank - 1)[:, mo:mo + 128], cEYE[:, :],
                    cTRIC[:, :], start=False, stop=True), inc=T)
                tSo_[i] = T.n

            def out_soft(i):
                h01, qt = ounits[i]
                head = 2 * p + h01
                pu = i % 2
                nk = (qt + 1) * 128
                nbank = (nk + 511) // 512
                w = [(T, tSo_[i])]
                if i >= 2:
                    w.append((SG, sE1o_[i - 2]))
                if pu == 0:
                    E("v", lambda v, nk=nk, pu=pu: v.reduce_max(
                        oNM[:, pu, 0:1], psS[:, 0:nk], axis=AX.X, negate=True),
                      wait=w, inc=V)
                else:
                    E("v", lambda v, nk=nk, pu=pu: v.reduce_max(
                        oNM[:, pu, 0:1], psSB[:, 0:nk], axis=AX.X,
                        negate=True),
                      wait=w, inc=V)
                vNM = V.n
                for kb in range(nbank):
                    klo = kb * 512
                    kw = min(512, nk - klo)
                    w = [(V, vNM)]
                    if kb == 0 and i >= 2:
                        w.append((V, vRSo_[i - 2]))
                    E("s", lambda s, kb=kb, klo=klo, kw=kw, pu=pu: s.activation(
                        pSBp[:, pu, klo:klo + kw], sbank(pu, kb)[:, 0:kw],
                        AF.Exp, bias=oNM[:, pu, 0:1]),
                      wait=(w if kb == 0 else []),
                      inc=(SG if kb == nbank - 1 else None))
                sE1o_[i] = SG.n
                if stage == "dbg0" and i == 7:
                    E("v", lambda v: v.tensor_copy(Y[:, 3, 0:512], psA[:, :]),
                      wait=[(SG, sE1o_[i])])
                    E("v", lambda v: v.tensor_copy(Y[:, 4, 0:512], psB[:, :]))
                    E("v", lambda v: v.tensor_copy(
                        Y[:, 2, 8:9], oNM[:, 1, 0:1]))
                    for kb in range(4):
                        E("v", lambda v, kb=kb: v.tensor_copy(
                            Y[:, 2, 4 + kb:5 + kb], oMX[:, 1, kb, 0:1]))
                    E("s", lambda s: s.activation(
                        Y[:, 0, 0:512], pSBp[:, 1, 0:512], AF.Identity),
                      wait=[(SG, sE1o_[i])], inc=SG)
                    E("s", lambda s: s.activation(
                        Y[:, 1, 0:512], pSBp[:, 1, 512:1024], AF.Identity),
                      inc=SG)
                w2 = [(SG, sE1o_[i])]
                if i >= 2:
                    w2.append((SG, sRCo_[i - 2]))
                E("v", lambda v, nk=nk, pu=pu: v.reduce_sum(
                    oEL[:, pu, 0:1], pSBp[:, pu, 0:nk], axis=AX.X),
                  wait=w2, inc=V)
                vRSo_[i] = V.n
                E("s", lambda s, pu=pu: s.activation(
                    oLL[:, pu, 0:1], oEL[:, pu, 0:1], AF.Ln),
                  wait=[(V, vRSo_[i])])
                E("s", lambda s, pu=pu: s.activation(
                    oRC[:, pu, 0:1], oLL[:, pu, 0:1], AF.Exp, scale=cNEG1[:, :]),
                  inc=SG)
                sRCo_[i] = SG.n
                if stage == "dbg0" and i == 7:
                    E("v", lambda v: v.tensor_copy(Y[:, 2, 0:1], oEL[:, 1, 0:1]),
                      wait=[(SG, sRCo_[i])])
                    E("v", lambda v: v.tensor_copy(Y[:, 2, 1:2], oLL[:, 1, 0:1]))
                    E("v", lambda v: v.tensor_copy(Y[:, 2, 2:3], oRC[:, 1, 0:1]),
                      inc=V)
                E("s", lambda s, nk=nk, pu=pu: s.activation(
                    pSB2p[:, pu, 0:nk], pSBp[:, pu, 0:nk], AF.Identity,
                    scale=oRC[:, pu, 0:1]),
                  wait=([(T, tPTo_[i - 2])] if i >= 2 else []), inc=SG)
                sE2o_[i] = SG.n
                ptb = psS[:, 512:1024] if pu == 0 else psB[:, :]
                pvb = psS if pu == 0 else psA
                nkc = qt + 1
                vploc = None
                for g0 in range(0, nkc, 4):
                    gn = min(4, nkc - g0)
                    for ki in range(gn):
                        kc = g0 + ki
                        if ki == 0:
                            if g0 == 0:
                                w = [(SG, sE2o_[i])]
                                if i >= 2:
                                    w.append((V, vPTo_[i - 2]))
                            else:
                                w = [(V, vploc)]
                        else:
                            w = []
                        E("t", lambda t, kc=kc, ki=ki, pu=pu, ptb=ptb: t.matmul(
                            ptb[:, ki * 128:(ki + 1) * 128],
                            pSB2p[:, pu, kc * 128:(kc + 1) * 128],
                            cEYE[:, :], start=True, stop=True),
                          wait=w, inc=(T if ki == gn - 1 else None))
                    tg = T.n
                    E("v", lambda v, g0=g0, gn=gn, pu=pu, ptb=ptb: v.tensor_copy(
                        ptSBp[:, pu, g0 * 128:(g0 + gn) * 128],
                        ptb[:, 0:gn * 128]),
                      wait=([(T, tg)]
                            + ([(T, tPVo_[i - 2])]
                               if (g0 == 0 and i >= 2) else [])), inc=V)
                    vploc = V.n
                tPTo_[i] = T.n
                vPTo_[i] = vploc
                for kc in range(nkc):
                    E("t", lambda t, kc=kc, head=head, h01=h01, pu=pu, pvb=pvb,
                      st=(kc == 0), sp=(kc == nkc - 1): t.matmul(
                        pvb[0:64, 0:128] if h01 == 0
                        else pvb[64:128, 0:128],
                        vTM[:, kc, head * 64:head * 64 + 64],
                        ptSBp[:, pu, kc * 128:(kc + 1) * 128],
                        start=st, stop=sp,
                        tile_position=((0, 64) if h01 == 1 else None)),
                      wait=([(V, vPTo_[i])] if kc == 0 else []),
                      inc=(T if kc == nkc - 1 else None))
                tPVo_[i] = T.n
                ch, po = (head * 64) // 128, (head * 64) % 128
                E("v", lambda v, ch=ch, po=po, qt=qt, h01=h01, pvb=pvb:
                  v.tensor_copy(
                    OF[po:po + 64, ch, qt * 128:(qt + 1) * 128],
                    pvb[0:64, 0:128] if h01 == 0 else pvb[64:128, 0:128]),
                  wait=[(T, tPVo_[i])], inc=V)
                vOFo_[i] = V.n

            OPIPE = True
            if OPIPE:
                out_S(0)
                out_S(1)
                for i in range(32):
                    out_soft(i)
                    if i + 2 < 32:
                        out_S(i + 2)
            else:
                for i in range(32):
                    out_S(i)
                    out_soft(i)
        vAttO = V.n

        if stage == "dbg0":
            _finish(mk, E, o_fm, Y, stO, [(V, V.n), (SG, SG.n), (T, T.n)])
            _replay(mk, nc)
            return nc

        if stage == "of":
            for c in range(NC):
                E("s", lambda s, c=c: s.activation(
                    Y[:, c, :], OF[:, c, :], AF.Identity),
                  wait=[(V, vAttO), (T, T.n), (SG, SG.n)], inc=SG)
            _finish(mk, E, o_fm, Y, stO, [(SG, SG.n)])
            _replay(mk, nc)
            return nc

        g_dma(woSBo, dw3(dr["owout"]), woS, wait=[(T, T.n)])
        nwo = woS.n
        rl_last = [vAttO, vAttO]
        for oc in range(NC):
            for tb in range(4):
                rb = (oc * 4 + tb) % 2
                rctr = rlS0 if rb == 0 else rlS1
                y_dma(xrl[:, rb, :],
                      spX.rearrange("(c p) t -> p c t", p=128)
                      [:, oc, tb * 512:(tb + 1) * 512],
                      rctr, wait=[(spXS, 16), (V, rl_last[rb])])
                nrl = rctr.n
                pp = psA if tb % 2 == 0 else psB
                for c in range(NC):
                    E("t", lambda t, oc=oc, tb=tb, c=c, pp=pp: t.matmul(
                        pp[:, :], woSBo[:, c, oc * 128:(oc + 1) * 128],
                        OF[:, c, tb * 512:(tb + 1) * 512],
                        start=(c == 0), stop=(c == NC - 1)),
                      wait=BART([(woS, nwo), (V, vAttO)]),
                      inc=(T if c == NC - 1 else None))
                tWo = T.n
                E("s", lambda s, pp=pp, oc=oc: s.activation(
                    twx[:, :], pp[:, :], AF.Identity, scale=cPOOL[:, 1:2],
                    bias=cBO01[:, oc:oc + 1]),
                  wait=[(T, tWo), (V, V.n)], inc=SG)
                sWo = SG.n
                E("v", lambda v, oc=oc, tb=tb, rb=rb: v.tensor_add(
                    x1v[:, oc, tb * 512:(tb + 1) * 512], twx[:, :], xrl[:, rb, :]),
                  wait=[(SG, sWo), (rctr, nrl)], inc=V)
                rl_last[rb] = V.n
            y_dma(spX1.rearrange("(c p) t -> p c t", p=128)[:, oc, :],
                  x1v[:, oc, :], spX1S, wait=[(V, V.n)])
        vX1 = V.n

        if stage == "x1":
            _finish(mk, E, o_fm, Y, stO, [(V, vX1)])
            _replay(mk, nc)
            return nc

        vSt2 = emit_stats(x1v, XA, False)
        sCS2 = emit_row_cs(vSt2, "plain")
        vH2, _ = emit_normalize(sCS2, x1v, None, False, None, out_f32=Y)
        iwq_last0_init = T.n
        iwq_vgate0_init = V.n

        if stage == "h2":
            _finish(mk, E, o_fm, Y, stO, [(V, vH2)])
            _replay(mk, nc)
            return nc

        iwq_last = [iwq_last0_init]
        iwq_vgate = [iwq_vgate0_init]
        w1_last = [iwq_last0_init, iwq_last0_init]
        w2_last = [iwq_last0_init, iwq_last0_init]
        wo_last = [0]

        def inner_layer(L):
            nld0 = ldS.n
            y_dma(cLGB[:, 0:8], dcol(dr["ilng"][L]), ldS,
                  wait=[(SG, SG.n), (V, V.n)], slow=True)
            y_dma(cLGB[:, 8:16], dcol(dr["ilnb"][L]), ldS, slow=True)
            y_dma(cB1[:, 0:42],
                  dr["ib1"][L][0:5376].rearrange("(j p) -> p j", p=128),
                  ldS, slow=True)
            y_dma(cB1[0:42, 42:43], drow(dr["ib1"][L][5376:5418]).rearrange(
                "o n -> n o"), ldS, slow=True)
            y_dma(cB1[0:42, 43:44], drow(dr["ib1"][L][5418:5460]).rearrange(
                "o n -> n o"), ldS, slow=True)
            y_dma(cB2[:, :], dcol(dr["ib2"][L]), ldS, slow=True)
            y_dma(cQSR[0:1, 0:64], drow(dr["iqs"][L]), ldS)
            y_dma(cQSR[0:1, 128:192], drow(dr["iks"][L]), ldS)
            nld = ldS.n
            E("v", lambda v: v.tensor_scalar_mul(
                cQSR[0:1, 64:96], cQSR[0:1, 32:64], -1.0),
              wait=[(ldS, nld)])
            E("v", lambda v: v.tensor_copy(cQSR[0:1, 96:128], cQSR[0:1, 0:32]))
            E("v", lambda v: v.tensor_scalar_mul(
                cQSR[0:1, 192:224], cQSR[0:1, 160:192], -1.0))
            E("v", lambda v: v.tensor_copy(cQSR[0:1, 224:256], cQSR[0:1, 128:160]),
              inc=V)
            vQSR = V.n
            for r in range(4):
                E("t", lambda t, r=r: t.matmul(
                    psA[:, 0:64], cONE1[:, :], cQSR[0:1, r * 64:(r + 1) * 64],
                    start=True, stop=True),
                  wait=BART([(V, vQSR)]), inc=T)
                E("v", lambda v, r=r: v.tensor_copy(qsRep[:, r, :], psA[:, 0:64]),
                  wait=[(T, T.n)], inc=V)
            vRep = V.n
            for (tab, trig, r) in ((AQ, cC8, 0), (BQ, cS8, 1), (AK, cCI, 2),
                                   (BK, cSI, 3)):
                E("v", lambda v, tab=tab, trig=trig, r=r: v.tensor_tensor(
                    tab[:, :, :], trig[:, :, :],
                    qsRep[:, r:r + 1, :].broadcast_to([128, 16, 64]), ALU.mult),
                  wait=[(V, vRep), (cDg, NCDG)], inc=V)
            vAB = V.n

            for c in range(NC):
                if c >= 4:
                    E("s", lambda s, c=c: s.memzero(XA[:, c, 0:2]),
                      wait=[(T, T.n)])
                    E("s", lambda s, c=c: s.activation(
                        XA[:, c, 1:S], Y[:, c, 0:S - 1], AF.Identity), inc=SG)
                else:
                    E("s", lambda s, c=c: s.activation(
                        XA[:, c, :], Y[:, c, :], AF.Identity),
                      wait=[(T, T.n)], inc=SG)
            sXAi = SG.n

            for p in range(NP):
                inner_pair(L, p, sXAi, vAB)
            vAtt = V.n

            g_dma(wobuf, dw3(dr["iwo"][L]), woS,
                  wait=[(T, T.n), (SG, SG.n)])
            nwo2 = woS.n
            addg = [None, None]
            for oc in range(NC):
                for tb in range(4):
                    pp = psA if tb % 2 == 0 else psB
                    par = tb % 2
                    for c in range(NC):
                        if c == 0:
                            w = ([(woS, nwo2), (V, vAtt)]
                                 + ([(V, addg[par])] if addg[par] else []))
                            if oc == 0 and tb == 0:
                                w = BART(w)
                        else:
                            w = []
                        E("t", lambda t, oc=oc, tb=tb, c=c, pp=pp: t.matmul(
                            pp[:, :], wobuf[:, c, oc * 128:(oc + 1) * 128],
                            OF[:, c, tb * 512:(tb + 1) * 512],
                            start=(c == 0), stop=(c == NC - 1)),
                          wait=w, inc=(T if c == NC - 1 else None))
                    tWo = T.n
                    E("v", lambda v, oc=oc, tb=tb, pp=pp: v.tensor_add(
                        Y[:, oc, tb * 512:(tb + 1) * 512], pp[:, :],
                        Y[:, oc, tb * 512:(tb + 1) * 512]),
                      wait=[(T, tWo)], inc=V)
                    addg[par] = V.n
            ffn_tgate = T.n
            if stage == ("att", L):
                return

            vStf = emit_stats(Y, XA, True)
            if stage == ("fst", L):
                E("s", lambda s: s.activation(
                    Y[:, 0, 0:2], Y[:, 0, 0:2], AF.Identity),
                  wait=[(V, vStf)], inc=SG)
                return "fxn"
            sCSf = emit_row_cs(vStf, "plain")
            if stage == ("fs2", L):
                E("s", lambda s: s.activation(
                    Y[:, 0, 0:2], Y[:, 0, 0:2], AF.Identity),
                  wait=[(SG, sCSf)], inc=SG)
                return "fxn"
            vXN2, sXN2 = emit_normalize(
                sCSf, Y, XA, True, (cLGB[:, 0:8], cLGB[:, 8:16]))
            if stage == ("fxn", L):
                for c in range(NC):
                    E("s", lambda s, c=c: s.activation(
                        Y[:, c, :], XA[:, c, :], AF.Identity),
                      wait=[(SG, sXN2), (V, vXN2)], inc=SG)
                return "fxn"
            w1_last[0] = max(w1_last[0], ffn_tgate)
            w1_last[1] = max(w1_last[1], ffn_tgate)
            w2_last[0] = max(w2_last[0], ffn_tgate)
            w2_last[1] = max(w2_last[1], ffn_tgate)
            for th in range(2):
                ffn_half(L, th, vXN2, sXN2, nld)

        def inner_pair(L, p, sXAi, vAB):
            g_dma(wqp, dw3(dr["iwqkv"][L])[:, :, p * 384:(p + 1) * 384], wS0,
                  wait=[(T, iwq_last[0]), (V, iwq_vgate[0])])
            nw = wS0.n
            qev = [V.n, V.n]
            for tt in range(16):
                pp = psA if tt % 2 == 0 else psB
                for c in range(NC):
                    w = ((BART([(wS0, nw), (SG, sXAi)]) if tt == 0
                          else [(wS0, nw), (SG, sXAi), (V, qev[tt % 2])])
                         if c == 0 else [])
                    E("t", lambda t, tt=tt, c=c, pp=pp: t.matmul(
                        pp[:, 0:384], XA[:, c, tt * 128:(tt + 1) * 128],
                        wqp[:, c, :], start=(c == 0), stop=(c == NC - 1)),
                      wait=w, inc=(T if c == NC - 1 else None))
                tQ = T.n
                E("v", lambda v, tt=tt, pp=pp: v.tensor_copy(
                    qkvTM[:, tt, :], pp[:, 0:384]),
                  wait=[(T, tQ)], inc=V)
                qev[tt % 2] = V.n
            iwq_last[0] = T.n
            vQKV = V.n
            E("v", lambda v: v.tensor_mul(
                sqB[:, :, :], qkvTM[:, :, 0:256], qkvTM[:, :, 0:256]),
              wait=[(V, vQKV)])
            E("v", lambda v: v.tensor_reduce(
                cRN[:, :], sq64[:, :, :], axis=AX.X, op=ALU.add), inc=V)
            vNsq = V.n
            E("s", lambda s: s.activation(cRN[:, :], cRN[:, :], AF.Ln,
                                          bias=cPOOL[:, 3:4], scale=1.0),
              wait=[(V, vNsq)])
            E("s", lambda s: s.activation(cRN[:, :], cRN[:, :], AF.Exp,
                                          scale=cPOOL[:, 2:3]), inc=SG)
            E("v", lambda v: v.tensor_scalar_min(cRN[:, :], cRN[:, :], 1e12),
              wait=[(SG, SG.n)], inc=V)
            for qk in range(2):
                Atab = AQ if qk == 0 else AK
                Btab = BQ if qk == 0 else BK
                sub0 = 2 * qk
                E("v", lambda v, sub0=sub0: v.tensor_copy(
                    qrB[:, :, :, 0:32], qv6[:, :, sub0:sub0 + 2, 32:64]),
                  wait=[(V, V.n), (V, vAB)])
                E("v", lambda v, sub0=sub0: v.tensor_copy(
                    qrB[:, :, :, 32:64], qv6[:, :, sub0:sub0 + 2, 0:32]), inc=V)
                E("v", lambda v, qk=qk, Atab=Atab: v.tensor_tensor(
                    m14[:, :, :, :], qv6[:, :, 2 * qk:2 * qk + 2, :],
                    Atab[:, :, :].rearrange("p a (o c) -> p a o c", o=1)
                    .broadcast_to([128, 16, 2, 64]), ALU.mult),
                  wait=[(V, V.n)])
                E("v", lambda v, qk=qk, Btab=Btab: v.tensor_tensor(
                    q2k4[:, :, 2 * qk:2 * qk + 2, :], qrB[:, :, :, :],
                    Btab[:, :, :].rearrange("p a (o c) -> p a o c", o=1)
                    .broadcast_to([128, 16, 2, 64]), ALU.mult))
                E("v", lambda v, qk=qk: v.tensor_add(
                    q2k4[:, :, 2 * qk:2 * qk + 2, :],
                    q2k4[:, :, 2 * qk:2 * qk + 2, :], m14[:, :, :, :]), inc=V)
            vRope = V.n
            tTr_, vQF_ = {}, {}
            for k in range(64):
                tt, j = k // 4, k % 4
                qk, sub = (0 if j < 2 else 1), j % 2
                pr = k % 2
                w = [(V, vRope)]
                if k >= 2:
                    w.append((T, tTr_[k - 2]))
                E("v", lambda v, tt=tt, j=j, pr=pr: v.tensor_scalar_mul(
                    diagT2[:, pr, :], cEYE[:, :],
                    cRN[:, tt * 4 + j:tt * 4 + j + 1]),
                  wait=w, inc=V)
                vD = V.n
                wt = [(V, vD)]
                if j == 0 and tt >= 1:
                    wt.append((V, vQF_[tt - 1]))
                E("t", lambda t, tt=tt, qk=qk, sub=sub, pr=pr: t.matmul(
                    psT[0:64, qk * 128:qk * 128 + 128] if sub == 0
                    else psT[64:128, qk * 128:qk * 128 + 128],
                    q2k4[:, tt, 2 * qk + sub, :], diagT2[:, pr, :],
                    start=True, stop=True,
                    tile_position=((0, 64) if sub == 1 else None)),
                  wait=(BART(wt) if k == 0 else wt), inc=T)
                tTr_[k] = T.n
                if j == 3:
                    E("v", lambda v, tt=tt: v.tensor_copy(
                        qkFM[:, 0, tt * 128:(tt + 1) * 128], psT[:, 0:128]),
                      wait=[(T, tTr_[k])])
                    E("v", lambda v, tt=tt: v.tensor_copy(
                        qkFM[:, 1, tt * 128:(tt + 1) * 128], psT[:, 128:256]),
                      inc=V)
                    vQF_[tt] = V.n
            vFM = V.n
            units = [(h01, qt) for h01 in range(2) for qt in range(16)]
            NU = len(units)
            tS_, tPT_, tPV_ = {}, {}, {}
            sE1_, sRC_, sE2_ = {}, {}, {}
            vRS_, vPT_, vOF_ = {}, {}, {}

            def att_S(i):
                h01, qt = units[i]
                b0, b1 = 64 * h01, 64 * h01 + 64
                sl = (i % 4) * 512
                klo = max(0, qt - 2) * 128
                n0 = qt * 128 - klo
                nk = n0 + 128
                w = [(V, vFM)]
                if i >= 4:
                    w.append((V, vOF_[i - 4]))
                E("t", lambda t, qt=qt, sl=sl, b0=b0, b1=b1, klo=klo, nk=nk:
                  t.matmul(psS[:, sl:sl + nk],
                           qkFM[b0:b1, 0, qt * 128:(qt + 1) * 128],
                           qkFM[b0:b1, 1, klo:klo + nk],
                           start=True, stop=False),
                  wait=(BART(w) if i == 0 else w))
                if qt >= 2:
                    E("t", lambda t, sl=sl: t.matmul(
                        psS[:, sl:sl + 128], cEYE[:, :], cTRIG[:, :],
                        start=False, stop=False))
                E("t", lambda t, sl=sl, n0=n0: t.matmul(
                    psS[:, sl + n0:sl + n0 + 128], cEYE[:, :], cTRIC[:, :],
                    start=False, stop=True), inc=T)
                tS_[i] = T.n

            def att_head(i):
                h01, qt = units[i]
                sl = (i % 4) * 512
                par = i % 2
                klo = max(0, qt - 2) * 128
                nk = qt * 128 - klo + 128
                w = [(T, tS_[i])]
                if i >= 2:
                    w.append((V, vRS_[i - 2]))
                E("s", lambda s, sl=sl, nk=nk, par=par: s.activation(
                    pIn[:, par, 0:nk], psS[:, sl:sl + nk], AF.Exp),
                  wait=w, inc=SG)
                sE1_[i] = SG.n
                w2 = [(SG, sE1_[i])]
                if i >= 2:
                    w2.append((SG, sRC_[i - 2]))
                E("v", lambda v, nk=nk, par=par: v.reduce_sum(
                    sxL2[:, par, 0:1], pIn[:, par, 0:nk], axis=AX.X),
                  wait=w2, inc=V)
                vRS_[i] = V.n
                E("s", lambda s, par=par: s.activation(
                    sxLL[:, par, 0:1], sxL2[:, par, 0:1], AF.Ln),
                  wait=[(V, vRS_[i])])
                E("s", lambda s, par=par: s.activation(
                    sxRC[:, par, 0:1], sxLL[:, par, 0:1], AF.Exp,
                    scale=cNEG1[:, :]), inc=SG)
                sRC_[i] = SG.n

            def att_tail(i):
                h01, qt = units[i]
                head = 2 * p + h01
                sl = (i % 4) * 512
                par = i % 2
                klo = max(0, qt - 2) * 128
                nk = qt * 128 - klo + 128
                E("s", lambda s, nk=nk, par=par: s.activation(
                    pNr[:, par, 0:nk], pIn[:, par, 0:nk], AF.Identity,
                    scale=sxRC[:, par, 0:1]),
                  wait=([(T, tPT_[i - 2])] if i >= 2 else []), inc=SG)
                sE2_[i] = SG.n
                pt = (psT, psC, psA, psB)[i % 4]
                nkc = nk // 128
                for ki in range(nkc):
                    E("t", lambda t, ki=ki, par=par, pt=pt: t.matmul(
                        pt[:, ki * 128:(ki + 1) * 128],
                        pNr[:, par, ki * 128:(ki + 1) * 128],
                        cEYE[:, :], start=True, stop=True),
                      wait=(([(SG, sE2_[i])]
                             + ([(V, vPT_[i - 4])] if i >= 4 else []))
                            if ki == 0 else []),
                      inc=(T if ki == nkc - 1 else None))
                tPT_[i] = T.n
                E("v", lambda v, nk=nk, par=par, pt=pt: v.tensor_copy(
                    ptIn[:, par, 0:nk], pt[:, 0:nk]),
                  wait=([(T, tPT_[i])]
                        + ([(T, tPV_[i - 2])] if i >= 2 else [])), inc=V)
                vPT_[i] = V.n
                for ki in range(nkc):
                    kc = klo // 128 + ki
                    E("t", lambda t, ki=ki, kc=kc, h01=h01, sl=sl, par=par,
                      st=(ki == 0), sp=(ki == nkc - 1): t.matmul(
                        psS[0:64, sl:sl + 128] if h01 == 0
                        else psS[64:128, sl:sl + 128],
                        qv6[:, kc, 4 + h01, :],
                        ptIn[:, par, ki * 128:(ki + 1) * 128],
                        start=st, stop=sp,
                        tile_position=((0, 64) if h01 == 1 else None)),
                      wait=([(V, vPT_[i])] if ki == 0 else []),
                      inc=(T if ki == nkc - 1 else None))
                tPV_[i] = T.n
                ch, po = (head * 64) // 128, (head * 64) % 128
                E("v", lambda v, ch=ch, po=po, qt=qt, h01=h01, sl=sl:
                  v.tensor_copy(
                    OF[po:po + 64, ch, qt * 128:(qt + 1) * 128],
                    psS[0:64, sl:sl + 128] if h01 == 0
                    else psS[64:128, sl:sl + 128]),
                  wait=[(T, tPV_[i])], inc=V)
                vOF_[i] = V.n

            for j in range(min(4, NU)):
                att_S(j)
            for i in range(NU):
                att_head(i)
                if i >= 1:
                    att_tail(i - 1)
                if 4 <= i + 3 < NU:
                    att_S(i + 3)
            att_tail(NU - 1)

        def ffn_half(L, th, vXN2, sXN2, nld):
            t0 = th * 1024
            gelg = [None, None]
            sttg = [None, None]
            for jp in range(NJ):
                wa = JW[jp]
                par = jp % 2
                wbuf = w1b[par]
                wctr = w1S0 if par == 0 else w1S1
                g_dma(wbuf[:, :, 0:2 * wa],
                      dw3(dr["iw1"][L])[:, :, jp * 256:jp * 256 + 2 * wa],
                      wctr, wait=[(T, w1_last[par])])
                nw = wctr.n
                for tb in range(2):
                    pu = tb
                    gb = psA if pu == 0 else psC
                    ab = psB if pu == 0 else psT
                    lo = t0 + tb * 512
                    hb = tb * 512
                    for c in range(NC):
                        if c == 0:
                            w = [(wctr, nw), (SG, sXN2), (V, vXN2)]
                            if gelg[pu]:
                                w.append((SG, gelg[pu]))
                            if jp == 0 and tb == 0:
                                w = BART(w)
                        else:
                            w = []
                        E("t", lambda t, c=c, wa=wa, wbuf=wbuf, lo=lo, gb=gb:
                          t.matmul(
                            gb[0:wa, :], wbuf[:, c, wa:2 * wa],
                            XA[:, c, lo:lo + 512],
                            start=(c == 0), stop=(c == NC - 1)),
                          wait=w, inc=(T if c == NC - 1 else None))
                    tG = T.n
                    E("s", lambda s, wa=wa, jp=jp, pu=pu, gb=gb: s.activation(
                        gelB2[0:wa, pu, :], gb[0:wa, :], AF.Gelu,
                        bias=cB1[0:wa, 2 * jp + 1:2 * jp + 2]),
                      wait=([(T, tG), (ldS, nld)]
                            + [(V, sttg[pu]) if sttg[pu] else (V, V.n)]),
                      inc=SG)
                    gelg[pu] = SG.n
                    sGl = SG.n
                    for c in range(NC):
                        w = (([(wctr, nw)]
                              + [(V, sttg[pu]) if sttg[pu] else (V, V.n)])
                             if c == 0 else [])
                        E("t", lambda t, c=c, wa=wa, wbuf=wbuf, lo=lo, ab=ab:
                          t.matmul(
                            ab[0:wa, :], wbuf[:, c, 0:wa],
                            XA[:, c, lo:lo + 512],
                            start=(c == 0), stop=(c == NC - 1)),
                          wait=w, inc=(T if c == NC - 1 else None))
                    tA = T.n
                    E("v", lambda v, wa=wa, jp=jp, hb=hb, pu=pu, ab=ab:
                      v.scalar_tensor_tensor(
                        H_ap(jp, hb, 512)[0:wa, :], ab[0:wa, :],
                        cB1[0:wa, 2 * jp:2 * jp + 1], gelB2[0:wa, pu, :],
                        op0=ALU.add, op1=ALU.mult),
                      wait=[(T, tA), (SG, sGl)], inc=V)
                    sttg[pu] = V.n
                w1_last[par] = T.n
            vH = V.n
            for tb in range(2):
                lo = t0 + tb * 512
                hb = tb * 512
                for jp in range(NJ):
                    wa = JW[jp]
                    par = jp % 2
                    wbuf = w2b[par]
                    wctr = w2S0 if par == 0 else w2S1
                    g_dma(wbuf[0:wa, :], dr["iw2"][L][jp * 128:jp * 128 + wa, :],
                          wctr, wait=[(T, w2_last[par])])
                    nw2 = wctr.n
                    for oc in range(8):
                        E("t", lambda t, jp=jp, wa=wa, oc=oc, wbuf=wbuf, hb=hb:
                          t.matmul(
                            ps_oc(oc), wbuf[0:wa, oc * 128:(oc + 1) * 128],
                            H_ap(jp, hb, 512)[0:wa, :],
                            start=(jp == 0), stop=(jp == NJ - 1)),
                          wait=(BART([(wctr, nw2), (V, vH)])
                                if jp == 0 and oc == 0
                                else [(wctr, nw2)]),
                          inc=(T if oc == 7 else None))
                    w2_last[par] = T.n
                tW2 = T.n
                for oc in range(8):
                    E("v", lambda v, oc=oc, lo=lo: v.scalar_tensor_tensor(
                        Y[:, oc, lo:lo + 512], ps_oc(oc), cB2[:, oc:oc + 1],
                        Y[:, oc, lo:lo + 512], op0=ALU.add, op1=ALU.add),
                      wait=[(T, tW2), (ldS, nld)], inc=V)

        for L in range(n_layers):
            r = inner_layer(L)
            iwq_last[0] = T.n
            iwq_vgate[0] = V.n
            w1_last[0] = w1_last[1] = T.n
            w2_last[0] = w2_last[1] = T.n
            if r == "fxn":
                _finish(mk, E, o_fm, Y, stO, [(V, V.n), (T, T.n), (SG, SG.n)])
                _replay(mk, nc)
                return nc
            if stage == ("att", L) or stage == ("ffn", L):
                _finish(mk, E, o_fm, Y, stO, [(V, V.n), (T, T.n), (SG, SG.n)])
                _replay(mk, nc)
                return nc

        rl2_last = [0, 0]
        for c in range(NC):
            rb = c % 2
            rctr = rlS0 if rb == 0 else rlS1
            y_dma(x1rl[:, rb, :],
                  spX1.rearrange("(c p) t -> p c t", p=128)[:, c, :],
                  rctr, wait=[(spX1S, spX1S.n), (V, rl2_last[rb]),
                              (T, T.n), (SG, SG.n)])
            nrl = rctr.n
            E("v", lambda v, c=c, rb=rb: v.tensor_add(
                Y[:, c, :], Y[:, c, :], x1rl[:, rb, :]),
              wait=[(rctr, nrl)], inc=V)
            rl2_last[rb] = V.n
        _finish(mk, E, o_fm, Y, stO, [(V, V.n)])
        _replay(mk, nc)
    return nc


def _finish(mk, E, o_fm, src, stO, wait):
    E("y", lambda y: y.dma_start(
        o_fm.rearrange("(c p) t -> p c t", p=128), src[:, :, :]),
      wait=wait, inc=stO, inck=16)
    E("y", lambda y: None, wait=[(stO, 16)])


def _replay(mk, nc):
    with nc.Block() as block:
        for eng, reg in (("g", block.gpsimd), ("y", block.sync),
                         ("t", block.tensor), ("v", block.vector),
                         ("s", block.scalar)):
            ops = list(mk.q[eng])

            def body(e, ops=ops):
                for fn in ops:
                    fn(e)
            reg(body)


def kernel(**inputs):
    shared = _prep_shared(inputs)
    x = np.asarray(inputs["L0_attn_x"], dtype=np.float32)
    nc = build_program()
    in_maps = []
    for b in range(8):
        m = dict(shared)
        m["x_fm"] = np.ascontiguousarray(x[b].T)
        in_maps.append(m)
    res = run_bass_kernel_spmd(nc, in_maps, core_ids=list(range(8)))
    out = np.stack([np.ascontiguousarray(np.asarray(r["o_fm"]).T)
                    for r in res.results])
    return out.astype(np.float32)
